# version 02
# baseline (speedup 1.0000x reference)
import contextlib
import numpy as np
import concourse.bass as bass
import concourse.mybir as mybir
from concourse.bass_utils import run_bass_kernel_spmd

F32 = mybir.dt.float32
BF16 = mybir.dt.bfloat16
I32 = mybir.dt.int32
AF = mybir.ActivationFunctionType
ALU = mybir.AluOpType
AX = mybir.AxisListType

D = 1024
PC = D // 128
LNEPS = 1e-5


def pm(v):
    v = np.asarray(v)
    return np.ascontiguousarray(v.reshape(-1, 128).T)


def fm(x):
    return np.ascontiguousarray(np.asarray(x).T)


class Prog:

    ENGS = ("t", "v", "s", "g", "y", "sd", "gd", "yd")

    def __init__(self):
        self.ops = []
        self.last_w = {}
        self.readers = {}

    def op(self, eng, fn, reads=(), writes=(), dma=False, group=None):
        if dma:
            eng = {"y": "yd", "g": "gd", "s": "sd"}[eng]
        idx = len(self.ops)
        deps = set()
        for b in reads:
            w = self.last_w.get(b)
            if w is not None:
                deps.add(w)
        for b in writes:
            w = self.last_w.get(b)
            if w is not None:
                deps.add(w)
            deps.update(self.readers.get(b, ()))
        deps.discard(idx)
        self.ops.append(dict(eng=eng, fn=fn, reads=reads, writes=writes,
                             deps=deps, group=group if dma else None))
        for b in reads:
            self.readers.setdefault(b, []).append(idx)
        for b in writes:
            self.last_w[b] = idx
            self.readers[b] = []
        return idx

    def emit(self, nc):
        import contextlib
        DMAQ = ("sd", "gd", "yd")
        needed = set()
        for o in self.ops:
            needed.update(o["deps"])
        ord_ = {}
        cnt = {e: 0 for e in self.ENGS if e not in DMAQ}
        dma_sem_ids = {}
        n_dma = 0
        groups = {}
        gmembers = {}
        for i, o in enumerate(self.ops):
            if o.get("group") is not None and o["eng"] in DMAQ:
                gmembers.setdefault(o["group"], []).append(i)
        for i, o in enumerate(self.ops):
            gname = o.get("group")
            if o["eng"] in DMAQ and gname is not None:
                if any(m in needed for m in gmembers[gname]):
                    if gname not in groups:
                        groups[gname] = n_dma
                        n_dma += 1
                    dma_sem_ids[i] = groups[gname]
                continue
            if i in needed:
                if o["eng"] in DMAQ:
                    dma_sem_ids[i] = n_dma
                    n_dma += 1
                else:
                    cnt[o["eng"]] += 1
                    ord_[i] = cnt[o["eng"]]
        import bisect
        per_eng = {e: [] for e in self.ENGS}
        for i, o in enumerate(self.ops):
            per_eng[o["eng"]].append(i)
        with contextlib.ExitStack() as es:
            sems = {e: es.enter_context(nc.semaphore("sem_" + e))
                    for e in self.ENGS if e not in DMAQ}
            dsems = [es.enter_context(nc.semaphore(f"dma{j}"))
                     for j in range(n_dma)]
            block = es.enter_context(nc.Block())
            emap = dict(t="tensor", v="vector", s="scalar", g="gpsimd",
                        y="sync", sd="scalar", gd="gpsimd", yd="sync")
            phys = {}
            for e in self.ENGS:
                phys.setdefault(emap[e], []).append(e)

            def mk(pname, eng_keys):
                def body(eng):
                    watermark = {}
                    seen_insts = {}
                    mine = sorted([i for e in eng_keys for i in per_eng[e]])
                    for i in mine:
                        o = self.ops[i]
                        targets = {}
                        for d in o["deps"]:
                            od = self.ops[d]
                            if od["eng"] in DMAQ:
                                gname = od.get("group")
                                if gname is not None:
                                    nmem = bisect.bisect_left(gmembers[gname], i)
                                    k = ("d", dma_sem_ids[d])
                                    targets[k] = max(targets.get(k, 0), 16 * nmem)
                                else:
                                    k = ("d", dma_sem_ids[d])
                                    targets[k] = max(targets.get(k, 0), 16)
                            else:
                                k = ("e", od["eng"])
                                targets[k] = max(targets.get(k, 0), ord_[d])
                        for k, val in sorted(targets.items()):
                            if watermark.get(k, 0) >= val:
                                continue
                            sem = dsems[k[1]] if k[0] == "d" else sems[k[1]]
                            eng.wait_ge(sem, val)
                            watermark[k] = val
                        inst = o["fn"](eng)
                        ik = id(inst)
                        assert not (ik in seen_insts and seen_insts[ik] is inst), \
                            f"op {i} returned an instruction already inc'd"
                        if i in dma_sem_ids:
                            inst.then_inc(dsems[dma_sem_ids[i]], 16)
                            seen_insts[ik] = inst
                        elif i in needed:
                            inst.then_inc(sems[o["eng"]], 1)
                            seen_insts[ik] = inst
                    return
                return body

            for pname, eng_keys in phys.items():
                getattr(block, pname)(mk(pname, eng_keys))
        return nc


class Emit:

    def __init__(self, nc, es, names):
        self.nc = nc
        self.sems = {}
        self.cnt = {}
        for n in names:
            self.sems[n] = es.enter_context(nc.semaphore(n))
            self.cnt[n] = 0

    def inc(self, inst, name, n=1):
        inst.then_inc(self.sems[name], n)
        self.cnt[name] += n

    def w(self, eng, name, val):
        eng.wait_ge(self.sems[name], val)

    def sw(self, eng, name):
        eng.wait_ge(self.sems[name], self.cnt[name])


def emit_rneq(p, src, skey, ii, ikey, ff, fkey, gg, gkey, out, okey, lo=-128.0, hi=127.0):
    p.op("v", lambda e: e.tensor_copy(ii, src), reads=(skey,), writes=(ikey,))
    p.op("v", lambda e: e.tensor_copy(ff, ii), reads=(ikey,), writes=(fkey,))
    p.op("v", lambda e: e.tensor_sub(gg, src, ff), reads=(skey, fkey), writes=(gkey,))
    p.op("v", lambda e: e.tensor_single_scalar(src, gg, 0.5, ALU.is_ge),
         reads=(gkey,), writes=(skey,))
    p.op("v", lambda e: e.tensor_add(ff, ff, src), reads=(fkey, skey), writes=(fkey,))
    p.op("v", lambda e: e.tensor_single_scalar(gg, gg, -0.5, ALU.is_le),
         reads=(gkey,), writes=(gkey,))
    p.op("v", lambda e: e.tensor_sub(ff, ff, gg), reads=(fkey, gkey), writes=(fkey,))
    p.op("v", lambda e: e.tensor_scalar(out, ff, lo, hi, ALU.max, ALU.min),
         reads=(fkey,), writes=(okey,))


class LNT:
    def __init__(self, nc, QT, tag="", ext_vsq=None, ext_vsq_key=None):
        self.QT = QT
        a = nc.alloc_sbuf_tensor
        self.onesr = a(f"ln_onesr{tag}", [1, 128], F32)
        self.onesc = a(f"ln_onesc{tag}", [128, 2], F32)
        self.epsb = a(f"ln_epsb{tag}", [1, 1], F32)
        if ext_vsq is not None:
            self.vsq = ext_vsq
            self.vsq_key = ext_vsq_key
        else:
            self.vsq = a(f"ln_vsq{tag}", [128, PC * QT], F32)
            self.vsq_key = "ln_vsq"
        self.rv = a(f"ln_rv{tag}", [1, QT], F32)
        self.rq = a(f"ln_rq{tag}", [1, QT], F32)
        self.mrow = a(f"ln_mrow{tag}", [1, QT], F32)
        self.varrow = a(f"ln_var{tag}", [1, QT], F32)
        self.srow = a(f"ln_srow{tag}", [1, QT], F32)
        self.rrow = a(f"ln_rrow{tag}", [1, QT], F32)
        self.mb = a(f"ln_mb{tag}", [128, QT], F32)
        self.rb = a(f"ln_rb{tag}", [128, QT], F32)
        self.psr = nc.alloc_psum_tensor(f"ln_psr{tag}", [2, 2 * QT], F32)
        self.psb2 = nc.alloc_psum_tensor(f"ln_psb2{tag}", [128, 2 * QT], F32)

    def init_ops(self, p):
        p.op("v", lambda e: e.memset(self.onesr[:], 1.0), writes=("ln_onesr",))
        p.op("v", lambda e: e.memset(self.onesc[:], 1.0), writes=("ln_onesc",))
        p.op("v", lambda e: e.memset(self.epsb[:], LNEPS), writes=("ln_epsb",))


def emit_postln(p, L, vsb, vkey, out_ap, okey, eps_val=None):
    QT = L.QT
    VK = L.vsq_key
    p.op("s", lambda e: e.activation(L.vsq[:], vsb[:], AF.Square),
         reads=(vkey,), writes=(VK,))

    def statmm(e):
        for k in range(PC):
            e.matmul(L.psr[:, 0:QT], L.onesc[:], vsb[:, k * QT:(k + 1) * QT],
                     start=(k == 0), stop=(k == PC - 1))
        for k in range(PC):
            mm = e.matmul(L.psr[:, QT:2 * QT], L.onesc[:],
                          L.vsq[:, k * QT:(k + 1) * QT],
                          start=(k == 0), stop=(k == PC - 1))
        return mm
    p.op("t", statmm, reads=(vkey, VK, "ln_onesc"), writes=("ln_psr",))
    p.op("v", lambda e: e.tensor_copy(L.rv[:], L.psr[0:1, 0:QT]),
         reads=("ln_psr",), writes=("ln_rv",))
    p.op("v", lambda e: e.tensor_copy(L.rq[:], L.psr[0:1, QT:2 * QT]),
         reads=("ln_psr",), writes=("ln_rq",))
    p.op("v", lambda e: e.tensor_scalar_mul(L.mrow[:], L.rv[:], 1.0 / D),
         reads=("ln_rv",), writes=("ln_mrow",))
    p.op("v", lambda e: e.tensor_mul(L.rv[:], L.mrow[:], L.mrow[:]),
         reads=("ln_mrow",), writes=("ln_rv",))
    p.op("v", lambda e: e.tensor_scalar_mul(L.rq[:], L.rq[:], 1.0 / D),
         reads=("ln_rq",), writes=("ln_rq",))
    p.op("v", lambda e: e.tensor_sub(L.varrow[:], L.rq[:], L.rv[:]),
         reads=("ln_rq", "ln_rv"), writes=("ln_var",))
    p.op("s", lambda e: e.activation(L.srow[:], L.varrow[:], AF.Sqrt,
                                     bias=L.epsb[0:1, 0:1]),
         reads=("ln_var", "ln_epsb"), writes=("ln_srow",))
    p.op("v", lambda e: e.reciprocal(L.rrow[:], L.srow[:]),
         reads=("ln_srow",), writes=("ln_rrow",))

    def bcast(e):
        e.matmul(L.psb2[:, 0:QT], L.onesr[:], L.mrow[:], start=True, stop=True)
        return e.matmul(L.psb2[:, QT:2 * QT], L.onesr[:], L.rrow[:],
                        start=True, stop=True)
    p.op("t", bcast, reads=("ln_mrow", "ln_rrow", "ln_onesr"), writes=("ln_psb2",))
    p.op("v", lambda e: e.tensor_copy(L.mb[:], L.psb2[:, 0:QT]),
         reads=("ln_psb2",), writes=("ln_mb",))
    p.op("v", lambda e: e.tensor_copy(L.rb[:], L.psb2[:, QT:2 * QT]),
         reads=("ln_psb2",), writes=("ln_rb",))
    p.op("v", lambda e: e.tensor_tensor(
        L.vsq[:].rearrange("p (pc t) -> p pc t", pc=PC),
        vsb[:].rearrange("p (pc t) -> p pc t", pc=PC),
        L.mb[:].unsqueeze(1).broadcast_to((128, PC, QT)), ALU.subtract),
        reads=(vkey, "ln_mb"), writes=(VK,))
    p.op("v", lambda e: e.tensor_tensor(
        out_ap.rearrange("p (pc t) -> p pc t", pc=PC),
        L.vsq[:].rearrange("p (pc t) -> p pc t", pc=PC),
        L.rb[:].unsqueeze(1).broadcast_to((128, PC, QT)), ALU.mult),
        reads=(VK, "ln_rb"), writes=(okey,))


def build_l23f(T, QT=128):
    nc = bass.Bass("TRN2", target_bir_lowering=False)
    HID = 3072
    HC = HID // 128
    NCH = T // QT
    xd = nc.dram_tensor("x", [D, T], F32, kind="ExternalInput").ap()
    wgd = nc.dram_tensor("wgT", [D, HID], I32, kind="ExternalInput").ap()
    wud = nc.dram_tensor("wuT", [D, HID], I32, kind="ExternalInput").ap()
    wdd = nc.dram_tensor("wdT", [HID, D], I32, kind="ExternalInput").ap()
    yd = nc.dram_tensor("y", [D, T], F32, kind="ExternalOutput").ap()

    wg = nc.alloc_sbuf_tensor("wg", [128, PC * HID], BF16)
    wu = nc.alloc_sbuf_tensor("wu", [128, PC * HID], BF16)
    wdb = nc.alloc_sbuf_tensor("wdb", [128, HC * D], BF16)
    xs = nc.alloc_sbuf_tensor("xs", [128, PC * QT], F32)
    xb = nc.alloc_sbuf_tensor("xb", [128, PC * QT], BF16)
    gb = nc.alloc_sbuf_tensor("gb", [128, 2 * QT], BF16)
    hb = nc.alloc_sbuf_tensor("hb", [128, HC * QT], BF16)
    vsb = nc.alloc_sbuf_tensor("vsb", [128, PC * QT], F32)
    psg = [nc.alloc_psum_tensor(f"psg{i}", [128, QT], F32) for i in range(4)]
    L = LNT(nc, QT)

    p = Prog()
    for pc in range(PC):
        p.op("g", lambda e, pc=pc: e.dma_start(
            out=wg[:, pc * HID:(pc + 1) * HID], in_=wgd[pc * 128:(pc + 1) * 128, :]),
            writes=("wg",), dma=True)
        p.op("g", lambda e, pc=pc: e.dma_start(
            out=wu[:, pc * HID:(pc + 1) * HID], in_=wud[pc * 128:(pc + 1) * 128, :]),
            writes=("wu",), dma=True)
    for hc in range(HC):
        p.op("g", lambda e, hc=hc: e.dma_start(
            out=wdb[:, hc * D:(hc + 1) * D], in_=wdd[hc * 128:(hc + 1) * 128, :]),
            writes=("wdb",), dma=True)
    L.init_ops(p)

    for c in range(NCH):
        p.op("y", lambda e, c=c: e.dma_start(
            out=xs[:], in_=xd[:, c * QT:(c + 1) * QT]
            .rearrange("(pc p) t -> p pc t", p=128)), writes=("xs",), dma=True, group="ld")
        p.op("v", lambda e: e.tensor_copy(xb[:], xs[:]), reads=("xs",), writes=("xb",))
        for d in range(HC):
            def gmm(e, d=d):
                for k in range(PC):
                    mm = e.matmul(psg[d % 2][:],
                                  wg[:, k * HID + d * 128:k * HID + (d + 1) * 128],
                                  xb[:, k * QT:(k + 1) * QT],
                                  start=(k == 0), stop=(k == PC - 1))
                return mm
            p.op("t", gmm, reads=("wg", "xb"), writes=(f"psA{d % 2}",))
            p.op("s", lambda e, d=d: e.activation(
                gb[:, (d % 2) * QT:(d % 2 + 1) * QT], psg[d % 2][:], AF.Silu,
                scale=1.0 / 127.0), reads=(f"psA{d % 2}",), writes=(f"gb{d % 2}",))

            def umm(e, d=d):
                for k in range(PC):
                    mm = e.matmul(psg[2 + d % 2][:],
                                  wu[:, k * HID + d * 128:k * HID + (d + 1) * 128],
                                  xb[:, k * QT:(k + 1) * QT],
                                  start=(k == 0), stop=(k == PC - 1))
                return mm
            p.op("t", umm, reads=("wu", "xb"), writes=(f"psB{d % 2}",))
            p.op("v", lambda e, d=d: e.tensor_tensor(
                hb[:, d * QT:(d + 1) * QT], psg[2 + d % 2][:],
                gb[:, (d % 2) * QT:(d % 2 + 1) * QT], ALU.mult),
                reads=(f"psB{d % 2}", f"gb{d % 2}"), writes=(f"hb{d}",))
        hkeys = tuple(f"hb{d}" for d in range(HC))
        for dd in range(PC):
            def dmm(e, dd=dd):
                for k in range(HC):
                    mm = e.matmul(psg[dd % 4][:],
                                  wdb[:, k * D + dd * 128:k * D + (dd + 1) * 128],
                                  hb[:, k * QT:(k + 1) * QT],
                                  start=(k == 0), stop=(k == HC - 1))
                return mm
            p.op("t", dmm, reads=("wdb",) + hkeys,
                 writes=(f"ps{'AB'[dd % 4 // 2]}{dd % 2}",))
            p.op("v", lambda e, dd=dd: e.scalar_tensor_tensor(
                vsb[:, dd * QT:(dd + 1) * QT], psg[dd % 4][:], 1.0 / 16129.0,
                xs[:, dd * QT:(dd + 1) * QT], ALU.mult, ALU.add),
                reads=(f"ps{'AB'[dd % 4 // 2]}{dd % 2}", "xs"), writes=("vsb",))
        emit_postln(p, L, vsb, "vsb", xs[:], "xs")
        p.op("s", lambda e, c=c: e.dma_start(
            out=yd[:, c * QT:(c + 1) * QT].rearrange("(pc p) t -> p pc t", p=128),
            in_=xs[:]), reads=("xs",), writes=("ydone",), dma=True, group="st")
    p.op("y", lambda e: e.nop(), reads=("ydone",))
    p.emit(nc)
    return nc


def build_l3a(T, QT=512):
    nc = bass.Bass("TRN2", target_bir_lowering=False)
    NCH = T // QT
    xd = nc.dram_tensor("x", [D, T], F32, kind="ExternalInput").ap()
    wid = nc.dram_tensor("w_in", [8, 128, 128], F32, kind="ExternalInput").ap()
    wad = nc.dram_tensor("w_a", [8, 128, 128], F32, kind="ExternalInput").ap()
    apd = nc.dram_tensor("ap_pm", [128, PC], F32, kind="ExternalInput").ap()
    yd = nc.dram_tensor("y", [D, T], F32, kind="ExternalOutput").ap()

    wib = nc.alloc_sbuf_tensor("wib", [128, PC * 128], BF16)
    wab = nc.alloc_sbuf_tensor("wab", [128, PC * 128], BF16)
    aps = nc.alloc_sbuf_tensor("aps", [128, PC], F32)
    sp8 = nc.alloc_sbuf_tensor("sp8", [128, PC], F32)
    spt = nc.alloc_sbuf_tensor("spt", [128, PC], F32)
    onec = nc.alloc_sbuf_tensor("onec", [128, 1], F32)
    carry = nc.alloc_sbuf_tensor("carry3", [128, PC], F32)
    xs = nc.alloc_sbuf_tensor("xs", [128, PC * QT], F32)
    xb = nc.alloc_sbuf_tensor("xb", [128, PC * QT], BF16)
    gx = nc.alloc_sbuf_tensor("gx", [128, PC * QT], F32)
    ga = nc.alloc_sbuf_tensor("ga", [128, PC * QT], F32)
    av = nc.alloc_sbuf_tensor("av", [128, PC * QT], F32)
    bv = nc.alloc_sbuf_tensor("bv", [128, PC * QT], F32)
    sc = nc.alloc_sbuf_tensor("sc", [128, PC * QT], F32)
    vsb = nc.alloc_sbuf_tensor("vsb", [128, PC * QT], F32)
    psm = [nc.alloc_psum_tensor(f"psm{i}", [128, QT], F32) for i in range(4)]
    L = LNT(nc, QT)

    p = Prog()
    for h in range(PC):
        p.op("g", lambda e, h=h: e.dma_start(
            out=wib[:, h * 128:(h + 1) * 128], in_=wid[h]), writes=("wib",), dma=True)
        p.op("g", lambda e, h=h: e.dma_start(
            out=wab[:, h * 128:(h + 1) * 128], in_=wad[h]), writes=("wab",), dma=True)
    p.op("y", lambda e: e.dma_start(out=aps[:], in_=apd[:]), writes=("aps",), dma=True)
    L.init_ops(p)
    p.op("v", lambda e: e.memset(onec[:], 1.0), writes=("onec",))
    p.op("v", lambda e: e.memset(carry[:], 0.0), writes=("carry",))
    p.op("s", lambda e: e.activation(spt[:], aps[:], AF.Abs),
         reads=("aps",), writes=("spt",))
    p.op("s", lambda e: e.activation(spt[:], spt[:], AF.Exp, scale=-1.0),
         reads=("spt",), writes=("spt",))
    p.op("s", lambda e: e.activation(spt[:], spt[:], AF.Ln, bias=onec[:]),
         reads=("spt", "onec"), writes=("spt",))
    p.op("s", lambda e: e.activation(sp8[:], aps[:], AF.Relu),
         reads=("aps",), writes=("sp8",))
    p.op("v", lambda e: e.tensor_add(sp8[:], sp8[:], spt[:]),
         reads=("sp8", "spt"), writes=("sp8",))
    p.op("v", lambda e: e.tensor_scalar_mul(sp8[:], sp8[:], -8.0),
         reads=("sp8",), writes=("sp8",))

    for c in range(NCH):
        p.op("y", lambda e, c=c: e.dma_start(
            out=xs[:], in_=xd[:, c * QT:(c + 1) * QT]
            .rearrange("(pc p) t -> p pc t", p=128)), writes=("xs",), dma=True,
            group="ld")
        p.op("v", lambda e: e.tensor_copy(xb[:], xs[:]), reads=("xs",), writes=("xb",))
        for h in range(PC):
            p.op("t", lambda e, h=h: e.matmul(
                psm[h % 2][:], wib[:, h * 128:(h + 1) * 128],
                xb[:, h * QT:(h + 1) * QT], start=True, stop=True),
                reads=("wib", "xb"), writes=(f"pA{h % 2}",))
            p.op("s", lambda e, h=h: e.activation(
                gx[:, h * QT:(h + 1) * QT], psm[h % 2][:], AF.Sigmoid),
                reads=(f"pA{h % 2}",), writes=("gx",))
            p.op("t", lambda e, h=h: e.matmul(
                psm[2 + h % 2][:], wab[:, h * 128:(h + 1) * 128],
                xb[:, h * QT:(h + 1) * QT], start=True, stop=True),
                reads=("wab", "xb"), writes=(f"pB{h % 2}",))
            p.op("s", lambda e, h=h: e.activation(
                ga[:, h * QT:(h + 1) * QT], psm[2 + h % 2][:], AF.Sigmoid),
                reads=(f"pB{h % 2}",), writes=("ga",))
        p.op("v", lambda e: e.tensor_tensor(
            ga[:].rearrange("p (pc t) -> p pc t", pc=PC),
            ga[:].rearrange("p (pc t) -> p pc t", pc=PC),
            sp8[:].unsqueeze(2).broadcast_to((128, PC, QT)), ALU.mult),
            reads=("ga", "sp8"), writes=("ga",))
        p.op("s", lambda e: e.activation(av[:], ga[:], AF.Exp),
             reads=("ga",), writes=("av",))
        p.op("s", lambda e: e.activation(ga[:], ga[:], AF.Exp, scale=2.0),
             reads=("ga",), writes=("ga",))
        p.op("v", lambda e: e.tensor_scalar(ga[:], ga[:], 1.0, -1.0,
                                            ALU.subtract, ALU.mult),
             reads=("ga",), writes=("ga",))
        p.op("s", lambda e: e.activation(ga[:], ga[:], AF.Sqrt),
             reads=("ga",), writes=("ga",))
        p.op("v", lambda e: e.tensor_mul(bv[:], gx[:], xs[:]),
             reads=("gx", "xs"), writes=("bv",))
        p.op("v", lambda e: e.tensor_mul(bv[:], bv[:], ga[:]),
             reads=("bv", "ga"), writes=("bv",))
        for pc in range(PC):
            p.op("v", lambda e, pc=pc: e.tensor_tensor_scan(
                sc[:, pc * QT:(pc + 1) * QT], av[:, pc * QT:(pc + 1) * QT],
                bv[:, pc * QT:(pc + 1) * QT], carry[:, pc:pc + 1],
                ALU.mult, ALU.add), reads=("av", "bv", "carry"), writes=("sc", f"sc{pc}"))
        p.op("v", lambda e: e.tensor_copy(
            carry[:], sc[:].rearrange("p (pc t) -> p pc t", pc=PC)[:, :, QT - 1]),
            reads=("sc",), writes=("carry",))
        p.op("v", lambda e: e.tensor_add(vsb[:], sc[:], xs[:]),
             reads=("sc", "xs"), writes=("vsb",))
        emit_postln(p, L, vsb, "vsb", xs[:], "xs")
        p.op("s", lambda e, c=c: e.dma_start(
            out=yd[:, c * QT:(c + 1) * QT].rearrange("(pc p) t -> p pc t", p=128),
            in_=xs[:]), reads=("xs",), writes=("ydone",), dma=True, group="st")
    p.op("y", lambda e: e.nop(), reads=("ydone",))
    p.emit(nc)
    return nc


def build_l1a(T, QT=512):
    nc = bass.Bass("TRN2", target_bir_lowering=False)
    NCH = T // QT
    H = 16
    xd = nc.dram_tensor("x", [D, T], F32, kind="ExternalInput").ap()
    qd = nc.dram_tensor("q_pm", [128, PC], F32, kind="ExternalInput").ap()
    kd = nc.dram_tensor("k_pm", [128, PC], F32, kind="ExternalInput").ap()
    nwd = nc.dram_tensor("nw_pm", [128, PC], F32, kind="ExternalInput").ap()
    ldd = nc.dram_tensor("ld_row", [1, H], F32, kind="ExternalInput").ap()
    owd = nc.dram_tensor("owT", [D, D], F32, kind="ExternalInput").ap()
    hvd = nc.dram_tensor("halves", [128, 2], F32, kind="ExternalInput").ap()
    hmd = nc.dram_tensor("hm16", [H, PC], F32, kind="ExternalInput").ap()
    h64d = nc.dram_tensor("h64", [H, 128], F32, kind="ExternalInput").ap()
    idd = nc.dram_tensor("ident", [128, 128], F32, kind="ExternalInput").ap()
    yd = nc.dram_tensor("y", [D, T], F32, kind="ExternalOutput").ap()

    ow = nc.alloc_sbuf_tensor("ow", [128, PC * D], BF16)
    qs = nc.alloc_sbuf_tensor("qs", [128, PC], F32)
    ks = nc.alloc_sbuf_tensor("ks", [128, PC], F32)
    nw = nc.alloc_sbuf_tensor("nw", [128, PC], F32)
    lds = nc.alloc_sbuf_tensor("lds", [1, H], F32)
    hvs = nc.alloc_sbuf_tensor("hvs", [128, 2], F32)
    hms = nc.alloc_sbuf_tensor("hms", [H, PC], F32)
    h64s = nc.alloc_sbuf_tensor("h64s", [H, 128], F32)
    ids = nc.alloc_sbuf_tensor("ids", [128, 128], F32)
    qkp = nc.alloc_sbuf_tensor("qkp", [128, PC], F32)
    QH = nc.alloc_sbuf_tensor("QH", [128, H], F32)
    onec = nc.alloc_sbuf_tensor("onec", [128, 1], F32)
    col16 = nc.alloc_sbuf_tensor("col16", [H, 1], F32)
    lamcol16 = nc.alloc_sbuf_tensor("lamcol16", [H, 1], F32)
    rhs168 = nc.alloc_sbuf_tensor("rhs168", [H, PC], F32)
    qkc = nc.alloc_sbuf_tensor("qkc", [128, PC], F32)
    lamc = nc.alloc_sbuf_tensor("lamc", [128, PC], F32)
    eps2 = nc.alloc_sbuf_tensor("eps2", [1, 1], F32)
    carry = nc.alloc_sbuf_tensor("carry1", [128, PC], F32)
    xs = nc.alloc_sbuf_tensor("xs", [128, PC * QT], F32)
    sc = nc.alloc_sbuf_tensor("sc", [128, PC * QT], F32)
    ov = nc.alloc_sbuf_tensor("ov", [128, PC * QT], F32)
    osq = nc.alloc_sbuf_tensor("osq", [128, PC * QT], F32)
    ob = nc.alloc_sbuf_tensor("ob", [128, PC * QT], BF16)
    vsb = nc.alloc_sbuf_tensor("vsb", [128, PC * QT], F32)
    ror = nc.alloc_sbuf_tensor("ror", [1, QT], F32)
    sor = nc.alloc_sbuf_tensor("sor", [1, QT], F32)
    rob = nc.alloc_sbuf_tensor("rob", [128, QT], F32)
    psm = [nc.alloc_psum_tensor(f"psm{i}", [128, QT], F32) for i in range(2)]
    psro = nc.alloc_psum_tensor("psro", [2, QT], F32)
    L = LNT(nc, QT)
    psb1 = L.psb2[:, 0:QT]
    psi = psm[0][:, 0:2 + PC]

    p = Prog()
    for pc in range(PC):
        p.op("g", lambda e, pc=pc: e.dma_start(
            out=ow[:, pc * D:(pc + 1) * D], in_=owd[pc * 128:(pc + 1) * 128, :]),
            writes=("ow",), dma=True)
    for nm, dst, srcap in (("qs", qs, qd), ("ks", ks, kd), ("nw", nw, nwd),
                           ("lds", lds, ldd), ("hvs", hvs, hvd), ("hms", hms, hmd),
                           ("h64s", h64s, h64d), ("ids", ids, idd)):
        p.op("y", lambda e, dst=dst, srcap=srcap: e.dma_start(out=dst[:], in_=srcap[:]),
             writes=(nm,), dma=True)
    L.init_ops(p)
    p.op("v", lambda e: e.memset(onec[:], 1.0), writes=("onec",))
    p.op("v", lambda e: e.memset(eps2[:], 1e-6), writes=("eps2",))
    p.op("v", lambda e: e.memset(carry[:], 0.0), writes=("carry",))
    p.op("v", lambda e: e.tensor_mul(qkp[:], qs[:], ks[:]),
         reads=("qs", "ks"), writes=("qkp",))
    p.op("v", lambda e: e.tensor_tensor(
        QH[:].rearrange("p (a b) -> p a b", a=PC),
        qkp[:].unsqueeze(2).broadcast_to((128, PC, 2)),
        hvs[:].unsqueeze(1).broadcast_to((128, PC, 2)), ALU.mult),
        reads=("qkp", "hvs"), writes=("QH",))
    p.op("t", lambda e: e.matmul(psi[0:H, 0:1], QH[:], onec[:],
                                 start=True, stop=True),
         reads=("QH", "onec"), writes=("pA0",))
    p.op("v", lambda e: e.tensor_copy(col16[:], psi[0:H, 0:1]),
         reads=("pA0",), writes=("col16",))
    p.op("s", lambda e: e.activation(lds[:], lds[:], AF.Sigmoid),
         reads=("lds",), writes=("lds",))
    p.op("t", lambda e: e.transpose(psi[0:H, 1:2], lds[:], ids[0:1, 0:1]),
         reads=("lds", "ids"), writes=("pA0",))
    p.op("v", lambda e: e.tensor_copy(lamcol16[:], psi[0:H, 1:2]),
         reads=("pA0",), writes=("lamcol16",))
    for nm, col, out in (("qkc", col16, qkc), ("lamc", lamcol16, lamc)):
        p.op("v", lambda e, col=col: e.tensor_tensor(
            rhs168[:], hms[:], col[:].broadcast_to((H, PC)), ALU.mult),
            reads=("hms", nm + "_src" if False else ("col16" if col is col16 else "lamcol16")),
            writes=("rhs168",))
        p.op("t", lambda e: e.matmul(psi[:, 2:2 + PC], h64s[:], rhs168[:],
                                     start=True, stop=True),
             reads=("rhs168", "h64s"), writes=("pA0",))
        p.op("v", lambda e, out=out: e.tensor_copy(out[:], psi[:, 2:2 + PC]),
             reads=("pA0",), writes=(nm,))

    for c in range(NCH):
        p.op("y", lambda e, c=c: e.dma_start(
            out=xs[:], in_=xd[:, c * QT:(c + 1) * QT]
            .rearrange("(pc p) t -> p pc t", p=128)), writes=("xs",), dma=True,
            group="ld")
        for pc in range(PC):
            p.op("v", lambda e, pc=pc: e.tensor_tensor_scan(
                sc[:, pc * QT:(pc + 1) * QT],
                lamc[:, pc:pc + 1].broadcast_to((128, QT)),
                xs[:, pc * QT:(pc + 1) * QT], carry[:, pc:pc + 1],
                ALU.mult, ALU.add), reads=("lamc", "xs", "carry"), writes=("sc",))
        p.op("v", lambda e: e.tensor_copy(
            carry[:], sc[:].rearrange("p (pc t) -> p pc t", pc=PC)[:, :, QT - 1]),
            reads=("sc",), writes=("carry",))
        p.op("v", lambda e: e.tensor_tensor(
            ov[:].rearrange("p (pc t) -> p pc t", pc=PC),
            sc[:].rearrange("p (pc t) -> p pc t", pc=PC),
            qkc[:].unsqueeze(2).broadcast_to((128, PC, QT)), ALU.mult),
            reads=("sc", "qkc"), writes=("ov",))
        p.op("s", lambda e: e.activation(osq[:], ov[:], AF.Square),
             reads=("ov",), writes=("osq",))

        def rmsmm(e):
            for k in range(PC):
                mm = e.matmul(psro[:], L.onesc[:], osq[:, k * QT:(k + 1) * QT],
                              start=(k == 0), stop=(k == PC - 1))
            return mm
        p.op("t", rmsmm, reads=("osq", "ln_onesc"), writes=("psro",))
        p.op("v", lambda e: e.tensor_copy(ror[:], psro[0:1, :]),
             reads=("psro",), writes=("ror",))
        p.op("v", lambda e: e.tensor_scalar_mul(ror[:], ror[:], 1.0 / D),
             reads=("ror",), writes=("ror",))
        p.op("s", lambda e: e.activation(sor[:], ror[:], AF.Sqrt, bias=eps2[0:1, 0:1]),
             reads=("ror", "eps2"), writes=("sor",))
        p.op("v", lambda e: e.reciprocal(ror[:], sor[:]), reads=("sor",), writes=("ror",))
        p.op("t", lambda e: e.matmul(psb1, L.onesr[:], ror[:], start=True, stop=True),
             reads=("ror", "ln_onesr"), writes=("ln_psb2",))
        p.op("v", lambda e: e.tensor_copy(rob[:], psb1), reads=("ln_psb2",), writes=("rob",))
        p.op("v", lambda e: e.tensor_tensor(
            osq[:].rearrange("p (pc t) -> p pc t", pc=PC),
            ov[:].rearrange("p (pc t) -> p pc t", pc=PC),
            rob[:].unsqueeze(1).broadcast_to((128, PC, QT)), ALU.mult),
            reads=("ov", "rob"), writes=("osq",))
        p.op("v", lambda e: e.tensor_tensor(
            ob[:].rearrange("p (pc t) -> p pc t", pc=PC),
            osq[:].rearrange("p (pc t) -> p pc t", pc=PC),
            nw[:].unsqueeze(2).broadcast_to((128, PC, QT)), ALU.mult),
            reads=("osq", "nw"), writes=("ob",))
        for dd in range(PC):
            def omm(e, dd=dd):
                for k in range(PC):
                    mm = e.matmul(psm[dd % 2][:],
                                  ow[:, k * D + dd * 128:k * D + (dd + 1) * 128],
                                  ob[:, k * QT:(k + 1) * QT],
                                  start=(k == 0), stop=(k == PC - 1))
                return mm
            p.op("t", omm, reads=("ow", "ob"), writes=(f"pO{dd % 2}",))
            p.op("v", lambda e, dd=dd: e.tensor_tensor(
                vsb[:, dd * QT:(dd + 1) * QT], psm[dd % 2][:],
                xs[:, dd * QT:(dd + 1) * QT], ALU.add),
                reads=(f"pO{dd % 2}", "xs"), writes=("vsb",))
        emit_postln(p, L, vsb, "vsb", xs[:], "xs")
        p.op("s", lambda e, c=c: e.dma_start(
            out=yd[:, c * QT:(c + 1) * QT].rearrange("(pc p) t -> p pc t", p=128),
            in_=xs[:]), reads=("xs",), writes=("ydone",), dma=True, group="st")
    p.op("y", lambda e: e.nop(), reads=("ydone",))
    p.emit(nc)
    return nc


def build_l0f(T):
    QT = 128
    nc = bass.Bass("TRN2", target_bir_lowering=False)
    NCH = T // QT
    xd = nc.dram_tensor("x", [D, T], F32, kind="ExternalInput").ap()
    wd = nc.dram_tensor("w", [D, D], F32, kind="ExternalInput").ap()
    bd = nc.dram_tensor("bias_pm", [128, PC], F32, kind="ExternalInput").ap()
    idd = nc.dram_tensor("ident", [128, 128], F32, kind="ExternalInput").ap()
    yd = nc.dram_tensor("y", [D, T], F32, kind="ExternalOutput").ap()

    wf = nc.alloc_sbuf_tensor("wf", [128, PC * D], F32)
    wqn = nc.alloc_sbuf_tensor("wqn", [128, PC * D], BF16)
    wqT = nc.alloc_sbuf_tensor("wqT", [128, PC * D], BF16)
    wsc = nc.alloc_sbuf_tensor("wsc", [128, PC], F32)
    wrs = nc.alloc_sbuf_tensor("wrs", [128, PC], F32)
    wi32 = nc.alloc_sbuf_tensor("wi32", [128, D], I32)
    bias = nc.alloc_sbuf_tensor("bias", [128, PC], F32)
    ids = nc.alloc_sbuf_tensor("ids", [128, 128], F32)
    idb = nc.alloc_sbuf_tensor("idb", [128, 128], BF16)
    xs = nc.alloc_sbuf_tensor("xs", [128, PC * QT], F32)
    amax = nc.alloc_sbuf_tensor("amax", [128, 1], F32)
    am2 = nc.alloc_sbuf_tensor("am2", [128, 2], F32)
    amr1 = nc.alloc_sbuf_tensor("amr1", [1, 128], F32)
    amr2 = nc.alloc_sbuf_tensor("amr2", [1, 128], F32)
    rsb = nc.alloc_sbuf_tensor("rsb", [128, 2 * QT], F32)
    xq1 = nc.alloc_sbuf_tensor("xq1", [128, PC * QT], F32)
    xqi = nc.alloc_sbuf_tensor("xqi", [128, PC * QT], I32)
    xqf = nc.alloc_sbuf_tensor("xqf", [128, PC * QT], F32)
    xqb = nc.alloc_sbuf_tensor("xqb", [128, PC * QT], BF16)
    vsb = nc.alloc_sbuf_tensor("vsb", [128, PC * QT], F32)
    t1s = nc.alloc_sbuf_tensor("t1s", [128, QT], F32)
    pst = [nc.alloc_psum_tensor(f"pst{i}", [128, 128], F32) for i in range(2)]
    psq = [nc.alloc_psum_tensor(f"psq{i}", [128, QT], F32) for i in range(2)]
    psb = nc.alloc_psum_tensor("psbc", [128, 2 * QT], F32)
    L = LNT(nc, QT)

    p = Prog()
    p.op("y", lambda e: e.dma_start(out=ids[:], in_=idd[:]), writes=("ids",), dma=True)
    p.op("y", lambda e: e.dma_start(out=bias[:], in_=bd[:]), writes=("bias",), dma=True)
    for pc in range(PC):
        p.op("g", lambda e, pc=pc: e.dma_start(
            out=wf[:, pc * D:(pc + 1) * D], in_=wd[pc * 128:(pc + 1) * 128, :]),
            writes=("wf",), dma=True)
    L.init_ops(p)
    p.op("v", lambda e: e.tensor_copy(idb[:], ids[:]), reads=("ids",), writes=("idb",))
    for pc in range(PC):
        p.op("v", lambda e, pc=pc: e.tensor_reduce(
            wsc[:, pc:pc + 1], wf[:, pc * D:(pc + 1) * D], AX.X, ALU.max,
            apply_absolute_value=True), reads=("wf",), writes=("wsc",))
    p.op("v", lambda e: e.tensor_scalar_max(wsc[:], wsc[:], 1e-8),
         reads=("wsc",), writes=("wsc",))
    p.op("v", lambda e: e.tensor_scalar_mul(wsc[:], wsc[:], 1.0 / 127.0),
         reads=("wsc",), writes=("wsc",))
    p.op("v", lambda e: e.reciprocal(wrs[:], wsc[:]), reads=("wsc",), writes=("wrs",))
    for pc in range(PC):
        p.op("v", lambda e, pc=pc: e.tensor_scalar(
            wf[:, pc * D:(pc + 1) * D], wf[:, pc * D:(pc + 1) * D],
            wrs[:, pc:pc + 1], None, ALU.mult), reads=("wf", "wrs"), writes=("wf",))
    for pc in range(PC):
        emit_rneq(p, wf[:, pc * D:(pc + 1) * D], "wf", wi32[:], "wi32",
                  xq1[:, 0:D], "xq1", vsb[:, 0:D], "vsb",
                  wqn[:, pc * D:(pc + 1) * D], "wqn")
    for a in range(PC):
        for bb in range(PC):
            p.op("t", lambda e, a=a, bb=bb: e.transpose(
                pst[(a * PC + bb) % 2][:].bitcast(BF16)[:, 0:128],
                wqn[:, a * D + bb * 128:a * D + (bb + 1) * 128], idb[:]),
                reads=("wqn", "idb"), writes=(f"pt{(a * PC + bb) % 2}",))
            p.op("v", lambda e, a=a, bb=bb: e.tensor_copy(
                wqT[:, bb * D + a * 128:bb * D + (a + 1) * 128],
                pst[(a * PC + bb) % 2][:].bitcast(BF16)[:, 0:128]),
                reads=(f"pt{(a * PC + bb) % 2}",), writes=("wqT",))

    for c in range(NCH):
        p.op("y", lambda e, c=c: e.dma_start(
            out=xs[:], in_=xd[:, c * QT:(c + 1) * QT]
            .rearrange("(pc p) t -> p pc t", p=128)), writes=("xs",), dma=True,
            group="ld")
        for pc in range(PC):
            p.op("t", lambda e, pc=pc: e.transpose(
                pst[pc % 2][:], xs[:, pc * QT:(pc + 1) * QT], ids[:]),
                reads=("xs", "ids"), writes=(f"pt{pc % 2}",))
            if pc == 0:
                p.op("v", lambda e: e.tensor_reduce(
                    amax[:], pst[0][:], AX.X, ALU.max, apply_absolute_value=True),
                    reads=("pt0",), writes=("amax",))
            else:
                p.op("v", lambda e, pc=pc: e.tensor_reduce(
                    t1s[:, 0:1], pst[pc % 2][:], AX.X, ALU.max,
                    apply_absolute_value=True), reads=(f"pt{pc % 2}",), writes=("t1s",))
                p.op("v", lambda e: e.tensor_max(amax[:], amax[:], t1s[:, 0:1]),
                     reads=("amax", "t1s"), writes=("amax",))
        p.op("v", lambda e: e.tensor_scalar_max(amax[:], amax[:], 1e-8),
             reads=("amax",), writes=("amax",))
        p.op("v", lambda e: e.reciprocal(am2[:, 0:1], amax[:]),
             reads=("amax",), writes=("am2",))
        p.op("v", lambda e: e.tensor_scalar_mul(am2[:, 0:1], am2[:, 0:1], 127.0),
             reads=("am2",), writes=("am2",))
        p.op("v", lambda e: e.tensor_scalar_mul(am2[:, 1:2], amax[:], 1.0 / 127.0),
             reads=("amax",), writes=("am2",))
        p.op("t", lambda e: e.transpose(pst[0][0:1, 0:128], am2[:, 0:1], ids[:]),
             reads=("am2", "ids"), writes=("pt0",))
        p.op("t", lambda e: e.transpose(pst[1][0:1, 0:128], am2[:, 1:2], ids[:]),
             reads=("am2", "ids"), writes=("pt1",))
        p.op("v", lambda e: e.tensor_copy(amr1[:], pst[0][0:1, 0:128]),
             reads=("pt0",), writes=("amr1",))
        p.op("v", lambda e: e.tensor_copy(amr2[:], pst[1][0:1, 0:128]),
             reads=("pt1",), writes=("amr2",))
        def bc2(e):
            e.matmul(psb[:, 0:QT], L.onesr[:], amr1[:], start=True, stop=True)
            return e.matmul(psb[:, QT:2 * QT], L.onesr[:], amr2[:],
                            start=True, stop=True)
        p.op("t", bc2, reads=("amr1", "amr2", "ln_onesr"), writes=("psbc",))
        p.op("v", lambda e: e.tensor_copy(rsb[:], psb[:]), reads=("psbc",), writes=("rsb",))
        p.op("v", lambda e: e.tensor_tensor(
            xq1[:].rearrange("p (pc t) -> p pc t", pc=PC),
            xs[:].rearrange("p (pc t) -> p pc t", pc=PC),
            rsb[:, 0:QT].unsqueeze(1).broadcast_to((128, PC, QT)), ALU.mult),
            reads=("xs", "rsb"), writes=("xq1",))
        emit_rneq(p, xq1[:], "xq1", xqi[:], "xqi", xqf[:], "xqf",
                  vsb[:], "vsb", xqb[:], "xqb")
        for dd in range(PC):
            def qmm(e, dd=dd):
                for k in range(PC):
                    mm = e.matmul(psq[dd % 2][:],
                                  wqT[:, k * D + dd * 128:k * D + (dd + 1) * 128],
                                  xqb[:, k * QT:(k + 1) * QT],
                                  start=(k == 0), stop=(k == PC - 1))
                return mm
            p.op("t", qmm, reads=("wqT", "xqb"), writes=(f"pq{dd % 2}",))
            p.op("v", lambda e, dd=dd: e.scalar_tensor_tensor(
                xq1[:, dd * QT:(dd + 1) * QT], psq[dd % 2][:], wsc[:, dd:dd + 1],
                rsb[:, QT:2 * QT], ALU.mult, ALU.mult),
                reads=(f"pq{dd % 2}", "wsc", "rsb"), writes=("xq1",))
            p.op("v", lambda e, dd=dd: e.scalar_tensor_tensor(
                vsb[:, dd * QT:(dd + 1) * QT], xq1[:, dd * QT:(dd + 1) * QT],
                bias[:, dd:dd + 1], xs[:, dd * QT:(dd + 1) * QT],
                ALU.add, ALU.add),
                reads=("xq1", "bias", "xs"), writes=("vsb",))
        emit_postln(p, L, vsb, "vsb", xs[:], "xs")
        p.op("s", lambda e, c=c: e.dma_start(
            out=yd[:, c * QT:(c + 1) * QT].rearrange("(pc p) t -> p pc t", p=128),
            in_=xs[:]), reads=("xs",), writes=("ydone",), dma=True, group="st")
    p.op("y", lambda e: e.nop(), reads=("ydone",))
    p.emit(nc)
    return nc


def build_l1f(T):
    QT = 128
    H2 = 4096
    HC = H2 // 128
    nc = bass.Bass("TRN2", target_bir_lowering=False, dynamic_dma_scratch_size=8192)
    NCH = T // QT
    xd = nc.dram_tensor("x", [D, T], F32, kind="ExternalInput").ap()
    n1d = nc.dram_tensor("nw1_pm", [128, PC], F32, kind="ExternalInput").ap()
    n2d = nc.dram_tensor("nw2_pm", [128, HC], F32, kind="ExternalInput").ap()
    w1d = nc.dram_tensor("w1T", [D, H2], F32, kind="ExternalInput").ap()
    w2d = nc.dram_tensor("w2T", [H2, D], F32, kind="ExternalInput").ap()
    acd = nc.dram_tensor("acts", [1, 2], F32, kind="ExternalInput").ap()
    idd = nc.dram_tensor("ident", [128, 128], F32, kind="ExternalInput").ap()
    yd = nc.dram_tensor("y", [D, T], F32, kind="ExternalOutput").ap()

    wq1 = nc.alloc_sbuf_tensor("wq1", [128, PC * H2], BF16)
    wq2 = nc.alloc_sbuf_tensor("wq2", [128, HC * D], BF16)
    L = LNT(nc, QT)
    absacc = nc.alloc_sbuf_tensor("absacc", [128, 1], F32)
    redt = nc.alloc_sbuf_tensor("redt", [128, 1], F32)
    onec = nc.alloc_sbuf_tensor("onec", [128, 1], F32)
    cells = nc.alloc_sbuf_tensor("cells", [1, 8], F32)
    wsc1c = nc.alloc_sbuf_tensor("wsc1c", [128, 1], F32)
    wsc2c = nc.alloc_sbuf_tensor("wsc2c", [128, 1], F32)
    n1 = nc.alloc_sbuf_tensor("n1", [128, PC], F32)
    n2 = nc.alloc_sbuf_tensor("n2", [128, HC], F32)
    acs = nc.alloc_sbuf_tensor("acs", [1, 2], F32)
    acol = nc.alloc_sbuf_tensor("acol", [128, 1], F32)
    bcol = nc.alloc_sbuf_tensor("bcol", [128, 1], F32)
    ids = nc.alloc_sbuf_tensor("ids", [128, 128], F32)
    eps8 = nc.alloc_sbuf_tensor("eps8", [1, 1], F32)
    xs = nc.alloc_sbuf_tensor("xs", [128, PC * QT], F32)
    q1 = nc.alloc_sbuf_tensor("q1", [128, PC * QT], F32)
    q1g = nc.alloc_sbuf_tensor("q1g", [128, PC * QT], F32)
    q1b = nc.alloc_sbuf_tensor("q1b", [128, PC * QT], BF16)
    hb1 = nc.alloc_sbuf_tensor("hb1", [128, HC * QT], F32)
    h2i = nc.alloc_sbuf_tensor("h2i", [128, HC * QT // 2], I32)
    h2f = nc.alloc_sbuf_tensor("h2f", [128, HC * QT], F32)
    h2g = nc.alloc_sbuf_tensor("h2g", [128, HC * QT], F32)
    h2b = nc.alloc_sbuf_tensor("h2b", [128, HC * QT], BF16)
    amax = nc.alloc_sbuf_tensor("amax", [128, 1], F32)
    gwb = L.mb
    osb = L.rb
    r_am = L.rq
    r_t = L.rrow
    r_st = L.rv
    r_ax = L.mrow
    r_gw = L.varrow
    r_os = L.srow
    pst = [nc.alloc_psum_tensor(f"pstl{i}", [128, 128], F32) for i in range(2)]
    psq = [nc.alloc_psum_tensor(f"psql{i}", [128, QT], F32) for i in range(2)]
    psst = nc.alloc_psum_tensor("psst", [2, 2 * QT], F32)
    psbc = nc.alloc_psum_tensor("psbcl", [128, 2 * QT], F32)

    p = Prog()
    p.op("y", lambda e: e.dma_start(out=ids[:], in_=idd[:]), writes=("ids",), dma=True)
    p.op("y", lambda e: e.dma_start(out=n1[:], in_=n1d[:]), writes=("n1",), dma=True)
    p.op("y", lambda e: e.dma_start(out=n2[:], in_=n2d[:]), writes=("n2",), dma=True)
    p.op("y", lambda e: e.dma_start(out=acs[:], in_=acd[:]), writes=("acs",), dma=True)
    L.init_ops(p)
    p.op("v", lambda e: e.memset(onec[:], 1.0), writes=("onec",))
    p.op("v", lambda e: e.memset(eps8[:], 1e-8), writes=("eps8",))
    def expa(e):
        e.matmul(psbc[:, 0:1], L.onesr[:], acs[0:1, 0:1], start=True, stop=True)
        return e.matmul(psbc[:, 1:2], L.onesr[:], acs[0:1, 1:2], start=True, stop=True)
    p.op("t", expa, reads=("acs", "ln_onesr"), writes=("psbcl",))
    p.op("v", lambda e: e.tensor_copy(acol[:], psbc[:, 0:1]), reads=("psbcl",), writes=("acol",))
    p.op("v", lambda e: e.tensor_copy(bcol[:], psbc[:, 1:2]), reads=("psbcl",), writes=("bcol",))

    for wi, (wdap, nchunk, wqdst, wscc, cellix) in enumerate(
            ((w1d, PC, wq1, wsc1c, 0), (w2d, HC, wq2, wsc2c, 1))):
        wid = H2 if wi == 0 else D
        for s in range(nchunk):
            p.op("y", lambda e, s=s, wdap=wdap, wid=wid: e.dma_start(
                out=h2f[:, 0:wid], in_=wdap[s * 128:(s + 1) * 128, :]),
                writes=("h2f",), dma=True, group=f"wls{wi}")
            p.op("v", lambda e, wid=wid: e.tensor_reduce(
                redt[:], h2f[:, 0:wid], AX.X, ALU.add, apply_absolute_value=True),
                reads=("h2f",), writes=("redt",))
            if s == 0:
                p.op("v", lambda e: e.tensor_copy(absacc[:], redt[:]),
                     reads=("redt",), writes=("absacc",))
            else:
                p.op("v", lambda e: e.tensor_add(absacc[:], absacc[:], redt[:]),
                     reads=("absacc", "redt"), writes=("absacc",))
        p.op("t", lambda e, cellix=cellix: e.matmul(
            psbc[0:1, 2 + cellix:3 + cellix], absacc[:],
            onec[:], start=True, stop=True),
             reads=("absacc", "onec"), writes=("psbcl",))
        p.op("v", lambda e, cellix=cellix: e.tensor_copy(
            cells[0:1, cellix:cellix + 1], psbc[0:1, 2 + cellix:3 + cellix]),
            reads=("psbcl",), writes=("cells",))
        nel = float(1024 * 4096)
        p.op("v", lambda e, cellix=cellix, nel=nel: e.tensor_scalar(
            cells[0:1, 2 + cellix:3 + cellix], cells[0:1, cellix:cellix + 1],
            1.0 / nel, 1e-5, ALU.mult, ALU.max), reads=("cells",), writes=("cells",))
        p.op("v", lambda e, cellix=cellix: e.reciprocal(
            cells[0:1, 4 + cellix:5 + cellix], cells[0:1, 2 + cellix:3 + cellix]),
            reads=("cells",), writes=("cells",))
        p.op("t", lambda e, cellix=cellix: e.matmul(
            psbc[:, 4 + cellix:5 + cellix], L.onesr[:],
            cells[0:1, 4 + cellix:5 + cellix], start=True, stop=True),
            reads=("cells", "ln_onesr"), writes=("psbcl",))
        p.op("v", lambda e, wscc=wscc, cellix=cellix: e.tensor_copy(
            wscc[:], psbc[:, 4 + cellix:5 + cellix]),
             reads=("psbcl",), writes=("wsc1c" if cellix == 0 else "wsc2c",))
        for s in range(nchunk):
            p.op("y", lambda e, s=s, wdap=wdap, wid=wid: e.dma_start(
                out=h2f[:, 0:wid], in_=wdap[s * 128:(s + 1) * 128, :]),
                writes=("h2f",), dma=True, group=f"wls{wi}b")
            p.op("v", lambda e, wid=wid, wscc=wscc: e.tensor_scalar(
                h2f[:, 0:wid], h2f[:, 0:wid], wscc[:, 0:1], None, ALU.mult),
                reads=("h2f", "wsc1c" if wi == 0 else "wsc2c"), writes=("h2f",))
            for hf in range(wid // 2048 if wid > 2048 else 1):
                w0, w1 = hf * 2048, min(wid, (hf + 1) * 2048)
                emit_rneq(p, h2f[:, w0:w1], "h2f", h2i[:, 0:w1 - w0], "h2i",
                          hb1[:, 0:w1 - w0], "hb1", h2g[:, 0:w1 - w0], "h2g",
                          wqdst[:, s * wid + w0:s * wid + w1],
                          "wq1" if wi == 0 else "wq2", lo=-1.0, hi=1.0)

    for c in range(NCH):
        p.op("y", lambda e, c=c: e.dma_start(
            out=xs[:], in_=xd[:, c * QT:(c + 1) * QT]
            .rearrange("(pc p) t -> p pc t", p=128)), writes=("xs",), dma=True,
            group="ld")
        p.op("s", lambda e: e.activation(q1g[:], xs[:], AF.Square),
             reads=("xs",), writes=("q1g",))

        def st1(e):
            for k in range(PC):
                mm = e.matmul(psst[:, 0:QT], L.onesc[:], q1g[:, k * QT:(k + 1) * QT],
                              start=(k == 0), stop=(k == PC - 1))
            return mm
        p.op("t", st1, reads=("q1g", "ln_onesc"), writes=("psst",))
        p.op("v", lambda e: e.tensor_copy(r_st[:], psst[0:1, 0:QT]),
             reads=("psst",), writes=("ln_rv",))
        p.op("v", lambda e: e.tensor_scalar_mul(r_st[:], r_st[:], 1.0 / D),
             reads=("ln_rv",), writes=("ln_rv",))
        p.op("s", lambda e: e.activation(r_t[:], r_st[:], AF.Sqrt, bias=eps8[0:1, 0:1]),
             reads=("ln_rv", "eps8"), writes=("ln_rrow",))
        p.op("v", lambda e: e.reciprocal(r_st[:], r_t[:]), reads=("ln_rrow",), writes=("ln_rv",))
        p.op("v", lambda e: e.tensor_tensor(
            q1[:].rearrange("p (pc t) -> p pc t", pc=PC),
            xs[:].rearrange("p (pc t) -> p pc t", pc=PC),
            n1[:].unsqueeze(2).broadcast_to((128, PC, QT)), ALU.mult),
            reads=("xs", "n1"), writes=("q1",))
        for pc in range(PC):
            p.op("t", lambda e, pc=pc: e.transpose(
                pst[pc % 2][:], q1[:, pc * QT:(pc + 1) * QT], ids[:]),
                reads=("q1", "ids"), writes=(f"ptl{pc % 2}",))
            if pc == 0:
                p.op("v", lambda e: e.tensor_reduce(
                    amax[:], pst[0][:], AX.X, ALU.max, apply_absolute_value=True),
                    reads=("ptl0",), writes=("amax",))
            else:
                p.op("v", lambda e, pc=pc: e.tensor_reduce(
                    redt[:], pst[pc % 2][:], AX.X, ALU.max, apply_absolute_value=True),
                    reads=(f"ptl{pc % 2}",), writes=("redt",))
                p.op("v", lambda e: e.tensor_max(amax[:], amax[:], redt[:]),
                     reads=("amax", "redt"), writes=("amax",))
        p.op("t", lambda e: e.transpose(pst[0][0:1, 0:128], amax[:], ids[:]),
             reads=("amax", "ids"), writes=("ptl0",))
        p.op("v", lambda e: e.tensor_copy(r_am[:], pst[0][0:1, 0:128]),
             reads=("ptl0",), writes=("ln_rq",))
        p.op("v", lambda e: e.tensor_mul(r_ax[:], r_am[:], r_st[:]),
             reads=("ln_rq", "ln_rv"), writes=("ln_mrow",))
        p.op("v", lambda e: e.tensor_scalar_max(r_ax[:], r_ax[:], 1e-5),
             reads=("ln_mrow",), writes=("ln_mrow",))
        p.op("v", lambda e: e.reciprocal(r_t[:], r_ax[:]), reads=("ln_mrow",), writes=("ln_rrow",))
        p.op("v", lambda e: e.tensor_mul(r_gw[:], r_t[:], r_st[:]),
             reads=("ln_rrow", "ln_rv"), writes=("ln_var",))
        p.op("v", lambda e: e.tensor_scalar_mul(r_gw[:], r_gw[:], 127.0),
             reads=("ln_var",), writes=("ln_var",))
        p.op("v", lambda e: e.tensor_scalar(r_os[:], r_ax[:], cells[0:1, 2:3],
                                            1.0 / 127.0, ALU.mult, ALU.mult),
             reads=("ln_mrow", "cells"), writes=("ln_srow",))

        def bc1(e):
            e.matmul(psbc[:, 0:QT], L.onesr[:], r_gw[:], start=True, stop=True)
            return e.matmul(psbc[:, QT:2 * QT], L.onesr[:], r_os[:],
                            start=True, stop=True)
        p.op("t", bc1, reads=("ln_var", "ln_srow", "ln_onesr"), writes=("psbcl",))
        p.op("v", lambda e: e.tensor_copy(gwb[:], psbc[:, 0:QT]),
             reads=("psbcl",), writes=("ln_mb",))
        p.op("v", lambda e: e.tensor_copy(osb[:], psbc[:, QT:2 * QT]),
             reads=("psbcl",), writes=("ln_rb",))
        p.op("v", lambda e: e.tensor_tensor(
            q1[:].rearrange("p (pc t) -> p pc t", pc=PC),
            q1[:].rearrange("p (pc t) -> p pc t", pc=PC),
            gwb[:].unsqueeze(1).broadcast_to((128, PC, QT)), ALU.mult),
            reads=("q1", "ln_mb"), writes=("q1",))
        emit_rneq(p, q1[:], "q1", h2i[:, 0:PC * QT], "h2i",
                  h2f[:, 0:PC * QT], "h2f", q1g[:], "q1g", q1b[:], "q1b")
        for dd in range(HC):
            def m1(e, dd=dd):
                for k in range(PC):
                    mm = e.matmul(psq[dd % 2][:],
                                  wq1[:, k * H2 + dd * 128:k * H2 + (dd + 1) * 128],
                                  q1b[:, k * QT:(k + 1) * QT],
                                  start=(k == 0), stop=(k == PC - 1))
                return mm
            p.op("t", m1, reads=("wq1", "q1b"), writes=(f"pql{dd % 2}",))
            p.op("s", lambda e, dd=dd: e.activation(
                hb1[:, dd * QT:(dd + 1) * QT], psq[dd % 2][:], AF.Relu),
                reads=(f"pql{dd % 2}",), writes=("hb1",))
        p.op("v", lambda e: e.tensor_mul(h2g[:], hb1[:], hb1[:]),
             reads=("hb1",), writes=("h2g",))
        p.op("v", lambda e: e.tensor_mul(r_t[:], r_os[:], r_os[:]),
             reads=("ln_srow",), writes=("ln_rrow",))
        p.op("v", lambda e: e.tensor_scalar(r_t[:], r_t[:], acs[0:1, 0:1], None,
                                            ALU.mult),
             reads=("ln_rrow", "acs"), writes=("ln_rrow",))
        p.op("t", lambda e: e.matmul(psbc[:, 0:QT], L.onesr[:], r_t[:],
                                     start=True, stop=True),
             reads=("ln_rrow", "ln_onesr"), writes=("psbcl",))
        p.op("v", lambda e: e.tensor_copy(gwb[:], psbc[:, 0:QT]),
             reads=("psbcl",), writes=("ln_mb",))
        p.op("v", lambda e: e.tensor_tensor(
            hb1[:].rearrange("p (hc t) -> p hc t", hc=HC),
            h2g[:].rearrange("p (hc t) -> p hc t", hc=HC),
            gwb[:].unsqueeze(1).broadcast_to((128, HC, QT)), ALU.mult),
            reads=("h2g", "ln_mb"), writes=("hb1",))
        p.op("v", lambda e: e.tensor_scalar(hb1[:], hb1[:], bcol[:, 0:1], None,
                                            ALU.add),
             reads=("hb1", "bcol"), writes=("hb1",))
        p.op("s", lambda e: e.activation(h2g[:], hb1[:], AF.Square),
             reads=("hb1",), writes=("h2g",))

        def st2(e):
            for k in range(HC):
                mm = e.matmul(psst[:, QT:2 * QT], L.onesc[:],
                              h2g[:, k * QT:(k + 1) * QT],
                              start=(k == 0), stop=(k == HC - 1))
            return mm
        p.op("t", st2, reads=("h2g", "ln_onesc"), writes=("psst",))
        p.op("v", lambda e: e.tensor_copy(r_st[:], psst[0:1, QT:2 * QT]),
             reads=("psst",), writes=("ln_rv",))
        p.op("v", lambda e: e.tensor_scalar_mul(r_st[:], r_st[:], 1.0 / H2),
             reads=("ln_rv",), writes=("ln_rv",))
        p.op("s", lambda e: e.activation(r_t[:], r_st[:], AF.Sqrt, bias=eps8[0:1, 0:1]),
             reads=("ln_rv", "eps8"), writes=("ln_rrow",))
        p.op("v", lambda e: e.reciprocal(r_st[:], r_t[:]), reads=("ln_rrow",), writes=("ln_rv",))
        p.op("v", lambda e: e.tensor_tensor(
            h2g[:].rearrange("p (hc t) -> p hc t", hc=HC),
            hb1[:].rearrange("p (hc t) -> p hc t", hc=HC),
            n2[:].unsqueeze(2).broadcast_to((128, HC, QT)), ALU.mult),
            reads=("hb1", "n2"), writes=("h2g",))
        for hc in range(HC):
            p.op("t", lambda e, hc=hc: e.transpose(
                pst[hc % 2][:], h2g[:, hc * QT:(hc + 1) * QT], ids[:]),
                reads=("h2g", "ids"), writes=(f"ptl{hc % 2}",))
            if hc == 0:
                p.op("v", lambda e: e.tensor_reduce(
                    amax[:], pst[0][:], AX.X, ALU.max, apply_absolute_value=True),
                    reads=("ptl0",), writes=("amax",))
            else:
                p.op("v", lambda e, hc=hc: e.tensor_reduce(
                    redt[:], pst[hc % 2][:], AX.X, ALU.max, apply_absolute_value=True),
                    reads=(f"ptl{hc % 2}",), writes=("redt",))
                p.op("v", lambda e: e.tensor_max(amax[:], amax[:], redt[:]),
                     reads=("amax", "redt"), writes=("amax",))
        p.op("t", lambda e: e.transpose(pst[0][0:1, 0:128], amax[:], ids[:]),
             reads=("amax", "ids"), writes=("ptl0",))
        p.op("v", lambda e: e.tensor_copy(r_am[:], pst[0][0:1, 0:128]),
             reads=("ptl0",), writes=("ln_rq",))
        p.op("v", lambda e: e.tensor_mul(r_ax[:], r_am[:], r_st[:]),
             reads=("ln_rq", "ln_rv"), writes=("ln_mrow",))
        p.op("v", lambda e: e.tensor_scalar_max(r_ax[:], r_ax[:], 1e-5),
             reads=("ln_mrow",), writes=("ln_mrow",))
        p.op("v", lambda e: e.reciprocal(r_t[:], r_ax[:]), reads=("ln_mrow",), writes=("ln_rrow",))
        p.op("v", lambda e: e.tensor_mul(r_gw[:], r_t[:], r_st[:]),
             reads=("ln_rrow", "ln_rv"), writes=("ln_var",))
        p.op("v", lambda e: e.tensor_scalar_mul(r_gw[:], r_gw[:], 127.0),
             reads=("ln_var",), writes=("ln_var",))
        p.op("v", lambda e: e.tensor_scalar(r_os[:], r_ax[:], cells[0:1, 3:4],
                                            1.0 / 127.0, ALU.mult, ALU.mult),
             reads=("ln_mrow", "cells"), writes=("ln_srow",))

        def bc2(e):
            e.matmul(psbc[:, 0:QT], L.onesr[:], r_gw[:], start=True, stop=True)
            return e.matmul(psbc[:, QT:2 * QT], L.onesr[:], r_os[:],
                            start=True, stop=True)
        p.op("t", bc2, reads=("ln_var", "ln_srow", "ln_onesr"), writes=("psbcl",))
        p.op("v", lambda e: e.tensor_copy(gwb[:], psbc[:, 0:QT]),
             reads=("psbcl",), writes=("ln_mb",))
        p.op("v", lambda e: e.tensor_copy(osb[:], psbc[:, QT:2 * QT]),
             reads=("psbcl",), writes=("ln_rb",))
        p.op("v", lambda e: e.tensor_tensor(
            h2g[:].rearrange("p (hc t) -> p hc t", hc=HC),
            h2g[:].rearrange("p (hc t) -> p hc t", hc=HC),
            gwb[:].unsqueeze(1).broadcast_to((128, HC, QT)), ALU.mult),
            reads=("h2g", "ln_mb"), writes=("h2g",))
        HW2 = HC * QT // 2
        for half in range(2):
            emit_rneq(p, h2g[:, half * HW2:(half + 1) * HW2], "h2g",
                      h2i[:], "h2i", h2f[:, 0:HW2], "h2f",
                      hb1[:, 0:HW2], "hb1",
                      h2b[:, half * HW2:(half + 1) * HW2], "h2b")
        for dd in range(PC):
            def m2(e, dd=dd):
                for k in range(HC):
                    mm = e.matmul(psq[dd % 2][:],
                                  wq2[:, k * D + dd * 128:k * D + (dd + 1) * 128],
                                  h2b[:, k * QT:(k + 1) * QT],
                                  start=(k == 0), stop=(k == HC - 1))
                return mm
            p.op("t", m2, reads=("wq2", "h2b"), writes=(f"pql{dd % 2}",))
            p.op("v", lambda e, dd=dd: e.tensor_tensor(
                q1[:, dd * QT:(dd + 1) * QT], psq[dd % 2][:], osb[:], ALU.mult),
                reads=(f"pql{dd % 2}", "ln_rb"), writes=("q1",))
            p.op("v", lambda e, dd=dd: e.tensor_add(
                q1g[:, dd * QT:(dd + 1) * QT], q1[:, dd * QT:(dd + 1) * QT],
                xs[:, dd * QT:(dd + 1) * QT]),
                reads=("q1", "xs"), writes=("q1g",))
        emit_postln(p, L, q1g[:, 0:PC * QT], "q1g", xs[:], "xs")
        p.op("s", lambda e, c=c: e.dma_start(
            out=yd[:, c * QT:(c + 1) * QT].rearrange("(pc p) t -> p pc t", p=128),
            in_=xs[:]), reads=("xs",), writes=("ydone",), dma=True, group="st")
    p.op("y", lambda e: e.nop(), reads=("ydone",))
    p.emit(nc)
    return nc


def build_l2a(T, QT=512):
    nc = bass.Bass("TRN2", target_bir_lowering=False)
    NCH = T // QT
    H = 16
    DI = 1024
    xd = nc.dram_tensor("x", [D, T], F32, kind="ExternalInput").ap()
    gmd = nc.dram_tensor("gam_pm", [128, PC], F32, kind="ExternalInput").ap()
    wqd = nc.dram_tensor("wqkv", [D, 3 * DI], F32, kind="ExternalInput").ap()
    wod = nc.dram_tensor("wout", [DI, D], F32, kind="ExternalInput").ap()
    tpd = nc.dram_tensor("temp16", [1, H], F32, kind="ExternalInput").ap()
    hmd = nc.dram_tensor("hm16", [H, PC], F32, kind="ExternalInput").ap()
    h64d = nc.dram_tensor("h64", [H, 128], F32, kind="ExternalInput").ap()
    idd = nc.dram_tensor("ident", [128, 128], F32, kind="ExternalInput").ap()
    yd = nc.dram_tensor("y", [D, T], F32, kind="ExternalOutput").ap()

    wq = nc.alloc_sbuf_tensor("wq", [128, PC * 3 * DI], BF16)
    wo = nc.alloc_sbuf_tensor("wo", [128, PC * D], BF16)
    gm = nc.alloc_sbuf_tensor("gm", [128, PC], F32)
    tps = nc.alloc_sbuf_tensor("tps", [1, H], F32)
    hms = nc.alloc_sbuf_tensor("hms", [H, PC], F32)
    h64s = nc.alloc_sbuf_tensor("h64s", [H, 128], F32)
    ids = nc.alloc_sbuf_tensor("ids", [128, 128], F32)
    idb = nc.alloc_sbuf_tensor("idb", [128, 128], BF16)
    xs = nc.alloc_sbuf_tensor("xs", [128, PC * QT], F32)
    xnb = nc.alloc_sbuf_tensor("xnb", [128, PC * QT], BF16)
    qv = nc.alloc_sbuf_tensor("qv", [128, PC * QT], F32)
    kv = nc.alloc_sbuf_tensor("kv", [128, PC * QT], F32)
    sqs = nc.alloc_sbuf_tensor("sqs", [128, PC * QT], F32)
    vsb = nc.alloc_sbuf_tensor("vsb", [128, PC * QT], F32)
    vb16 = nc.alloc_sbuf_tensor("vb16", [128, PC * QT], BF16)
    qtb = nc.alloc_sbuf_tensor("qtb", [128, PC * QT], BF16)
    ktb = nc.alloc_sbuf_tensor("ktb", [128, PC * QT], BF16)
    simacc = nc.alloc_sbuf_tensor("simacc", [128, PC * 128], F32)
    attnb = nc.alloc_sbuf_tensor("attnb", [128, PC * 128], BF16)
    qss = nc.alloc_sbuf_tensor("qss", [128, PC], F32)
    kss = nc.alloc_sbuf_tensor("kss", [128, PC], F32)
    red1 = nc.alloc_sbuf_tensor("red1", [128, PC], F32)
    qsc = nc.alloc_sbuf_tensor("qsc", [128, PC], F32)
    ksc = nc.alloc_sbuf_tensor("ksc", [128, PC], F32)
    tcol = nc.alloc_sbuf_tensor("tcol", [128, PC], F32)
    t16 = nc.alloc_sbuf_tensor("t16", [H, 1], F32)
    r168 = nc.alloc_sbuf_tensor("r168", [H, PC], F32)
    smax = nc.alloc_sbuf_tensor("smax", [128, 1], F32)
    ssum = nc.alloc_sbuf_tensor("ssum", [128, 1], F32)
    nrow = nc.alloc_sbuf_tensor("nrow", [1, QT], F32)
    srow = nc.alloc_sbuf_tensor("srow2", [1, QT], F32)
    nb = nc.alloc_sbuf_tensor("nb", [128, QT], F32)
    pm1 = [nc.alloc_psum_tensor(f"pm1{i}", [128, QT], F32) for i in range(2)]
    pstt = [nc.alloc_psum_tensor(f"pstt{i}", [128, 128], F32) for i in range(2)]
    L = LNT(nc, QT, ext_vsq=sqs, ext_vsq_key="sqs")
    pss = L.psb2[:, 0:QT]

    p = Prog()
    for pc in range(PC):
        p.op("g", lambda e, pc=pc: e.dma_start(
            out=wq[:, pc * 3 * DI:(pc + 1) * 3 * DI],
            in_=wqd[pc * 128:(pc + 1) * 128, :]), writes=("wq",), dma=True)
        p.op("g", lambda e, pc=pc: e.dma_start(
            out=wo[:, pc * D:(pc + 1) * D], in_=wod[pc * 128:(pc + 1) * 128, :]),
            writes=("wo",), dma=True)
    for nm, dst, srcap in (("gm", gm, gmd), ("tps", tps, tpd), ("hms", hms, hmd),
                           ("h64s", h64s, h64d), ("ids", ids, idd)):
        p.op("y", lambda e, dst=dst, srcap=srcap: e.dma_start(out=dst[:], in_=srcap[:]),
             writes=(nm,), dma=True)
    L.init_ops(p)
    p.op("v", lambda e: e.tensor_copy(idb[:], ids[:]), reads=("ids",), writes=("idb",))
    p.op("v", lambda e: e.memset(simacc[:], 0.0), writes=("simacc",))
    p.op("v", lambda e: e.memset(qss[:], 0.0), writes=("qss",))
    p.op("v", lambda e: e.memset(kss[:], 0.0), writes=("kss",))

    def emit_xn(p):
        p.op("s", lambda e: e.activation(sqs[:], xs[:], AF.Square),
             reads=("xs",), writes=("sqs",))

        def stm(e):
            for k in range(PC):
                mm = e.matmul(L.psr[:, 0:QT], L.onesc[:], sqs[:, k * QT:(k + 1) * QT],
                              start=(k == 0), stop=(k == PC - 1))
            return mm
        p.op("t", stm, reads=("sqs", "ln_onesc"), writes=("ln_psr",))
        p.op("v", lambda e: e.tensor_copy(nrow[:], L.psr[0:1, 0:QT]),
             reads=("ln_psr",), writes=("nrow",))
        p.op("s", lambda e: e.activation(srow[:], nrow[:], AF.Sqrt),
             reads=("nrow",), writes=("srow2",))
        p.op("v", lambda e: e.tensor_scalar_max(srow[:], srow[:], 1e-12),
             reads=("srow2",), writes=("srow2",))
        p.op("v", lambda e: e.reciprocal(nrow[:], srow[:]),
             reads=("srow2",), writes=("nrow",))
        p.op("v", lambda e: e.tensor_scalar_mul(nrow[:], nrow[:], 32.0),
             reads=("nrow",), writes=("nrow",))
        p.op("t", lambda e: e.matmul(pss, L.onesr[:], nrow[:],
                                     start=True, stop=True),
             reads=("nrow", "ln_onesr"), writes=("ln_psb2",))
        p.op("v", lambda e: e.tensor_copy(nb[:], pss),
             reads=("ln_psb2",), writes=("nb",))
        p.op("v", lambda e: e.tensor_tensor(
            sqs[:].rearrange("p (pc t) -> p pc t", pc=PC),
            xs[:].rearrange("p (pc t) -> p pc t", pc=PC),
            nb[:].unsqueeze(1).broadcast_to((128, PC, QT)), ALU.mult),
            reads=("xs", "nb"), writes=("sqs",))
        p.op("v", lambda e: e.tensor_tensor(
            xnb[:].rearrange("p (pc t) -> p pc t", pc=PC),
            sqs[:].rearrange("p (pc t) -> p pc t", pc=PC),
            gm[:].unsqueeze(2).broadcast_to((128, PC, QT)), ALU.mult),
            reads=("sqs", "gm"), writes=("xnb",))

    def emit_proj(p, off, out_t, okey):
        for dd in range(PC):
            def qm(e, dd=dd, off=off):
                for k in range(PC):
                    mm = e.matmul(pm1[dd % 2][:],
                                  wq[:, k * 3 * DI + off + dd * 128:
                                     k * 3 * DI + off + (dd + 1) * 128],
                                  xnb[:, k * QT:(k + 1) * QT],
                                  start=(k == 0), stop=(k == PC - 1))
                return mm
            p.op("t", qm, reads=("wq", "xnb"), writes=(f"pm{dd % 2}",))
            p.op("v", lambda e, dd=dd, out_t=out_t, okey=okey: e.tensor_copy(
                out_t[:, dd * QT:(dd + 1) * QT], pm1[dd % 2][:]),
                reads=(f"pm{dd % 2}",), writes=(okey,))

    nTB = QT // 128
    for c in range(NCH):
        p.op("y", lambda e, c=c: e.dma_start(
            out=xs[:], in_=xd[:, c * QT:(c + 1) * QT]
            .rearrange("(pc p) t -> p pc t", p=128)), writes=("xs",), dma=True,
            group="ld")
        emit_xn(p)
        emit_proj(p, 0, qv, "qv")
        emit_proj(p, DI, kv, "kv")
        for src_t, acc, skey, akey in ((qv, qss, "qv", "qss"), (kv, kss, "kv", "kss")):
            p.op("s", lambda e, src_t=src_t, skey=skey: e.activation(
                sqs[:], src_t[:], AF.Square), reads=(skey,), writes=("sqs",))
            for pc in range(PC):
                p.op("v", lambda e, pc=pc: e.tensor_reduce(
                    red1[:, pc:pc + 1], sqs[:, pc * QT:(pc + 1) * QT], AX.X, ALU.add),
                    reads=("sqs",), writes=("red1",))
            p.op("v", lambda e, acc=acc, akey=akey: e.tensor_add(acc[:], acc[:], red1[:]),
                 reads=(akey, "red1"), writes=(akey,))
        for src_t, dstb, skey, dkey in ((qv, qtb, "qv", "qtb"), (kv, ktb, "kv", "ktb")):
            p.op("v", lambda e, src_t=src_t: e.tensor_copy(vb16[:], src_t[:]),
                 reads=(skey,), writes=("vb16",))
            for j in range(PC * nTB):
                p.op("t", lambda e, j=j: e.transpose(
                    pstt[j % 2][:].bitcast(BF16)[:, 0:128],
                    vb16[:, j * 128:(j + 1) * 128], idb[:]),
                    reads=("vb16", "idb"), writes=(f"pstt{j % 2}",))
                p.op("v", lambda e, j=j, dstb=dstb, dkey=dkey: e.tensor_copy(
                    dstb[:, j * 128:(j + 1) * 128],
                    pstt[j % 2][:].bitcast(BF16)[:, 0:128]),
                    reads=(f"pstt{j % 2}",), writes=(dkey,))
        for pc in range(PC):
            def smm(e, pc=pc):
                for tb in range(nTB):
                    j = pc * nTB + tb
                    mm = e.matmul(pstt[0][:],
                                  qtb[:, j * 128:(j + 1) * 128],
                                  ktb[:, j * 128:(j + 1) * 128],
                                  start=(tb == 0), stop=(tb == nTB - 1))
                return mm
            p.op("t", smm, reads=("qtb", "ktb"), writes=("pstt0",))
            p.op("v", lambda e, pc=pc: e.tensor_tensor(
                simacc[:, pc * 128:(pc + 1) * 128], simacc[:, pc * 128:(pc + 1) * 128],
                pstt[0][:], ALU.add), reads=("simacc", "pstt0"), writes=("simacc",))

    p.op("s", lambda e: e.activation(tps[:], tps[:], AF.Exp),
         reads=("tps",), writes=("tps",))
    p.op("t", lambda e: e.transpose(pstt[0][0:H, 0:1], tps[:], ids[0:1, 0:1]),
         reads=("tps", "ids"), writes=("pstt0",))
    p.op("v", lambda e: e.tensor_copy(t16[:], pstt[0][0:H, 0:1]),
         reads=("pstt0",), writes=("t16",))
    for acc, dst, akey, dkey in ((qss, qsc, "qss", "qsc"), (kss, ksc, "kss", "ksc")):
        p.op("s", lambda e, acc=acc, dst=dst, akey=akey: e.activation(
            dst[:], acc[:], AF.Sqrt), reads=(akey,), writes=(dkey,))
        p.op("v", lambda e, dst=dst, dkey=dkey: e.tensor_scalar_max(
            dst[:], dst[:], 1e-12), reads=(dkey,), writes=(dkey,))
        p.op("v", lambda e, dst=dst, dkey=dkey: e.reciprocal(dst[:], dst[:]),
             reads=(dkey,), writes=(dkey,))
    p.op("v", lambda e: e.tensor_tensor(r168[:], hms[:],
                                        t16[:].broadcast_to((H, PC)), ALU.mult),
         reads=("hms", "t16"), writes=("r168",))
    p.op("t", lambda e: e.matmul(pstt[0][:, 0:PC], h64s[:], r168[:],
                                 start=True, stop=True),
         reads=("h64s", "r168"), writes=("pstt0",))
    p.op("v", lambda e: e.tensor_copy(tcol[:], pstt[0][:, 0:PC]),
         reads=("pstt0",), writes=("tcol",))
    p.op("v", lambda e: e.tensor_scalar_mul(tcol[:], tcol[:], 8.0),
         reads=("tcol",), writes=("tcol",))
    p.op("v", lambda e: e.tensor_mul(qsc[:], qsc[:], tcol[:]),
         reads=("qsc", "tcol"), writes=("qsc",))
    for pc in range(PC):
        p.op("v", lambda e, pc=pc: e.tensor_scalar(
            simacc[:, pc * 128:(pc + 1) * 128], simacc[:, pc * 128:(pc + 1) * 128],
            qsc[:, pc:pc + 1], None, ALU.mult),
            reads=("simacc", "qsc"), writes=("simacc",))
        p.op("t", lambda e, pc=pc: e.transpose(
            pstt[0][:], simacc[:, pc * 128:(pc + 1) * 128], ids[:]),
            reads=("simacc", "ids"), writes=("pstt0",))
        p.op("v", lambda e, pc=pc: e.scalar_tensor_tensor(
            vsb[:, 0:128], pstt[0][:], ksc[:, pc:pc + 1], pstt[0][:],
            ALU.mult, ALU.bypass) if False else e.tensor_scalar(
            vsb[:, 0:128], pstt[0][:], ksc[:, pc:pc + 1], None, ALU.mult),
            reads=("pstt0", "ksc"), writes=("vsb",))
        p.op("t", lambda e, pc=pc: e.transpose(pstt[1][:], vsb[:, 0:128], ids[:]),
             reads=("vsb", "ids"), writes=("pstt1",))
        p.op("v", lambda e, pc=pc: e.tensor_copy(
            simacc[:, pc * 128:(pc + 1) * 128], pstt[1][:]),
            reads=("pstt1",), writes=("simacc",))
        for i in range(2):
            sl = simacc[i * 64:(i + 1) * 64, pc * 128 + i * 64:pc * 128 + (i + 1) * 64]
            p.op("v", lambda e, sl=sl, i=i: e.reduce_max(
                smax[i * 64:(i + 1) * 64, 0:1], sl, AX.X),
                reads=("simacc",), writes=("smax",))
            p.op("v", lambda e, sl=sl, i=i: e.tensor_scalar(
                sl, sl, smax[i * 64:(i + 1) * 64, 0:1], None, ALU.subtract),
                reads=("simacc", "smax"), writes=("simacc",))
            p.op("s", lambda e, sl=sl: e.activation(sl, sl, AF.Exp),
                 reads=("simacc",), writes=("simacc",))
            p.op("v", lambda e, sl=sl, i=i: e.reduce_sum(
                ssum[i * 64:(i + 1) * 64, 0:1], sl, AX.X),
                reads=("simacc",), writes=("ssum",))
            p.op("v", lambda e, i=i: e.reciprocal(
                ssum[i * 64:(i + 1) * 64, 0:1], ssum[i * 64:(i + 1) * 64, 0:1]),
                reads=("ssum",), writes=("ssum",))
            p.op("v", lambda e, sl=sl, i=i: e.tensor_scalar(
                sl, sl, ssum[i * 64:(i + 1) * 64, 0:1], None, ALU.mult),
                reads=("simacc", "ssum"), writes=("simacc",))
        p.op("v", lambda e, pc=pc: e.tensor_copy(
            vb16[:, 0:128], simacc[:, pc * 128:(pc + 1) * 128]),
            reads=("simacc",), writes=("vb16",))
        p.op("t", lambda e, pc=pc: e.transpose(
            pstt[0][:].bitcast(BF16)[:, 0:128], vb16[:, 0:128], idb[:]),
            reads=("vb16", "idb"), writes=("pstt0",))
        p.op("v", lambda e, pc=pc: e.tensor_copy(
            attnb[:, pc * 128:(pc + 1) * 128], pstt[0][:].bitcast(BF16)[:, 0:128]),
            reads=("pstt0",), writes=("attnb",))

    for c in range(NCH):
        p.op("y", lambda e, c=c: e.dma_start(
            out=xs[:], in_=xd[:, c * QT:(c + 1) * QT]
            .rearrange("(pc p) t -> p pc t", p=128)), writes=("xs",), dma=True,
            group="ld2")
        emit_xn(p)
        emit_proj(p, 2 * DI, qv, "qv")
        p.op("v", lambda e: e.tensor_copy(vb16[:], qv[:]), reads=("qv",), writes=("vb16",))
        for pc in range(PC):
            for i in range(2):
                def am(e, pc=pc, i=i):
                    return e.matmul(
                        pm1[pc % 2][i * 64:(i + 1) * 64, :],
                        attnb[i * 64:(i + 1) * 64,
                              pc * 128 + i * 64:pc * 128 + (i + 1) * 64],
                        vb16[i * 64:(i + 1) * 64, pc * QT:(pc + 1) * QT],
                        start=True, stop=True)
                p.op("t", am, reads=("attnb", "vb16"), writes=(f"pm{pc % 2}_{i}",))
            p.op("v", lambda e, pc=pc: e.tensor_copy(
                kv[:, pc * QT:(pc + 1) * QT], pm1[pc % 2][:]),
                reads=(f"pm{pc % 2}_0", f"pm{pc % 2}_1"), writes=("kv",))
        p.op("v", lambda e: e.tensor_copy(ktb[:], kv[:]), reads=("kv",), writes=("ktb",))
        for dd in range(PC):
            def om(e, dd=dd):
                for k in range(PC):
                    mm = e.matmul(pm1[dd % 2][:],
                                  wo[:, k * D + dd * 128:k * D + (dd + 1) * 128],
                                  ktb[:, k * QT:(k + 1) * QT],
                                  start=(k == 0), stop=(k == PC - 1))
                return mm
            p.op("t", om, reads=("wo", "ktb"), writes=(f"pm{dd % 2}",))
            p.op("v", lambda e, dd=dd: e.tensor_tensor(
                vsb[:, dd * QT:(dd + 1) * QT], pm1[dd % 2][:],
                xs[:, dd * QT:(dd + 1) * QT], ALU.add),
                reads=(f"pm{dd % 2}", "xs"), writes=("vsb",))
        emit_postln(p, L, vsb, "vsb", xs[:], "xs")
        p.op("s", lambda e, c=c: e.dma_start(
            out=yd[:, c * QT:(c + 1) * QT].rearrange("(pc p) t -> p pc t", p=128),
            in_=xs[:]), reads=("xs",), writes=("ydone",), dma=True, group="st")
    p.op("y", lambda e: e.nop(), reads=("ydone",))
    p.emit(nc)
    return nc



def build_l0a(T):
    QT = 128
    nc = bass.Bass("TRN2", target_bir_lowering=False, dynamic_dma_scratch_size=8192)
    NCH = T // QT
    NH, HD, DS = 32, 64, 128
    DIP = 4384
    DIN = 2048
    CC = 18
    xd = nc.dram_tensor("x", [D, T], F32, kind="ExternalInput").ap()
    wipd = nc.dram_tensor("wipT", [D, DIP], F32, kind="ExternalInput").ap()
    wopd = nc.dram_tensor("wopT", [DIN, D], F32, kind="ExternalInput").ap()
    cwd = nc.dram_tensor("cw_pm", [128, CC * 4], F32, kind="ExternalInput").ap()
    cbd = nc.dram_tensor("cb_pm", [128, CC], F32, kind="ExternalInput").ap()
    dtbd = nc.dram_tensor("dtb_col", [NH, 1], F32, kind="ExternalInput").ap()
    ald = nc.dram_tensor("alog_row", [1, 96], F32, kind="ExternalInput").ap()
    dpd = nc.dram_tensor("d_pm", [128, 16], F32, kind="ExternalInput").ap()
    nwd = nc.dram_tensor("nw_pm", [128, 16], F32, kind="ExternalInput").ap()
    mkd = nc.dram_tensor("maskS", [128, 128], F32, kind="ExternalInput").ap()
    sbd = nc.dram_tensor("selbig", [NH, 16 * 128], F32, kind="ExternalInput").ap()
    sod = nc.dram_tensor("selone", [NH, NH * 128], F32, kind="ExternalInput").ap()
    idd = nc.dram_tensor("ident", [128, 128], F32, kind="ExternalInput").ap()
    yd = nc.dram_tensor("y", [D, T], F32, kind="ExternalOutput").ap()

    wip = nc.alloc_sbuf_tensor("wip", [128, PC * DIP], BF16)
    wop = nc.alloc_sbuf_tensor("wop", [128, 16 * D], BF16)
    cw = nc.alloc_sbuf_tensor("cw", [128, CC * 4], F32)
    cb = nc.alloc_sbuf_tensor("cb", [128, CC], F32)
    dtb = nc.alloc_sbuf_tensor("dtb", [NH, 1], F32)
    alr = nc.alloc_sbuf_tensor("alr", [1, 96], F32)
    acol = nc.alloc_sbuf_tensor("acol", [NH, 1], F32)
    dpm = nc.alloc_sbuf_tensor("dpm", [128, 16], F32)
    nwm = nc.alloc_sbuf_tensor("nwm", [128, 16], F32)
    mks = nc.alloc_sbuf_tensor("mks", [128, 128], F32)
    selb = nc.alloc_sbuf_tensor("selb", [NH, 16 * 128], F32)
    selo = nc.alloc_sbuf_tensor("selo", [NH, NH * 128], F32)
    ids = nc.alloc_sbuf_tensor("ids", [128, 128], F32)
    idb = nc.alloc_sbuf_tensor("idb", [128, 128], BF16)
    onecell = nc.alloc_sbuf_tensor("onecell", [1, 1], F32)
    onec32 = nc.alloc_sbuf_tensor("onec32", [32, 1], F32)
    xs = nc.alloc_sbuf_tensor("xs", [128, PC * QT], F32)
    xb = nc.alloc_sbuf_tensor("xb", [128, PC * QT], BF16)
    zs = nc.alloc_sbuf_tensor("zs", [128, 16 * QT], F32)
    xbce = nc.alloc_sbuf_tensor("xbce", [128, CC * (QT + 3)], F32)
    xbcs = nc.alloc_sbuf_tensor("xbcs", [128, CC * QT], BF16)
    convt = nc.alloc_sbuf_tensor("convt", [128, QT], F32)
    dtraw = nc.alloc_sbuf_tensor("dtraw", [NH, QT], F32)
    dtt = nc.alloc_sbuf_tensor("dtt", [NH, QT], F32)
    dts = nc.alloc_sbuf_tensor("dts", [NH, QT], F32)
    La = nc.alloc_sbuf_tensor("La", [NH, QT], F32)
    eLa = nc.alloc_sbuf_tensor("eLa", [NH, QT], F32)
    eEnd = nc.alloc_sbuf_tensor("eEnd", [NH, QT], F32)
    LaT = nc.alloc_sbuf_tensor("LaT", [128, NH], F32)
    dteT = nc.alloc_sbuf_tensor("dteT", [128, NH], F32)
    dtT = nc.alloc_sbuf_tensor("dtT", [128, NH], F32)
    e32r = nc.alloc_sbuf_tensor("e32r", [1, NH], F32)
    E32 = nc.alloc_sbuf_tensor("E32", [128, NH], F32)
    XTb = nc.alloc_sbuf_tensor("XTb", [128, 16 * QT], BF16)
    BTb = nc.alloc_sbuf_tensor("BTb", [128, QT], BF16)
    CBTm = nc.alloc_sbuf_tensor("CBTm", [128, QT], BF16)
    Esb = nc.alloc_sbuf_tensor("Esb", [128, 16 * QT], F32)
    ysb = nc.alloc_sbuf_tensor("ysb", [128, 16 * QT], F32)
    Wg = nc.alloc_sbuf_tensor("Wg", [128, 16 * QT], BF16)
    hst = nc.alloc_sbuf_tensor("hst", [128, NH * HD], F32)
    hbf = nc.alloc_sbuf_tensor("hbf", [128, NH * HD], BF16)
    xds = nc.alloc_sbuf_tensor("xds", [128, NH * HD], BF16)
    silz = nc.alloc_sbuf_tensor("silz", [128, 16 * QT], F32)
    vsb = nc.alloc_sbuf_tensor("vsb", [128, PC * QT], F32)
    pm1 = [nc.alloc_psum_tensor(f"pml{i}", [128, QT], F32) for i in range(2)]
    psW = nc.alloc_psum_tensor("psW", [128, 16 * QT], F32)
    L = LNT(nc, QT, ext_vsq=silz[:, 0:PC * QT], ext_vsq_key="silz")
    rmsr = L.rv
    rmsb = L.mb

    p = Prog()
    for pc in range(PC):
        p.op("g", lambda e, pc=pc: e.dma_start(
            out=wip[:, pc * DIP:(pc + 1) * DIP], in_=wipd[pc * 128:(pc + 1) * 128, :]),
            writes=("wip",), dma=True)
    for kk in range(16):
        p.op("g", lambda e, kk=kk: e.dma_start(
            out=wop[:, kk * D:(kk + 1) * D], in_=wopd[kk * 128:(kk + 1) * 128, :]),
            writes=("wop",), dma=True)
    for nm, dst, srcap in (("cw", cw, cwd), ("cb", cb, cbd), ("dtb", dtb, dtbd),
                           ("alr", alr, ald), ("dpm", dpm, dpd), ("nwm", nwm, nwd),
                           ("mks", mks, mkd), ("selb", selb, sbd), ("selo", selo, sod),
                           ("ids", ids, idd)):
        p.op("y", lambda e, dst=dst, srcap=srcap: e.dma_start(out=dst[:], in_=srcap[:]),
             writes=(nm,), dma=True)
    L.init_ops(p)
    p.op("v", lambda e: e.memset(onecell[:], 1.0), writes=("onecell",))
    p.op("v", lambda e: e.memset(onec32[:], 1.0), writes=("onec32",))
    p.op("v", lambda e: e.memset(xbce[:], 0.0), writes=("xbce",))
    p.op("v", lambda e: e.memset(hst[:], 0.0), writes=("hst",))
    p.op("v", lambda e: e.tensor_copy(idb[:], ids[:]), reads=("ids",), writes=("idb",))
    p.op("s", lambda e: e.activation(alr[:], alr[:], AF.Exp),
         reads=("alr",), writes=("alr",))
    p.op("v", lambda e: e.tensor_add(alr[0:1, 0:32], alr[0:1, 0:32], alr[0:1, 32:64]),
         reads=("alr",), writes=("alr",))
    p.op("v", lambda e: e.tensor_add(alr[0:1, 0:32], alr[0:1, 0:32], alr[0:1, 64:96]),
         reads=("alr",), writes=("alr",))
    p.op("v", lambda e: e.tensor_scalar_mul(alr[0:1, 0:32], alr[0:1, 0:32], -1.0 / 3.0),
         reads=("alr",), writes=("alr",))
    p.op("t", lambda e: e.transpose(pm1[0][0:32, 0:1], alr[0:1, 0:32], ids[0:1, 0:1]),
         reads=("alr", "ids"), writes=("pm0",))
    p.op("v", lambda e: e.tensor_copy(acol[:], pm1[0][0:32, 0:1]),
         reads=("pm0",), writes=("acol",))
    for kk in range(16):
        p.op("v", lambda e, kk=kk: e.tensor_scalar(
            wop[:, kk * D:(kk + 1) * D], wop[:, kk * D:(kk + 1) * D],
            nwm[:, kk:kk + 1], None, ALU.mult), reads=("wop", "nwm"), writes=("wop",))

    for c in range(NCH):
        p.op("y", lambda e, c=c: e.dma_start(
            out=xs[:], in_=xd[:, c * QT:(c + 1) * QT]
            .rearrange("(pc p) t -> p pc t", p=128)), writes=("xs",), dma=True,
            group="ld")
        p.op("v", lambda e: e.tensor_copy(xb[:], xs[:]), reads=("xs",), writes=("xb",))
        p.op("v", lambda e: e.tensor_copy(
            xbce[:].rearrange("p (cc t) -> p cc t", cc=CC)[:, :, QT:QT + 3],
            xbce[:].rearrange("p (cc t) -> p cc t", cc=CC)[:, :, 0:3]) if False
            else e.tensor_copy(
            xbce[:].rearrange("p (cc t) -> p cc t", cc=CC)[:, :, 0:3],
            xbce[:].rearrange("p (cc t) -> p cc t", cc=CC)[:, :, QT:QT + 3]),
            reads=("xbce",), writes=("xbce",))
        for d in range(35):
            dw = 128 if d < 34 else 32
            def ipm(e, d=d, dw=dw):
                for k in range(PC):
                    mm = e.matmul(pm1[d % 2][0:dw, :],
                                  wip[:, k * DIP + d * 128:k * DIP + d * 128 + dw],
                                  xb[:, k * QT:(k + 1) * QT],
                                  start=(k == 0), stop=(k == PC - 1))
                return mm
            p.op("t", ipm, reads=("wip", "xb"), writes=(f"pm{d % 2}",))
            if d < 16:
                p.op("v", lambda e, d=d: e.tensor_copy(
                    zs[:, d * QT:(d + 1) * QT], pm1[d % 2][:]),
                    reads=(f"pm{d % 2}",), writes=("zs",))
            elif d < 34:
                cc = d - 16
                p.op("v", lambda e, d=d, cc=cc: e.tensor_copy(
                    xbce[:, cc * (QT + 3) + 3:cc * (QT + 3) + 3 + QT], pm1[d % 2][:]),
                    reads=(f"pm{d % 2}",), writes=("xbce",))
            else:
                p.op("v", lambda e, d=d: e.tensor_copy(
                    dtraw[:], pm1[d % 2][0:32, :]), reads=(f"pm{d % 2}",),
                    writes=("dtraw",))
        for cc in range(CC):
            p.op("v", lambda e, cc=cc: e.tensor_scalar(
                convt[:], xbce[:, cc * (QT + 3):cc * (QT + 3) + QT],
                cw[:, cc * 4:cc * 4 + 1], None, ALU.mult),
                reads=("xbce", "cw"), writes=("convt",))
            for kk in range(1, 4):
                p.op("v", lambda e, cc=cc, kk=kk: e.scalar_tensor_tensor(
                    convt[:], xbce[:, cc * (QT + 3) + kk:cc * (QT + 3) + kk + QT],
                    cw[:, cc * 4 + kk:cc * 4 + kk + 1], convt[:], ALU.mult, ALU.add),
                    reads=("xbce", "cw", "convt"), writes=("convt",))
            p.op("s", lambda e, cc=cc: e.activation(
                xbcs[:, cc * QT:(cc + 1) * QT], convt[:], AF.Silu,
                bias=cb[:, cc:cc + 1]), reads=("convt", "cb"), writes=("xbcs",))
        p.op("s", lambda e: e.activation(dtt[:], dtraw[:], AF.Abs, bias=dtb[:]),
             reads=("dtraw", "dtb"), writes=("dtt",))
        p.op("s", lambda e: e.activation(dtt[:], dtt[:], AF.Exp, scale=-1.0),
             reads=("dtt",), writes=("dtt",))
        p.op("s", lambda e: e.activation(dtt[:], dtt[:], AF.Ln, bias=onec32[:]),
             reads=("dtt", "onec32"), writes=("dtt",))
        p.op("s", lambda e: e.activation(dts[:], dtraw[:], AF.Relu, bias=dtb[:]),
             reads=("dtraw", "dtb"), writes=("dts",))
        p.op("v", lambda e: e.tensor_add(dts[:], dts[:], dtt[:]),
             reads=("dts", "dtt"), writes=("dts",))
        p.op("v", lambda e: e.tensor_scalar(dtt[:], dts[:], acol[:], None, ALU.mult),
             reads=("dts", "acol"), writes=("dtt",))
        p.op("v", lambda e: e.tensor_tensor_scan(La[:], dtt[:], dtt[:], 0.0,
                                                 ALU.add, ALU.bypass),
             reads=("dtt",), writes=("La",))
        p.op("s", lambda e: e.activation(eLa[:], La[:], AF.Exp),
             reads=("La",), writes=("eLa",))
        p.op("v", lambda e: e.tensor_scalar(eEnd[:], La[:], La[:, QT - 1:QT], None,
                                            ALU.subtract),
             reads=("La",), writes=("eEnd",))
        p.op("s", lambda e: e.activation(eEnd[:], eEnd[:], AF.Exp, scale=-1.0),
             reads=("eEnd",), writes=("eEnd",))
        p.op("v", lambda e: e.tensor_mul(eEnd[:], eEnd[:], dts[:]),
             reads=("eEnd", "dts"), writes=("eEnd",))
        p.op("t", lambda e: e.transpose(pm1[0][:, 0:NH], La[:], ids[0:32, 0:32]),
             reads=("La", "ids"), writes=("pm0",))
        p.op("v", lambda e: e.tensor_copy(LaT[:], pm1[0][:, 0:NH]),
             reads=("pm0",), writes=("LaT",))
        p.op("t", lambda e: e.transpose(pm1[1][:, 0:NH], eEnd[:], ids[0:32, 0:32]),
             reads=("eEnd", "ids"), writes=("pm1",))
        p.op("v", lambda e: e.tensor_copy(dteT[:], pm1[1][:, 0:NH]),
             reads=("pm1",), writes=("dteT",))
        p.op("t", lambda e: e.transpose(pm1[0][:, 0:NH], dts[:], ids[0:32, 0:32]),
             reads=("dts", "ids"), writes=("pm0",))
        p.op("v", lambda e: e.tensor_copy(dtT[:], pm1[0][:, 0:NH]),
             reads=("pm0",), writes=("dtT",))
        p.op("t", lambda e: e.matmul(pm1[0][:], xbcs[:, 16 * QT:17 * QT],
                                     xbcs[:, 17 * QT:18 * QT], start=True, stop=True),
             reads=("xbcs",), writes=("pm0",))
        p.op("v", lambda e: e.tensor_tensor(CBTm[:], pm1[0][:], mks[:], ALU.mult),
             reads=("pm0", "mks"), writes=("CBTm",))
        for j in range(16):
            p.op("t", lambda e, j=j: e.transpose(
                pm1[j % 2][:].bitcast(BF16)[:, 0:128],
                xbcs[:, j * QT:(j + 1) * QT], idb[:]),
                reads=("xbcs", "idb"), writes=(f"pm{j % 2}",))
            p.op("v", lambda e, j=j: e.tensor_copy(
                XTb[:, j * QT:(j + 1) * QT], pm1[j % 2][:].bitcast(BF16)[:, 0:128]),
                reads=(f"pm{j % 2}",), writes=("XTb",))
        p.op("v", lambda e: e.tensor_copy(hbf[:], hst[:]), reads=("hst",), writes=("hbf",))
        def emm(e):
            for pc2 in range(16):
                mm = e.matmul(psW[:, pc2 * QT:(pc2 + 1) * QT],
                              selb[:, pc2 * 128:(pc2 + 1) * 128], eLa[:],
                              start=True, stop=True)
            return mm
        p.op("t", emm, reads=("selb", "eLa"), writes=("psW",))
        p.op("s", lambda e: e.activation(Esb[:], psW[:], AF.Copy),
             reads=("psW",), writes=("Esb",))
        for gq in range(2):
            def lrm(e, gq=gq):
                for i in range(16):
                    h = gq * 16 + i
                    mm = e.matmul(psW[:, i * QT:(i + 1) * QT],
                                  selo[:, h * 128:(h + 1) * 128], La[:],
                                  start=True, stop=True)
                return mm
            p.op("t", lrm, reads=("selo", "La"), writes=("psW",))
            p.op("v", lambda e, gq=gq: e.tensor_tensor(
                silz[:].rearrange("p (g t) -> p g t", g=16),
                psW[:].rearrange("p (g t) -> p g t", g=16),
                LaT[:, gq * 16:(gq + 1) * 16].unsqueeze(2).broadcast_to((128, 16, QT)),
                ALU.min), reads=("psW", "LaT"), writes=("silz",))
            p.op("v", lambda e, gq=gq: e.tensor_tensor(
                silz[:].rearrange("p (g t) -> p g t", g=16),
                silz[:].rearrange("p (g t) -> p g t", g=16),
                LaT[:, gq * 16:(gq + 1) * 16].unsqueeze(2).broadcast_to((128, 16, QT)),
                ALU.subtract), reads=("silz", "LaT"), writes=("silz",))
            p.op("s", lambda e: e.activation(Wg[:], silz[:], AF.Exp),
                 reads=("silz",), writes=("Wg",))
            p.op("v", lambda e, gq=gq: e.tensor_tensor(
                Wg[:].rearrange("p (g t) -> p g t", g=16),
                Wg[:].rearrange("p (g t) -> p g t", g=16),
                dtT[:, gq * 16:(gq + 1) * 16].unsqueeze(2).broadcast_to((128, 16, QT)),
                ALU.mult), reads=("Wg", "dtT"), writes=("Wg",))
            p.op("v", lambda e: e.tensor_tensor(
                Wg[:].rearrange("p (g t) -> p g t", g=16),
                Wg[:].rearrange("p (g t) -> p g t", g=16),
                CBTm[:].unsqueeze(1).broadcast_to((128, 16, QT)), ALU.mult),
                reads=("Wg", "CBTm"), writes=("Wg",))
            def ymm(e, gq=gq):
                for i in range(16):
                    h = gq * 16 + i
                    sl = psW[(h % 2) * 64:(h % 2) * 64 + 64,
                             (h // 2) * QT:(h // 2) * QT + QT]
                    mm = e.matmul(sl, XTb[:, h * 64:(h + 1) * 64],
                                  Wg[:, i * QT:(i + 1) * QT], start=True, stop=True)
                return mm
            p.op("t", ymm, reads=("XTb", "Wg", "ysbF" if gq == 0 else "ysbG"),
                 writes=("psW",))
            if gq == 0:
                p.op("v", lambda e: e.tensor_copy(
                    ysb[:, 0:8 * QT], psW[:, 0:8 * QT]),
                    reads=("psW",), writes=("ysbG",))
            else:
                p.op("v", lambda e: e.tensor_copy(
                    ysb[:, 8 * QT:16 * QT], psW[:, 8 * QT:16 * QT]),
                    reads=("psW",), writes=("ysbG",))

        def yim(e):
            for h in range(NH):
                sl = psW[(h % 2) * 64:(h % 2) * 64 + 64,
                         (h // 2) * QT:(h // 2) * QT + QT]
                mm = e.matmul(sl, hbf[:, h * 64:(h + 1) * 64],
                              xbcs[:, 17 * QT:18 * QT], start=True, stop=True)
            return mm
        p.op("t", yim, reads=("hbf", "xbcs", "ysbG"), writes=("psW",))
        p.op("v", lambda e: e.tensor_mul(silz[:], psW[:], Esb[:]),
             reads=("psW", "Esb"), writes=("silz",))
        p.op("v", lambda e: e.tensor_add(ysb[:], ysb[:], silz[:]),
             reads=("ysbG", "silz"), writes=("ysbF",))
        p.op("v", lambda e: e.tensor_tensor(
            silz[:].rearrange("p (g t) -> p g t", g=16),
            xbcs[:, 0:16 * QT].rearrange("p (g t) -> p g t", g=16),
            dpm[:].unsqueeze(2).broadcast_to((128, 16, QT)), ALU.mult),
            reads=("xbcs", "dpm"), writes=("silz",))
        p.op("v", lambda e: e.tensor_add(ysb[:], ysb[:], silz[:]),
             reads=("ysbF", "silz"), writes=("ysbF",))
        p.op("t", lambda e: e.transpose(pm1[0][:].bitcast(BF16)[:, 0:128],
                                        xbcs[:, 16 * QT:17 * QT], idb[:]),
             reads=("xbcs", "idb"), writes=("pm0",))
        p.op("v", lambda e: e.tensor_copy(BTb[:], pm1[0][:].bitcast(BF16)[:, 0:128]),
             reads=("pm0",), writes=("BTb",))
        p.op("v", lambda e: e.tensor_tensor(
            xds[:].rearrange("p (h q) -> p h q", h=NH),
            XTb[:].rearrange("p (h q) -> p h q", h=NH),
            dteT[:].unsqueeze(2).broadcast_to((128, NH, HD)), ALU.mult),
            reads=("XTb", "dteT"), writes=("xds",))

        def dhm(e):
            for j in range(4):
                mm = e.matmul(psW[:, j * 512:(j + 1) * 512], BTb[:],
                              xds[:, j * 512:(j + 1) * 512], start=True, stop=True)
            return mm
        p.op("t", dhm, reads=("BTb", "xds", "ysbF"), writes=("psW",))
        p.op("t", lambda e: e.transpose(pm1[0][0:1, 0:NH], eLa[:, QT - 1:QT],
                                        ids[0:32, 0:32]),
             reads=("eLa", "ids"), writes=("pm0",))
        p.op("v", lambda e: e.tensor_copy(e32r[:], pm1[0][0:1, 0:NH]),
             reads=("pm0",), writes=("e32r",))
        p.op("t", lambda e: e.matmul(pm1[1][:, 0:NH], L.onesr[:], e32r[:],
                                     start=True, stop=True),
             reads=("e32r", "ln_onesr"), writes=("pm1",))
        p.op("v", lambda e: e.tensor_copy(E32[:], pm1[1][:, 0:NH]),
             reads=("pm1",), writes=("E32",))
        p.op("v", lambda e: e.tensor_tensor(
            hst[:].rearrange("n (h q) -> n h q", h=NH),
            hst[:].rearrange("n (h q) -> n h q", h=NH),
            E32[:].unsqueeze(2).broadcast_to((128, NH, HD)), ALU.mult),
            reads=("hst", "E32", "hbf"), writes=("hst",))
        p.op("v", lambda e: e.tensor_add(hst[:], hst[:], psW[:]),
             reads=("hst", "psW"), writes=("hst",))
        p.op("s", lambda e: e.activation(silz[:], zs[:], AF.Silu),
             reads=("zs",), writes=("silz",))
        p.op("s", lambda e: e.activation(Esb[:], ysb[:], AF.Square),
             reads=("ysbF",), writes=("Esb",))

        def rmm(e):
            for k in range(16):
                mm = e.matmul(L.psr[:, 0:QT], L.onesc[:], Esb[:, k * QT:(k + 1) * QT],
                              start=(k == 0), stop=(k == 15))
            return mm
        p.op("t", rmm, reads=("Esb", "ln_onesc"), writes=("ln_psr",))
        p.op("v", lambda e: e.tensor_copy(rmsr[:], L.psr[0:1, 0:QT]),
             reads=("ln_psr",), writes=("ln_rv",))
        p.op("v", lambda e: e.tensor_scalar_mul(rmsr[:], rmsr[:], 1.0 / DIN),
             reads=("ln_rv",), writes=("ln_rv",))
        p.op("s", lambda e: e.activation(rmsr[:], rmsr[:], AF.Sqrt,
                                         bias=L.epsb[0:1, 0:1]),
             reads=("ln_rv", "ln_epsb"), writes=("ln_rv",))
        p.op("v", lambda e: e.reciprocal(rmsr[:], rmsr[:]),
             reads=("ln_rv",), writes=("ln_rv",))
        p.op("t", lambda e: e.matmul(L.psb2[:, 0:QT], L.onesr[:], rmsr[:],
                                     start=True, stop=True),
             reads=("ln_rv", "ln_onesr"), writes=("ln_psb2",))
        p.op("v", lambda e: e.tensor_copy(rmsb[:], L.psb2[:, 0:QT]),
             reads=("ln_psb2",), writes=("ln_mb",))
        p.op("v", lambda e: e.tensor_tensor(
            ysb[:].rearrange("p (g t) -> p g t", g=16),
            ysb[:].rearrange("p (g t) -> p g t", g=16),
            rmsb[:].unsqueeze(1).broadcast_to((128, 16, QT)), ALU.mult),
            reads=("ysbF", "ln_mb"), writes=("ysbF",))
        p.op("v", lambda e: e.tensor_mul(xds[:, 0:16 * QT], ysb[:], silz[:]),
             reads=("ysbF", "silz"), writes=("xds",))
        for dd in range(PC):
            def opm(e, dd=dd):
                for k in range(16):
                    mm = e.matmul(pm1[dd % 2][:],
                                  wop[:, k * D + dd * 128:k * D + (dd + 1) * 128],
                                  xds[:, k * QT:(k + 1) * QT],
                                  start=(k == 0), stop=(k == 15))
                return mm
            p.op("t", opm, reads=("wop", "xds"), writes=(f"pm{dd % 2}",))
            p.op("v", lambda e, dd=dd: e.tensor_tensor(
                vsb[:, dd * QT:(dd + 1) * QT], pm1[dd % 2][:],
                xs[:, dd * QT:(dd + 1) * QT], ALU.add),
                reads=(f"pm{dd % 2}", "xs"), writes=("vsb",))
        emit_postln(p, L, vsb, "vsb", xs[:], "xs")
        p.op("s", lambda e, c=c: e.dma_start(
            out=yd[:, c * QT:(c + 1) * QT].rearrange("(pc p) t -> p pc t", p=128),
            in_=xs[:]), reads=("xs",), writes=("ydone",), dma=True, group="st")
    p.op("y", lambda e: e.nop(), reads=("ydone",))
    p.emit(nc)
    return nc


def prep_l23f(x_fm, w_gate, w_up, w_down):
    return {"x": x_fm,
            "wgT": np.ascontiguousarray(w_gate.T),
            "wuT": np.ascontiguousarray(w_up.T),
            "wdT": np.ascontiguousarray(w_down.T)}


def _consts():
    H = 16
    NH = 32
    halves = np.zeros((128, 2), np.float32)
    halves[:64, 0] = 1
    halves[64:, 1] = 1
    hm16 = np.zeros((H, 8), np.float32)
    for h in range(H):
        hm16[h, h // 2] = 1
    h64 = np.zeros((H, 128), np.float32)
    for h in range(H):
        h64[h, (h % 2) * 64:(h % 2 + 1) * 64] = 1
    selb = np.zeros((NH, 16 * 128), np.float32)
    for pc in range(16):
        for pp in range(128):
            selb[2 * pc + pp // 64, pc * 128 + pp] = 1
    selo = np.zeros((NH, NH * 128), np.float32)
    for h in range(NH):
        selo[h, h * 128:(h + 1) * 128] = 1
    mask = np.triu(np.ones((128, 128), np.float32))
    ident = np.eye(128, dtype=np.float32)
    return halves, hm16, h64, selb, selo, mask, ident


def kernel(**g):
    T = 8192
    NH = 32
    halves, hm16, h64, selb, selo, mask, ident = _consts()
    u = np.asarray(g["L0_attn_u"], dtype=np.float32)
    xcur = [fm(u[b]) for b in range(4)]
    zeros = np.zeros((D, T), np.float32)

    def launch(nc, base):
        in_maps = []
        for c in range(8):
            m = dict(base)
            m["x"] = xcur[c] if c < 4 else zeros
            in_maps.append(m)
        res = run_bass_kernel_spmd(nc, in_maps, core_ids=list(range(8)))
        for b in range(4):
            xcur[b] = np.asarray(res.results[b]["y"], dtype=np.float32)

    cw = np.asarray(g["L0_attn_conv_w"], np.float32).reshape(2304, 4) \
        .reshape(18, 128, 4).transpose(1, 0, 2)
    launch(build_l0a(T), {
        "wipT": np.ascontiguousarray(np.asarray(g["L0_attn_in_proj_w"], np.float32).T),
        "wopT": np.ascontiguousarray(np.asarray(g["L0_attn_out_proj_w"], np.float32).T),
        "cw_pm": np.ascontiguousarray(cw.reshape(128, 72)),
        "cb_pm": np.ascontiguousarray(np.asarray(g["L0_attn_conv_b"], np.float32)
                                      .reshape(18, 128).T),
        "dtb_col": np.asarray(g["L0_attn_dt_bias"], np.float32).reshape(NH, 1),
        "alog_row": np.asarray(g["L0_attn_A_log"], np.float32).reshape(1, 96),
        "d_pm": np.ascontiguousarray(
            np.repeat(np.asarray(g["L0_attn_D"], np.float32), 64).reshape(16, 128).T),
        "nw_pm": np.ascontiguousarray(
            np.asarray(g["L0_attn_norm_w"], np.float32).reshape(16, 128).T),
        "maskS": mask, "selbig": selb, "selone": selo, "ident": ident})
    launch(build_l0f(T), {
        "w": np.asarray(g["L0_ffn_weight"], np.float32),
        "bias_pm": pm(np.asarray(g["L0_ffn_bias"], np.float32)),
        "ident": ident})
    launch(build_l1a(T), {
        "q_pm": pm(np.asarray(g["L1_attn_q"], np.float32)),
        "k_pm": pm(np.asarray(g["L1_attn_k"], np.float32)),
        "nw_pm": pm(np.asarray(g["L1_attn_norm_weight"], np.float32)),
        "ld_row": np.asarray(g["L1_attn_log_decay"], np.float32)[None, :],
        "owT": np.ascontiguousarray(np.asarray(g["L1_attn_o_w"], np.float32).T),
        "halves": halves, "hm16": hm16, "h64": h64, "ident": ident})
    launch(build_l1f(T), {
        "nw1_pm": pm(np.asarray(g["L1_ffn_norm1_w"], np.float32)),
        "nw2_pm": np.ascontiguousarray(
            np.asarray(g["L1_ffn_norm2_w"], np.float32).reshape(32, 128).T),
        "w1T": np.ascontiguousarray(np.asarray(g["L1_ffn_w1"], np.float32).T),
        "w2T": np.ascontiguousarray(np.asarray(g["L1_ffn_w2"], np.float32).T),
        "acts": np.array([[np.asarray(g["L1_ffn_act_scale"]).ravel()[0],
                           np.asarray(g["L1_ffn_act_bias"]).ravel()[0]]], np.float32),
        "ident": ident})
    launch(build_l2a(T), {
        "gam_pm": pm(np.asarray(g["L2_attn_gamma"], np.float32)),
        "wqkv": np.asarray(g["L2_attn_w_qkv"], np.float32),
        "wout": np.asarray(g["L2_attn_w_out"], np.float32),
        "temp16": np.asarray(g["L2_attn_temperature"], np.float32).reshape(1, 16),
        "hm16": hm16, "h64": h64, "ident": ident})
    nc23 = build_l23f(T)
    launch(nc23, {
        "wgT": np.ascontiguousarray(np.asarray(g["L2_ffn_w_gate"], np.int32).T),
        "wuT": np.ascontiguousarray(np.asarray(g["L2_ffn_w_up"], np.int32).T),
        "wdT": np.ascontiguousarray(np.asarray(g["L2_ffn_w_down"], np.int32).T)})
    launch(build_l3a(T), {
        "w_in": np.asarray(g["L3_attn_w_in"], np.float32),
        "w_a": np.asarray(g["L3_attn_w_a"], np.float32),
        "ap_pm": pm(np.asarray(g["L3_attn_a_param"], np.float32))})
    launch(nc23, {
        "wgT": np.ascontiguousarray(np.asarray(g["L3_ffn_w_gate"], np.int32).T),
        "wuT": np.ascontiguousarray(np.asarray(g["L3_ffn_w_up"], np.int32).T),
        "wdT": np.ascontiguousarray(np.asarray(g["L3_ffn_w_down"], np.int32).T)})
    out = np.stack([np.ascontiguousarray(xcur[b].T) for b in range(4)])
    return np.ascontiguousarray(out.astype(np.float32))


# version 05
# speedup vs baseline: 1.0049x; 1.0049x over previous
import contextlib
import numpy as np
import concourse.bass as bass
import concourse.mybir as mybir
from concourse.bass_utils import run_bass_kernel_spmd

F32 = mybir.dt.float32
BF16 = mybir.dt.bfloat16
I32 = mybir.dt.int32
AF = mybir.ActivationFunctionType
ALU = mybir.AluOpType
AX = mybir.AxisListType

D = 1024
PC = D // 128
LNEPS = 1e-5
HID = 3072
H2 = 4096
DIP = 4384
DIN = 2048
CC = 18
NH = 32
H = 16
DI = 1024
HD = 64
DS = 128


def pm(v):
    v = np.asarray(v)
    return np.ascontiguousarray(v.reshape(-1, 128).T)


def fm(x):
    return np.ascontiguousarray(np.asarray(x).T)


class Prog:

    ENGS = ("t", "v", "s", "g", "y", "sd", "gd", "yd")

    def __init__(self):
        self.ops = []
        self.last_w = {}
        self.readers = {}
        self.fence = None
        self.bar = {}

    def op(self, eng, fn, reads=(), writes=(), dma=False, group=None):
        if dma:
            eng = {"y": "yd", "g": "gd", "s": "sd"}[eng]
            if group is None:
                group = "dg_" + (writes[0] if writes else "anon")
        idx = len(self.ops)
        if self.fence is not None:
            reads = tuple(reads) + (self.fence,)
        deps = set(self.bar.values())
        for b in reads:
            w = self.last_w.get(b)
            if w is not None:
                deps.add(w)
        for b in writes:
            w = self.last_w.get(b)
            if w is not None:
                deps.add(w)
            deps.update(self.readers.get(b, ()))
        deps.discard(idx)
        self.ops.append(dict(eng=eng, fn=fn, reads=reads, writes=writes,
                             deps=deps, group=group if dma else None))
        for b in reads:
            self.readers.setdefault(b, []).append(idx)
        for b in writes:
            self.last_w[b] = idx
            self.readers[b] = []
        return idx

    def barrier_all(self):
        last = {}
        for i, o in enumerate(self.ops):
            last[o["eng"]] = i
        self.bar = last

    def emit(self, nc):
        import contextlib
        DMAQ = ("sd", "gd", "yd")
        needed = set()
        for o in self.ops:
            needed.update(o["deps"])
        ord_ = {}
        cnt = {e: 0 for e in self.ENGS if e not in DMAQ}
        dma_sem_ids = {}
        n_dma = 0
        groups = {}
        gmembers = {}
        for i, o in enumerate(self.ops):
            if o.get("group") is not None and o["eng"] in DMAQ:
                gmembers.setdefault(o["group"], []).append(i)
        for i, o in enumerate(self.ops):
            gname = o.get("group")
            if o["eng"] in DMAQ and gname is not None:
                if any(m in needed for m in gmembers[gname]):
                    if gname not in groups:
                        groups[gname] = n_dma
                        n_dma += 1
                    dma_sem_ids[i] = groups[gname]
                continue
            if i in needed:
                if o["eng"] in DMAQ:
                    dma_sem_ids[i] = n_dma
                    n_dma += 1
                else:
                    cnt[o["eng"]] += 1
                    ord_[i] = cnt[o["eng"]]
        import bisect
        per_eng = {e: [] for e in self.ENGS}
        for i, o in enumerate(self.ops):
            per_eng[o["eng"]].append(i)
        with contextlib.ExitStack() as es:
            sems = {e: es.enter_context(nc.semaphore("sem_" + e))
                    for e in self.ENGS if e not in DMAQ}
            dsems = [es.enter_context(nc.semaphore(f"dma{j}"))
                     for j in range(n_dma)]
            block = es.enter_context(nc.Block())
            emap = dict(t="tensor", v="vector", s="scalar", g="gpsimd",
                        y="sync", sd="scalar", gd="gpsimd", yd="sync")
            phys = {}
            for e in self.ENGS:
                phys.setdefault(emap[e], []).append(e)

            def mk(pname, eng_keys):
                def body(eng):
                    watermark = {}
                    seen_insts = {}
                    mine = sorted([i for e in eng_keys for i in per_eng[e]])
                    for i in mine:
                        o = self.ops[i]
                        targets = {}
                        for d in o["deps"]:
                            od = self.ops[d]
                            if od["eng"] in DMAQ:
                                gname = od.get("group")
                                if gname is not None:
                                    nmem = bisect.bisect_left(gmembers[gname], i)
                                    k = ("d", dma_sem_ids[d])
                                    targets[k] = max(targets.get(k, 0), 16 * nmem)
                                else:
                                    k = ("d", dma_sem_ids[d])
                                    targets[k] = max(targets.get(k, 0), 16)
                            else:
                                k = ("e", od["eng"])
                                targets[k] = max(targets.get(k, 0), ord_[d])
                        for k, val in sorted(targets.items()):
                            if watermark.get(k, 0) >= val:
                                continue
                            sem = dsems[k[1]] if k[0] == "d" else sems[k[1]]
                            eng.wait_ge(sem, val)
                            watermark[k] = val
                        inst = o["fn"](eng)
                        ik = id(inst)
                        assert not (ik in seen_insts and seen_insts[ik] is inst), \
                            f"op {i} returned an instruction already inc'd"
                        if i in dma_sem_ids:
                            inst.then_inc(dsems[dma_sem_ids[i]], 16)
                            seen_insts[ik] = inst
                        elif i in needed:
                            inst.then_inc(sems[o["eng"]], 1)
                            seen_insts[ik] = inst
                    return
                return body

            for pname, eng_keys in phys.items():
                getattr(block, pname)(mk(pname, eng_keys))
        return nc


class Emit:

    def __init__(self, nc, es, names):
        self.nc = nc
        self.sems = {}
        self.cnt = {}
        for n in names:
            self.sems[n] = es.enter_context(nc.semaphore(n))
            self.cnt[n] = 0

    def inc(self, inst, name, n=1):
        inst.then_inc(self.sems[name], n)
        self.cnt[name] += n

    def w(self, eng, name, val):
        eng.wait_ge(self.sems[name], val)

    def sw(self, eng, name):
        eng.wait_ge(self.sems[name], self.cnt[name])


def emit_rneq(p, src, skey, ii, ikey, ff, fkey, gg, gkey, out, okey, lo=-128.0, hi=127.0):
    p.op("v", lambda e: e.tensor_copy(ii, src), reads=(skey,), writes=(ikey,))
    p.op("v", lambda e: e.tensor_copy(ff, ii), reads=(ikey,), writes=(fkey,))
    p.op("v", lambda e: e.tensor_sub(gg, src, ff), reads=(skey, fkey), writes=(gkey,))
    p.op("v", lambda e: e.tensor_single_scalar(src, gg, 0.5, ALU.is_ge),
         reads=(gkey,), writes=(skey,))
    p.op("v", lambda e: e.tensor_add(ff, ff, src), reads=(fkey, skey), writes=(fkey,))
    p.op("v", lambda e: e.tensor_single_scalar(gg, gg, -0.5, ALU.is_le),
         reads=(gkey,), writes=(gkey,))
    p.op("v", lambda e: e.tensor_sub(ff, ff, gg), reads=(fkey, gkey), writes=(fkey,))
    p.op("v", lambda e: e.tensor_scalar(out, ff, lo, hi, ALU.max, ALU.min),
         reads=(fkey,), writes=(okey,))


class LNT:
    def __init__(self, nc, QT, tag="", ext_vsq=None, ext_vsq_key=None):
        self.QT = QT
        a = nc.alloc_sbuf_tensor
        self.onesr = a(f"ln_onesr{tag}", [1, 128], F32)
        self.onesc = a(f"ln_onesc{tag}", [128, 2], F32)
        self.epsb = a(f"ln_epsb{tag}", [1, 1], F32)
        if ext_vsq is not None:
            self.vsq = ext_vsq
            self.vsq_key = ext_vsq_key
        else:
            self.vsq = a(f"ln_vsq{tag}", [128, PC * QT], F32)
            self.vsq_key = "ln_vsq"
        self.rv = a(f"ln_rv{tag}", [1, QT], F32)
        self.rq = a(f"ln_rq{tag}", [1, QT], F32)
        self.mrow = a(f"ln_mrow{tag}", [1, QT], F32)
        self.varrow = a(f"ln_var{tag}", [1, QT], F32)
        self.srow = a(f"ln_srow{tag}", [1, QT], F32)
        self.rrow = a(f"ln_rrow{tag}", [1, QT], F32)
        self.mb = a(f"ln_mb{tag}", [128, QT], F32)
        self.rb = a(f"ln_rb{tag}", [128, QT], F32)
        self.psr = nc.alloc_psum_tensor(f"ln_psr{tag}", [2, 2 * QT], F32)
        self.psb2 = nc.alloc_psum_tensor(f"ln_psb2{tag}", [128, 2 * QT], F32)

    def init_ops(self, p):
        p.op("v", lambda e: e.memset(self.onesr[:], 1.0), writes=("ln_onesr",))
        p.op("v", lambda e: e.memset(self.onesc[:], 1.0), writes=("ln_onesc",))
        p.op("v", lambda e: e.memset(self.epsb[:], LNEPS), writes=("ln_epsb",))


def emit_postln(p, L, vsb, vkey, out_ap, okey, eps_val=None):
    QT = L.QT
    VK = L.vsq_key
    p.op("s", lambda e: e.activation(L.vsq[:], vsb[:], AF.Square),
         reads=(vkey,), writes=(VK,))

    def statmm(e):
        for k in range(PC):
            e.matmul(L.psr[:, 0:QT], L.onesc[:], vsb[:, k * QT:(k + 1) * QT],
                     start=(k == 0), stop=(k == PC - 1))
        for k in range(PC):
            mm = e.matmul(L.psr[:, QT:2 * QT], L.onesc[:],
                          L.vsq[:, k * QT:(k + 1) * QT],
                          start=(k == 0), stop=(k == PC - 1))
        return mm
    p.op("t", statmm, reads=(vkey, VK, "ln_onesc"), writes=("ln_psr",))
    p.op("v", lambda e: e.tensor_copy(L.rv[:], L.psr[0:1, 0:QT]),
         reads=("ln_psr",), writes=("ln_rv",))
    p.op("v", lambda e: e.tensor_copy(L.rq[:], L.psr[0:1, QT:2 * QT]),
         reads=("ln_psr",), writes=("ln_rq",))
    p.op("v", lambda e: e.tensor_scalar_mul(L.mrow[:], L.rv[:], 1.0 / D),
         reads=("ln_rv",), writes=("ln_mrow",))
    p.op("v", lambda e: e.tensor_mul(L.rv[:], L.mrow[:], L.mrow[:]),
         reads=("ln_mrow",), writes=("ln_rv",))
    p.op("v", lambda e: e.tensor_scalar_mul(L.rq[:], L.rq[:], 1.0 / D),
         reads=("ln_rq",), writes=("ln_rq",))
    p.op("v", lambda e: e.tensor_sub(L.varrow[:], L.rq[:], L.rv[:]),
         reads=("ln_rq", "ln_rv"), writes=("ln_var",))
    p.op("s", lambda e: e.activation(L.srow[:], L.varrow[:], AF.Sqrt,
                                     bias=L.epsb[0:1, 0:1]),
         reads=("ln_var", "ln_epsb"), writes=("ln_srow",))
    p.op("v", lambda e: e.reciprocal(L.rrow[:], L.srow[:]),
         reads=("ln_srow",), writes=("ln_rrow",))

    def bcast(e):
        e.matmul(L.psb2[:, 0:QT], L.onesr[:], L.mrow[:], start=True, stop=True)
        return e.matmul(L.psb2[:, QT:2 * QT], L.onesr[:], L.rrow[:],
                        start=True, stop=True)
    p.op("t", bcast, reads=("ln_mrow", "ln_rrow", "ln_onesr"), writes=("ln_psb2",))
    p.op("v", lambda e: e.tensor_copy(L.mb[:], L.psb2[:, 0:QT]),
         reads=("ln_psb2",), writes=("ln_mb",))
    p.op("v", lambda e: e.tensor_copy(L.rb[:], L.psb2[:, QT:2 * QT]),
         reads=("ln_psb2",), writes=("ln_rb",))
    p.op("v", lambda e: e.tensor_tensor(
        L.vsq[:].rearrange("p (pc t) -> p pc t", pc=PC),
        vsb[:].rearrange("p (pc t) -> p pc t", pc=PC),
        L.mb[:].unsqueeze(1).broadcast_to((128, PC, QT)), ALU.subtract),
        reads=(vkey, "ln_mb"), writes=(VK,))
    p.op("v", lambda e: e.tensor_tensor(
        out_ap.rearrange("p (pc t) -> p pc t", pc=PC),
        L.vsq[:].rearrange("p (pc t) -> p pc t", pc=PC),
        L.rb[:].unsqueeze(1).broadcast_to((128, PC, QT)), ALU.mult),
        reads=(VK, "ln_rb"), writes=(okey,))


def emit_l23f(nc, p, io, T, QT=128, tag=""):
    HID = 3072
    HC = HID // 128
    NCH = T // QT
    xd = io["x"]
    wgd = io["wgT"]
    wud = io["wuT"]
    wdd = io["wdT"]
    yd = io["y"]

    wg = nc.alloc_sbuf_tensor(tag + "wg", [128, PC * HID], BF16)
    wu = nc.alloc_sbuf_tensor(tag + "wu", [128, PC * HID], BF16)
    wdb = nc.alloc_sbuf_tensor(tag + "wdb", [128, HC * D], BF16)
    xs = nc.alloc_sbuf_tensor(tag + "xs", [128, PC * QT], F32)
    xb = nc.alloc_sbuf_tensor(tag + "xb", [128, PC * QT], BF16)
    gb = nc.alloc_sbuf_tensor(tag + "gb", [128, 2 * QT], BF16)
    hb = nc.alloc_sbuf_tensor(tag + "hb", [128, HC * QT], BF16)
    vsb = nc.alloc_sbuf_tensor(tag + "vsb", [128, PC * QT], F32)
    psg = [nc.alloc_psum_tensor(tag + f"psg{i}", [128, QT], F32) for i in range(4)]
    L = LNT(nc, QT, tag=tag)


    p = p
    p.op("g", lambda e: e.dma_start(
        out=wg[:].rearrange("p (pc h) -> p pc h", pc=PC),
        in_=wgd.rearrange("(pc p) h -> p pc h", p=128)), writes=("wg",), dma=True)
    p.op("g", lambda e: e.dma_start(
        out=wu[:].rearrange("p (pc h) -> p pc h", pc=PC),
        in_=wud.rearrange("(pc p) h -> p pc h", p=128)), writes=("wu",), dma=True)
    for hf in range(2):
        p.op("g", lambda e, hf=hf: e.dma_start(
            out=wdb[:, hf * 12 * D:(hf + 1) * 12 * D]
            .rearrange("p (hc d) -> p hc d", hc=12),
            in_=wdd[hf * 12 * 128:(hf + 1) * 12 * 128, :]
            .rearrange("(hc p) d -> p hc d", p=128)), writes=("wdb",), dma=True)
    L.init_ops(p)

    for c in range(NCH):
        p.op("y", lambda e, c=c: e.dma_start(
            out=xs[:], in_=xd[:, c * QT:(c + 1) * QT]
            .rearrange("(pc p) t -> p pc t", p=128)), writes=("xs",), dma=True, group="ld")
        p.op("v", lambda e: e.tensor_copy(xb[:], xs[:]), reads=("xs",), writes=("xb",))
        for d in range(HC):
            def gmm(e, d=d):
                for k in range(PC):
                    mm = e.matmul(psg[d % 2][:],
                                  wg[:, k * HID + d * 128:k * HID + (d + 1) * 128],
                                  xb[:, k * QT:(k + 1) * QT],
                                  start=(k == 0), stop=(k == PC - 1))
                return mm
            p.op("t", gmm, reads=("wg", "xb"), writes=(f"psA{d % 2}",))
            p.op("s", lambda e, d=d: e.activation(
                gb[:, (d % 2) * QT:(d % 2 + 1) * QT], psg[d % 2][:], AF.Silu,
                scale=1.0 / 127.0), reads=(f"psA{d % 2}",), writes=(f"gb{d % 2}",))

            def umm(e, d=d):
                for k in range(PC):
                    mm = e.matmul(psg[2 + d % 2][:],
                                  wu[:, k * HID + d * 128:k * HID + (d + 1) * 128],
                                  xb[:, k * QT:(k + 1) * QT],
                                  start=(k == 0), stop=(k == PC - 1))
                return mm
            p.op("t", umm, reads=("wu", "xb"), writes=(f"psB{d % 2}",))
            p.op("v", lambda e, d=d: e.tensor_tensor(
                hb[:, d * QT:(d + 1) * QT], psg[2 + d % 2][:],
                gb[:, (d % 2) * QT:(d % 2 + 1) * QT], ALU.mult),
                reads=(f"psB{d % 2}", f"gb{d % 2}"), writes=(f"hb{d}",))
        hkeys = tuple(f"hb{d}" for d in range(HC))
        for dd in range(PC):
            def dmm(e, dd=dd):
                for k in range(HC):
                    mm = e.matmul(psg[dd % 4][:],
                                  wdb[:, k * D + dd * 128:k * D + (dd + 1) * 128],
                                  hb[:, k * QT:(k + 1) * QT],
                                  start=(k == 0), stop=(k == HC - 1))
                return mm
            p.op("t", dmm, reads=("wdb",) + hkeys,
                 writes=(f"ps{'AB'[dd % 4 // 2]}{dd % 2}",))
            p.op("v", lambda e, dd=dd: e.scalar_tensor_tensor(
                vsb[:, dd * QT:(dd + 1) * QT], psg[dd % 4][:], 1.0 / 16129.0,
                xs[:, dd * QT:(dd + 1) * QT], ALU.mult, ALU.add),
                reads=(f"ps{'AB'[dd % 4 // 2]}{dd % 2}", "xs"), writes=("vsb",))
        emit_postln(p, L, vsb, "vsb", xs[:], "xs")
        p.op("s", lambda e, c=c: e.dma_start(
            out=yd[:, c * QT:(c + 1) * QT].rearrange("(pc p) t -> p pc t", p=128),
            in_=xs[:]), reads=("xs",), writes=("ydone",), dma=True, group="st")



def build_l23f(T, QT=128):
    nc = bass.Bass("TRN2", target_bir_lowering=False)
    io = {}
    io["x"] = nc.dram_tensor("x", [D, T], F32, kind="ExternalInput").ap()
    io["wgT"] = nc.dram_tensor("wgT", [D, HID], I32, kind="ExternalInput").ap()
    io["wuT"] = nc.dram_tensor("wuT", [D, HID], I32, kind="ExternalInput").ap()
    io["wdT"] = nc.dram_tensor("wdT", [HID, D], I32, kind="ExternalInput").ap()
    io["y"] = nc.dram_tensor("y", [D, T], F32, kind="ExternalOutput").ap()
    p = Prog()
    emit_l23f(nc, p, io, T, QT)
    p.op("y", lambda e: e.nop(), reads=("ydone",))
    p.emit(nc)
    return nc
def emit_l3a(nc, p, io, T, QT=512, tag=""):
    NCH = T // QT
    xd = io["x"]
    wid = io["w_in"]
    wad = io["w_a"]
    apd = io["ap_pm"]
    yd = io["y"]

    wib = nc.alloc_sbuf_tensor(tag + "wib", [128, PC * 128], BF16)
    wab = nc.alloc_sbuf_tensor(tag + "wab", [128, PC * 128], BF16)
    aps = nc.alloc_sbuf_tensor(tag + "aps", [128, PC], F32)
    sp8 = nc.alloc_sbuf_tensor(tag + "sp8", [128, PC], F32)
    spt = nc.alloc_sbuf_tensor(tag + "spt", [128, PC], F32)
    onec = nc.alloc_sbuf_tensor(tag + "onec", [128, 1], F32)
    carry = nc.alloc_sbuf_tensor(tag + "carry3", [128, PC], F32)
    xs = nc.alloc_sbuf_tensor(tag + "xs", [128, PC * QT], F32)
    xb = nc.alloc_sbuf_tensor(tag + "xb", [128, PC * QT], BF16)
    gx = nc.alloc_sbuf_tensor(tag + "gx", [128, PC * QT], F32)
    ga = nc.alloc_sbuf_tensor(tag + "ga", [128, PC * QT], F32)
    av = nc.alloc_sbuf_tensor(tag + "av", [128, PC * QT], F32)
    bv = nc.alloc_sbuf_tensor(tag + "bv", [128, PC * QT], F32)
    sc = nc.alloc_sbuf_tensor(tag + "sc", [128, PC * QT], F32)
    vsb = nc.alloc_sbuf_tensor(tag + "vsb", [128, PC * QT], F32)
    psm = [nc.alloc_psum_tensor(tag + f"psm{i}", [128, QT], F32) for i in range(4)]
    L = LNT(nc, QT, tag=tag)


    p = p
    for h in range(PC):
        p.op("g", lambda e, h=h: e.dma_start(
            out=wib[:, h * 128:(h + 1) * 128], in_=wid[h]), writes=("wib",), dma=True)
        p.op("g", lambda e, h=h: e.dma_start(
            out=wab[:, h * 128:(h + 1) * 128], in_=wad[h]), writes=("wab",), dma=True)
    p.op("y", lambda e: e.dma_start(out=aps[:], in_=apd[:]), writes=("aps",), dma=True)
    L.init_ops(p)
    p.op("v", lambda e: e.memset(onec[:], 1.0), writes=("onec",))
    p.op("v", lambda e: e.memset(carry[:], 0.0), writes=("carry",))
    p.op("s", lambda e: e.activation(spt[:], aps[:], AF.Abs),
         reads=("aps",), writes=("spt",))
    p.op("s", lambda e: e.activation(spt[:], spt[:], AF.Exp, scale=-1.0),
         reads=("spt",), writes=("spt",))
    p.op("s", lambda e: e.activation(spt[:], spt[:], AF.Ln, bias=onec[:]),
         reads=("spt", "onec"), writes=("spt",))
    p.op("s", lambda e: e.activation(sp8[:], aps[:], AF.Relu),
         reads=("aps",), writes=("sp8",))
    p.op("v", lambda e: e.tensor_add(sp8[:], sp8[:], spt[:]),
         reads=("sp8", "spt"), writes=("sp8",))
    p.op("v", lambda e: e.tensor_scalar_mul(sp8[:], sp8[:], -8.0),
         reads=("sp8",), writes=("sp8",))

    for c in range(NCH):
        p.op("y", lambda e, c=c: e.dma_start(
            out=xs[:], in_=xd[:, c * QT:(c + 1) * QT]
            .rearrange("(pc p) t -> p pc t", p=128)), writes=("xs",), dma=True,
            group="ld")
        p.op("v", lambda e: e.tensor_copy(xb[:], xs[:]), reads=("xs",), writes=("xb",))
        for h in range(PC):
            p.op("t", lambda e, h=h: e.matmul(
                psm[h % 2][:], wib[:, h * 128:(h + 1) * 128],
                xb[:, h * QT:(h + 1) * QT], start=True, stop=True),
                reads=("wib", "xb"), writes=(f"pA{h % 2}",))
            p.op("s", lambda e, h=h: e.activation(
                gx[:, h * QT:(h + 1) * QT], psm[h % 2][:], AF.Sigmoid),
                reads=(f"pA{h % 2}",), writes=("gx",))
            p.op("t", lambda e, h=h: e.matmul(
                psm[2 + h % 2][:], wab[:, h * 128:(h + 1) * 128],
                xb[:, h * QT:(h + 1) * QT], start=True, stop=True),
                reads=("wab", "xb"), writes=(f"pB{h % 2}",))
            p.op("s", lambda e, h=h: e.activation(
                ga[:, h * QT:(h + 1) * QT], psm[2 + h % 2][:], AF.Sigmoid),
                reads=(f"pB{h % 2}",), writes=("ga",))
        p.op("v", lambda e: e.tensor_tensor(
            ga[:].rearrange("p (pc t) -> p pc t", pc=PC),
            ga[:].rearrange("p (pc t) -> p pc t", pc=PC),
            sp8[:].unsqueeze(2).broadcast_to((128, PC, QT)), ALU.mult),
            reads=("ga", "sp8"), writes=("ga",))
        p.op("s", lambda e: e.activation(av[:], ga[:], AF.Exp),
             reads=("ga",), writes=("av",))
        p.op("s", lambda e: e.activation(ga[:], ga[:], AF.Exp, scale=2.0),
             reads=("ga",), writes=("ga",))
        p.op("v", lambda e: e.tensor_scalar(ga[:], ga[:], 1.0, -1.0,
                                            ALU.subtract, ALU.mult),
             reads=("ga",), writes=("ga",))
        p.op("s", lambda e: e.activation(ga[:], ga[:], AF.Sqrt),
             reads=("ga",), writes=("ga",))
        p.op("v", lambda e: e.tensor_mul(bv[:], gx[:], xs[:]),
             reads=("gx", "xs"), writes=("bv",))
        p.op("v", lambda e: e.tensor_mul(bv[:], bv[:], ga[:]),
             reads=("bv", "ga"), writes=("bv",))
        for pc in range(PC):
            p.op("v", lambda e, pc=pc: e.tensor_tensor_scan(
                sc[:, pc * QT:(pc + 1) * QT], av[:, pc * QT:(pc + 1) * QT],
                bv[:, pc * QT:(pc + 1) * QT], carry[:, pc:pc + 1],
                ALU.mult, ALU.add), reads=("av", "bv", "carry"), writes=("sc", f"sc{pc}"))
        p.op("v", lambda e: e.tensor_copy(
            carry[:], sc[:].rearrange("p (pc t) -> p pc t", pc=PC)[:, :, QT - 1]),
            reads=("sc",), writes=("carry",))
        p.op("v", lambda e: e.tensor_add(vsb[:], sc[:], xs[:]),
             reads=("sc", "xs"), writes=("vsb",))
        emit_postln(p, L, vsb, "vsb", xs[:], "xs")
        p.op("s", lambda e, c=c: e.dma_start(
            out=yd[:, c * QT:(c + 1) * QT].rearrange("(pc p) t -> p pc t", p=128),
            in_=xs[:]), reads=("xs",), writes=("ydone",), dma=True, group="st")



def build_l3a(T, QT=512):
    nc = bass.Bass("TRN2", target_bir_lowering=False)
    io = {}
    io["x"] = nc.dram_tensor("x", [D, T], F32, kind="ExternalInput").ap()
    io["w_in"] = nc.dram_tensor("w_in", [8, 128, 128], F32, kind="ExternalInput").ap()
    io["w_a"] = nc.dram_tensor("w_a", [8, 128, 128], F32, kind="ExternalInput").ap()
    io["ap_pm"] = nc.dram_tensor("ap_pm", [128, PC], F32, kind="ExternalInput").ap()
    io["y"] = nc.dram_tensor("y", [D, T], F32, kind="ExternalOutput").ap()
    p = Prog()
    emit_l3a(nc, p, io, T, QT)
    p.op("y", lambda e: e.nop(), reads=("ydone",))
    p.emit(nc)
    return nc
def emit_l1a(nc, p, io, T, QT=512, tag=""):
    NCH = T // QT
    H = 16
    xd = io["x"]
    qd = io["q_pm"]
    kd = io["k_pm"]
    nwd = io["nw_pm"]
    ldd = io["ld_row"]
    owd = io["owT"]
    hvd = io["halves"]
    hmd = io["hm16"]
    h64d = io["h64"]
    idd = io["ident"]
    yd = io["y"]

    ow = nc.alloc_sbuf_tensor(tag + "ow", [128, PC * D], BF16)
    qs = nc.alloc_sbuf_tensor(tag + "qs", [128, PC], F32)
    ks = nc.alloc_sbuf_tensor(tag + "ks", [128, PC], F32)
    nw = nc.alloc_sbuf_tensor(tag + "nw", [128, PC], F32)
    lds = nc.alloc_sbuf_tensor(tag + "lds", [1, H], F32)
    hvs = nc.alloc_sbuf_tensor(tag + "hvs", [128, 2], F32)
    hms = nc.alloc_sbuf_tensor(tag + "hms", [H, PC], F32)
    h64s = nc.alloc_sbuf_tensor(tag + "h64s", [H, 128], F32)
    ids = nc.alloc_sbuf_tensor(tag + "ids", [128, 128], F32)
    qkp = nc.alloc_sbuf_tensor(tag + "qkp", [128, PC], F32)
    QH = nc.alloc_sbuf_tensor(tag + "QH", [128, H], F32)
    onec = nc.alloc_sbuf_tensor(tag + "onec", [128, 1], F32)
    col16 = nc.alloc_sbuf_tensor(tag + "col16", [H, 1], F32)
    lamcol16 = nc.alloc_sbuf_tensor(tag + "lamcol16", [H, 1], F32)
    rhs168 = nc.alloc_sbuf_tensor(tag + "rhs168", [H, PC], F32)
    qkc = nc.alloc_sbuf_tensor(tag + "qkc", [128, PC], F32)
    lamc = nc.alloc_sbuf_tensor(tag + "lamc", [128, PC], F32)
    eps2 = nc.alloc_sbuf_tensor(tag + "eps2", [1, 1], F32)
    carry = nc.alloc_sbuf_tensor(tag + "carry1", [128, PC], F32)
    xs = nc.alloc_sbuf_tensor(tag + "xs", [128, PC * QT], F32)
    sc = nc.alloc_sbuf_tensor(tag + "sc", [128, PC * QT], F32)
    ov = nc.alloc_sbuf_tensor(tag + "ov", [128, PC * QT], F32)
    osq = nc.alloc_sbuf_tensor(tag + "osq", [128, PC * QT], F32)
    ob = nc.alloc_sbuf_tensor(tag + "ob", [128, PC * QT], BF16)
    vsb = nc.alloc_sbuf_tensor(tag + "vsb", [128, PC * QT], F32)
    ror = nc.alloc_sbuf_tensor(tag + "ror", [1, QT], F32)
    sor = nc.alloc_sbuf_tensor(tag + "sor", [1, QT], F32)
    rob = nc.alloc_sbuf_tensor(tag + "rob", [128, QT], F32)
    psm = [nc.alloc_psum_tensor(tag + f"psm{i}", [128, QT], F32) for i in range(2)]
    psro = nc.alloc_psum_tensor(tag + "psro", [2, QT], F32)
    L = LNT(nc, QT, tag=tag)
    psb1 = L.psb2[:, 0:QT]
    psi = psm[0][:, 0:2 + PC]


    p = p
    p.op("g", lambda e: e.dma_start(
        out=ow[:].rearrange("p (pc d) -> p pc d", pc=PC),
        in_=owd.rearrange("(pc p) d -> p pc d", p=128)), writes=("ow",), dma=True)
    for nm, dst, srcap in (("qs", qs, qd), ("ks", ks, kd), ("nw", nw, nwd),
                           ("lds", lds, ldd), ("hvs", hvs, hvd), ("hms", hms, hmd),
                           ("h64s", h64s, h64d), ("ids", ids, idd)):
        p.op("y", lambda e, dst=dst, srcap=srcap: e.dma_start(out=dst[:], in_=srcap[:]),
             writes=(nm,), dma=True)
    L.init_ops(p)
    p.op("v", lambda e: e.memset(onec[:], 1.0), writes=("onec",))
    p.op("v", lambda e: e.memset(eps2[:], 1e-6), writes=("eps2",))
    p.op("v", lambda e: e.memset(carry[:], 0.0), writes=("carry",))
    p.op("v", lambda e: e.tensor_mul(qkp[:], qs[:], ks[:]),
         reads=("qs", "ks"), writes=("qkp",))
    p.op("v", lambda e: e.tensor_tensor(
        QH[:].rearrange("p (a b) -> p a b", a=PC),
        qkp[:].unsqueeze(2).broadcast_to((128, PC, 2)),
        hvs[:].unsqueeze(1).broadcast_to((128, PC, 2)), ALU.mult),
        reads=("qkp", "hvs"), writes=("QH",))
    p.op("t", lambda e: e.matmul(psi[0:H, 0:1], QH[:], onec[:],
                                 start=True, stop=True),
         reads=("QH", "onec"), writes=("pA0",))
    p.op("v", lambda e: e.tensor_copy(col16[:], psi[0:H, 0:1]),
         reads=("pA0",), writes=("col16",))
    p.op("s", lambda e: e.activation(lds[:], lds[:], AF.Sigmoid),
         reads=("lds",), writes=("lds",))
    p.op("t", lambda e: e.transpose(psi[0:H, 1:2], lds[:], ids[0:1, 0:1]),
         reads=("lds", "ids"), writes=("pA0",))
    p.op("v", lambda e: e.tensor_copy(lamcol16[:], psi[0:H, 1:2]),
         reads=("pA0",), writes=("lamcol16",))
    for nm, col, out in (("qkc", col16, qkc), ("lamc", lamcol16, lamc)):
        p.op("v", lambda e, col=col: e.tensor_tensor(
            rhs168[:], hms[:], col[:].broadcast_to((H, PC)), ALU.mult),
            reads=("hms", nm + "_src" if False else ("col16" if col is col16 else "lamcol16")),
            writes=("rhs168",))
        p.op("t", lambda e: e.matmul(psi[:, 2:2 + PC], h64s[:], rhs168[:],
                                     start=True, stop=True),
             reads=("rhs168", "h64s"), writes=("pA0",))
        p.op("v", lambda e, out=out: e.tensor_copy(out[:], psi[:, 2:2 + PC]),
             reads=("pA0",), writes=(nm,))

    for c in range(NCH):
        p.op("y", lambda e, c=c: e.dma_start(
            out=xs[:], in_=xd[:, c * QT:(c + 1) * QT]
            .rearrange("(pc p) t -> p pc t", p=128)), writes=("xs",), dma=True,
            group="ld")
        for pc in range(PC):
            p.op("v", lambda e, pc=pc: e.tensor_tensor_scan(
                sc[:, pc * QT:(pc + 1) * QT],
                lamc[:, pc:pc + 1].broadcast_to((128, QT)),
                xs[:, pc * QT:(pc + 1) * QT], carry[:, pc:pc + 1],
                ALU.mult, ALU.add), reads=("lamc", "xs", "carry"), writes=("sc",))
        p.op("v", lambda e: e.tensor_copy(
            carry[:], sc[:].rearrange("p (pc t) -> p pc t", pc=PC)[:, :, QT - 1]),
            reads=("sc",), writes=("carry",))
        p.op("v", lambda e: e.tensor_tensor(
            ov[:].rearrange("p (pc t) -> p pc t", pc=PC),
            sc[:].rearrange("p (pc t) -> p pc t", pc=PC),
            qkc[:].unsqueeze(2).broadcast_to((128, PC, QT)), ALU.mult),
            reads=("sc", "qkc"), writes=("ov",))
        p.op("s", lambda e: e.activation(osq[:], ov[:], AF.Square),
             reads=("ov",), writes=("osq",))

        def rmsmm(e):
            for k in range(PC):
                mm = e.matmul(psro[:], L.onesc[:], osq[:, k * QT:(k + 1) * QT],
                              start=(k == 0), stop=(k == PC - 1))
            return mm
        p.op("t", rmsmm, reads=("osq", "ln_onesc"), writes=("psro",))
        p.op("v", lambda e: e.tensor_copy(ror[:], psro[0:1, :]),
             reads=("psro",), writes=("ror",))
        p.op("v", lambda e: e.tensor_scalar_mul(ror[:], ror[:], 1.0 / D),
             reads=("ror",), writes=("ror",))
        p.op("s", lambda e: e.activation(sor[:], ror[:], AF.Sqrt, bias=eps2[0:1, 0:1]),
             reads=("ror", "eps2"), writes=("sor",))
        p.op("v", lambda e: e.reciprocal(ror[:], sor[:]), reads=("sor",), writes=("ror",))
        p.op("t", lambda e: e.matmul(psb1, L.onesr[:], ror[:], start=True, stop=True),
             reads=("ror", "ln_onesr"), writes=("ln_psb2",))
        p.op("v", lambda e: e.tensor_copy(rob[:], psb1), reads=("ln_psb2",), writes=("rob",))
        p.op("v", lambda e: e.tensor_tensor(
            osq[:].rearrange("p (pc t) -> p pc t", pc=PC),
            ov[:].rearrange("p (pc t) -> p pc t", pc=PC),
            rob[:].unsqueeze(1).broadcast_to((128, PC, QT)), ALU.mult),
            reads=("ov", "rob"), writes=("osq",))
        p.op("v", lambda e: e.tensor_tensor(
            ob[:].rearrange("p (pc t) -> p pc t", pc=PC),
            osq[:].rearrange("p (pc t) -> p pc t", pc=PC),
            nw[:].unsqueeze(2).broadcast_to((128, PC, QT)), ALU.mult),
            reads=("osq", "nw"), writes=("ob",))
        for dd in range(PC):
            def omm(e, dd=dd):
                for k in range(PC):
                    mm = e.matmul(psm[dd % 2][:],
                                  ow[:, k * D + dd * 128:k * D + (dd + 1) * 128],
                                  ob[:, k * QT:(k + 1) * QT],
                                  start=(k == 0), stop=(k == PC - 1))
                return mm
            p.op("t", omm, reads=("ow", "ob"), writes=(f"pO{dd % 2}",))
            p.op("v", lambda e, dd=dd: e.tensor_tensor(
                vsb[:, dd * QT:(dd + 1) * QT], psm[dd % 2][:],
                xs[:, dd * QT:(dd + 1) * QT], ALU.add),
                reads=(f"pO{dd % 2}", "xs"), writes=("vsb",))
        emit_postln(p, L, vsb, "vsb", xs[:], "xs")
        p.op("s", lambda e, c=c: e.dma_start(
            out=yd[:, c * QT:(c + 1) * QT].rearrange("(pc p) t -> p pc t", p=128),
            in_=xs[:]), reads=("xs",), writes=("ydone",), dma=True, group="st")



def build_l1a(T, QT=512):
    nc = bass.Bass("TRN2", target_bir_lowering=False)
    io = {}
    io["x"] = nc.dram_tensor("x", [D, T], F32, kind="ExternalInput").ap()
    io["q_pm"] = nc.dram_tensor("q_pm", [128, PC], F32, kind="ExternalInput").ap()
    io["k_pm"] = nc.dram_tensor("k_pm", [128, PC], F32, kind="ExternalInput").ap()
    io["nw_pm"] = nc.dram_tensor("nw_pm", [128, PC], F32, kind="ExternalInput").ap()
    io["ld_row"] = nc.dram_tensor("ld_row", [1, H], F32, kind="ExternalInput").ap()
    io["owT"] = nc.dram_tensor("owT", [D, D], F32, kind="ExternalInput").ap()
    io["halves"] = nc.dram_tensor("halves", [128, 2], F32, kind="ExternalInput").ap()
    io["hm16"] = nc.dram_tensor("hm16", [H, PC], F32, kind="ExternalInput").ap()
    io["h64"] = nc.dram_tensor("h64", [H, 128], F32, kind="ExternalInput").ap()
    io["ident"] = nc.dram_tensor("ident", [128, 128], F32, kind="ExternalInput").ap()
    io["y"] = nc.dram_tensor("y", [D, T], F32, kind="ExternalOutput").ap()
    p = Prog()
    emit_l1a(nc, p, io, T, QT)
    p.op("y", lambda e: e.nop(), reads=("ydone",))
    p.emit(nc)
    return nc
def emit_l0f(nc, p, io, T, tag=""):
    QT = 128
    NCH = T // QT
    xd = io["x"]
    wd = io["w"]
    bd = io["bias_pm"]
    idd = io["ident"]
    yd = io["y"]

    wf = nc.alloc_sbuf_tensor(tag + "wf", [128, PC * D], F32)
    wqn = nc.alloc_sbuf_tensor(tag + "wqn", [128, PC * D], BF16)
    wqT = nc.alloc_sbuf_tensor(tag + "wqT", [128, PC * D], BF16)
    wsc = nc.alloc_sbuf_tensor(tag + "wsc", [128, PC], F32)
    wrs = nc.alloc_sbuf_tensor(tag + "wrs", [128, PC], F32)
    wi32 = nc.alloc_sbuf_tensor(tag + "wi32", [128, D], I32)
    bias = nc.alloc_sbuf_tensor(tag + "bias", [128, PC], F32)
    ids = nc.alloc_sbuf_tensor(tag + "ids", [128, 128], F32)
    idb = nc.alloc_sbuf_tensor(tag + "idb", [128, 128], BF16)
    xs = nc.alloc_sbuf_tensor(tag + "xs", [128, PC * QT], F32)
    amax = nc.alloc_sbuf_tensor(tag + "amax", [128, 1], F32)
    am2 = nc.alloc_sbuf_tensor(tag + "am2", [128, 2], F32)
    amr1 = nc.alloc_sbuf_tensor(tag + "amr1", [1, 128], F32)
    amr2 = nc.alloc_sbuf_tensor(tag + "amr2", [1, 128], F32)
    rsb = nc.alloc_sbuf_tensor(tag + "rsb", [128, 2 * QT], F32)
    xq1 = nc.alloc_sbuf_tensor(tag + "xq1", [128, PC * QT], F32)
    xqi = nc.alloc_sbuf_tensor(tag + "xqi", [128, PC * QT], I32)
    xqf = nc.alloc_sbuf_tensor(tag + "xqf", [128, PC * QT], F32)
    xqb = nc.alloc_sbuf_tensor(tag + "xqb", [128, PC * QT], BF16)
    vsb = nc.alloc_sbuf_tensor(tag + "vsb", [128, PC * QT], F32)
    t1s = nc.alloc_sbuf_tensor(tag + "t1s", [128, QT], F32)
    pst = [nc.alloc_psum_tensor(tag + f"pst{i}", [128, 128], F32) for i in range(2)]
    psq = [nc.alloc_psum_tensor(tag + f"psq{i}", [128, QT], F32) for i in range(2)]
    psb = nc.alloc_psum_tensor(tag + "psbc", [128, 2 * QT], F32)
    L = LNT(nc, QT, tag=tag)


    p = p
    p.op("y", lambda e: e.dma_start(out=ids[:], in_=idd[:]), writes=("ids",), dma=True)
    p.op("y", lambda e: e.dma_start(out=bias[:], in_=bd[:]), writes=("bias",), dma=True)
    p.op("y", lambda e: e.dma_start(
        out=wf[:].rearrange("p (pc d) -> p pc d", pc=PC),
        in_=wd.rearrange("(pc p) d -> p pc d", p=128)), writes=("wf",), dma=True)
    L.init_ops(p)
    p.op("v", lambda e: e.tensor_copy(idb[:], ids[:]), reads=("ids",), writes=("idb",))
    for pc in range(PC):
        p.op("v", lambda e, pc=pc: e.tensor_reduce(
            wsc[:, pc:pc + 1], wf[:, pc * D:(pc + 1) * D], AX.X, ALU.max,
            apply_absolute_value=True), reads=("wf",), writes=("wsc",))
    p.op("v", lambda e: e.tensor_scalar_max(wsc[:], wsc[:], 1e-8),
         reads=("wsc",), writes=("wsc",))
    p.op("v", lambda e: e.tensor_scalar_mul(wsc[:], wsc[:], 1.0 / 127.0),
         reads=("wsc",), writes=("wsc",))
    p.op("v", lambda e: e.reciprocal(wrs[:], wsc[:]), reads=("wsc",), writes=("wrs",))
    for pc in range(PC):
        p.op("v", lambda e, pc=pc: e.tensor_scalar(
            wf[:, pc * D:(pc + 1) * D], wf[:, pc * D:(pc + 1) * D],
            wrs[:, pc:pc + 1], None, ALU.mult), reads=("wf", "wrs"), writes=("wf",))
    for pc in range(PC):
        emit_rneq(p, wf[:, pc * D:(pc + 1) * D], "wf", wi32[:], "wi32",
                  xq1[:, 0:D], "xq1", vsb[:, 0:D], "vsb",
                  wqn[:, pc * D:(pc + 1) * D], "wqn")
    for a in range(PC):
        for bb in range(PC):
            p.op("t", lambda e, a=a, bb=bb: e.transpose(
                pst[(a * PC + bb) % 2][:].bitcast(BF16)[:, 0:128],
                wqn[:, a * D + bb * 128:a * D + (bb + 1) * 128], idb[:]),
                reads=("wqn", "idb"), writes=(f"pt{(a * PC + bb) % 2}",))
            p.op("v", lambda e, a=a, bb=bb: e.tensor_copy(
                wqT[:, bb * D + a * 128:bb * D + (a + 1) * 128],
                pst[(a * PC + bb) % 2][:].bitcast(BF16)[:, 0:128]),
                reads=(f"pt{(a * PC + bb) % 2}",), writes=("wqT",))

    for c in range(NCH):
        p.op("y", lambda e, c=c: e.dma_start(
            out=xs[:], in_=xd[:, c * QT:(c + 1) * QT]
            .rearrange("(pc p) t -> p pc t", p=128)), writes=("xs",), dma=True,
            group="ld")
        for pc in range(PC):
            p.op("t", lambda e, pc=pc: e.transpose(
                pst[pc % 2][:], xs[:, pc * QT:(pc + 1) * QT], ids[:]),
                reads=("xs", "ids"), writes=(f"pt{pc % 2}",))
            if pc == 0:
                p.op("v", lambda e: e.tensor_reduce(
                    amax[:], pst[0][:], AX.X, ALU.max, apply_absolute_value=True),
                    reads=("pt0",), writes=("amax",))
            else:
                p.op("v", lambda e, pc=pc: e.tensor_reduce(
                    t1s[:, 0:1], pst[pc % 2][:], AX.X, ALU.max,
                    apply_absolute_value=True), reads=(f"pt{pc % 2}",), writes=("t1s",))
                p.op("v", lambda e: e.tensor_max(amax[:], amax[:], t1s[:, 0:1]),
                     reads=("amax", "t1s"), writes=("amax",))
        p.op("v", lambda e: e.tensor_scalar_max(amax[:], amax[:], 1e-8),
             reads=("amax",), writes=("amax",))
        p.op("v", lambda e: e.reciprocal(am2[:, 0:1], amax[:]),
             reads=("amax",), writes=("am2",))
        p.op("v", lambda e: e.tensor_scalar_mul(am2[:, 0:1], am2[:, 0:1], 127.0),
             reads=("am2",), writes=("am2",))
        p.op("v", lambda e: e.tensor_scalar_mul(am2[:, 1:2], amax[:], 1.0 / 127.0),
             reads=("amax",), writes=("am2",))
        p.op("t", lambda e: e.transpose(pst[0][0:1, 0:128], am2[:, 0:1], ids[:]),
             reads=("am2", "ids"), writes=("pt0",))
        p.op("t", lambda e: e.transpose(pst[1][0:1, 0:128], am2[:, 1:2], ids[:]),
             reads=("am2", "ids"), writes=("pt1",))
        p.op("v", lambda e: e.tensor_copy(amr1[:], pst[0][0:1, 0:128]),
             reads=("pt0",), writes=("amr1",))
        p.op("v", lambda e: e.tensor_copy(amr2[:], pst[1][0:1, 0:128]),
             reads=("pt1",), writes=("amr2",))
        def bc2(e):
            e.matmul(psb[:, 0:QT], L.onesr[:], amr1[:], start=True, stop=True)
            return e.matmul(psb[:, QT:2 * QT], L.onesr[:], amr2[:],
                            start=True, stop=True)
        p.op("t", bc2, reads=("amr1", "amr2", "ln_onesr"), writes=("psbc",))
        p.op("v", lambda e: e.tensor_copy(rsb[:], psb[:]), reads=("psbc",), writes=("rsb",))
        p.op("v", lambda e: e.tensor_tensor(
            xq1[:].rearrange("p (pc t) -> p pc t", pc=PC),
            xs[:].rearrange("p (pc t) -> p pc t", pc=PC),
            rsb[:, 0:QT].unsqueeze(1).broadcast_to((128, PC, QT)), ALU.mult),
            reads=("xs", "rsb"), writes=("xq1",))
        emit_rneq(p, xq1[:], "xq1", xqi[:], "xqi", xqf[:], "xqf",
                  vsb[:], "vsb", xqb[:], "xqb")
        for dd in range(PC):
            def qmm(e, dd=dd):
                for k in range(PC):
                    mm = e.matmul(psq[dd % 2][:],
                                  wqT[:, k * D + dd * 128:k * D + (dd + 1) * 128],
                                  xqb[:, k * QT:(k + 1) * QT],
                                  start=(k == 0), stop=(k == PC - 1))
                return mm
            p.op("t", qmm, reads=("wqT", "xqb"), writes=(f"pq{dd % 2}",))
            p.op("v", lambda e, dd=dd: e.scalar_tensor_tensor(
                xq1[:, dd * QT:(dd + 1) * QT], psq[dd % 2][:], wsc[:, dd:dd + 1],
                rsb[:, QT:2 * QT], ALU.mult, ALU.mult),
                reads=(f"pq{dd % 2}", "wsc", "rsb"), writes=("xq1",))
            p.op("v", lambda e, dd=dd: e.scalar_tensor_tensor(
                vsb[:, dd * QT:(dd + 1) * QT], xq1[:, dd * QT:(dd + 1) * QT],
                bias[:, dd:dd + 1], xs[:, dd * QT:(dd + 1) * QT],
                ALU.add, ALU.add),
                reads=("xq1", "bias", "xs"), writes=("vsb",))
        emit_postln(p, L, vsb, "vsb", xs[:], "xs")
        p.op("s", lambda e, c=c: e.dma_start(
            out=yd[:, c * QT:(c + 1) * QT].rearrange("(pc p) t -> p pc t", p=128),
            in_=xs[:]), reads=("xs",), writes=("ydone",), dma=True, group="st")



def build_l0f(T):
    nc = bass.Bass("TRN2", target_bir_lowering=False)
    io = {}
    io["x"] = nc.dram_tensor("x", [D, T], F32, kind="ExternalInput").ap()
    io["w"] = nc.dram_tensor("w", [D, D], F32, kind="ExternalInput").ap()
    io["bias_pm"] = nc.dram_tensor("bias_pm", [128, PC], F32, kind="ExternalInput").ap()
    io["ident"] = nc.dram_tensor("ident", [128, 128], F32, kind="ExternalInput").ap()
    io["y"] = nc.dram_tensor("y", [D, T], F32, kind="ExternalOutput").ap()
    p = Prog()
    emit_l0f(nc, p, io, T)
    p.op("y", lambda e: e.nop(), reads=("ydone",))
    p.emit(nc)
    return nc
def emit_l1f(nc, p, io, T, tag=""):
    QT = 128
    H2 = 4096
    HC = H2 // 128
    NCH = T // QT
    xd = io["x"]
    n1d = io["nw1_pm"]
    n2d = io["nw2_pm"]
    w1d = io["w1T"]
    w2d = io["w2T"]
    acd = io["acts"]
    idd = io["ident"]
    yd = io["y"]

    wq1 = nc.alloc_sbuf_tensor(tag + "wq1", [128, PC * H2], BF16)
    wq2 = nc.alloc_sbuf_tensor(tag + "wq2", [128, HC * D], BF16)
    L = LNT(nc, QT, tag=tag)
    absacc = nc.alloc_sbuf_tensor(tag + "absacc", [128, 1], F32)
    redt = nc.alloc_sbuf_tensor(tag + "redt", [128, 1], F32)
    onec = nc.alloc_sbuf_tensor(tag + "onec", [128, 1], F32)
    cells = nc.alloc_sbuf_tensor(tag + "cells", [1, 8], F32)
    wsc1c = nc.alloc_sbuf_tensor(tag + "wsc1c", [128, 1], F32)
    wsc2c = nc.alloc_sbuf_tensor(tag + "wsc2c", [128, 1], F32)
    n1 = nc.alloc_sbuf_tensor(tag + "n1", [128, PC], F32)
    n2 = nc.alloc_sbuf_tensor(tag + "n2", [128, HC], F32)
    acs = nc.alloc_sbuf_tensor(tag + "acs", [1, 2], F32)
    acol = nc.alloc_sbuf_tensor(tag + "acol", [128, 1], F32)
    bcol = nc.alloc_sbuf_tensor(tag + "bcol", [128, 1], F32)
    ids = nc.alloc_sbuf_tensor(tag + "ids", [128, 128], F32)
    eps8 = nc.alloc_sbuf_tensor(tag + "eps8", [1, 1], F32)
    xs = nc.alloc_sbuf_tensor(tag + "xs", [128, PC * QT], F32)
    q1 = nc.alloc_sbuf_tensor(tag + "q1", [128, PC * QT], F32)
    q1g = nc.alloc_sbuf_tensor(tag + "q1g", [128, PC * QT], F32)
    q1b = nc.alloc_sbuf_tensor(tag + "q1b", [128, PC * QT], BF16)
    hb1 = nc.alloc_sbuf_tensor(tag + "hb1", [128, HC * QT], F32)
    h2i = nc.alloc_sbuf_tensor(tag + "h2i", [128, HC * QT // 4], I32)
    h2f = nc.alloc_sbuf_tensor(tag + "h2f", [128, HC * QT // 2], F32)
    h2g = nc.alloc_sbuf_tensor(tag + "h2g", [128, HC * QT], F32)
    h2b = nc.alloc_sbuf_tensor(tag + "h2b", [128, HC * QT], BF16)
    amax = nc.alloc_sbuf_tensor(tag + "amax", [128, 1], F32)
    gwb = L.mb
    osb = L.rb
    r_am = L.rq
    r_t = L.rrow
    r_st = L.rv
    r_ax = L.mrow
    r_gw = L.varrow
    r_os = L.srow
    pst = [nc.alloc_psum_tensor(tag + f"pstl{i}", [128, 128], F32) for i in range(2)]
    psq = [nc.alloc_psum_tensor(tag + f"psql{i}", [128, QT], F32) for i in range(2)]
    psst = nc.alloc_psum_tensor(tag + "psst", [2, 2 * QT], F32)
    psbc = nc.alloc_psum_tensor(tag + "psbcl", [128, 2 * QT], F32)


    p = p
    p.op("y", lambda e: e.dma_start(out=ids[:], in_=idd[:]), writes=("ids",), dma=True)
    p.op("y", lambda e: e.dma_start(out=n1[:], in_=n1d[:]), writes=("n1",), dma=True)
    p.op("y", lambda e: e.dma_start(out=n2[:], in_=n2d[:]), writes=("n2",), dma=True)
    p.op("y", lambda e: e.dma_start(out=acs[:], in_=acd[:]), writes=("acs",), dma=True)
    L.init_ops(p)
    p.op("v", lambda e: e.memset(onec[:], 1.0), writes=("onec",))
    p.op("v", lambda e: e.memset(eps8[:], 1e-8), writes=("eps8",))
    def expa(e):
        e.matmul(psbc[:, 0:1], L.onesr[:], acs[0:1, 0:1], start=True, stop=True)
        return e.matmul(psbc[:, 1:2], L.onesr[:], acs[0:1, 1:2], start=True, stop=True)
    p.op("t", expa, reads=("acs", "ln_onesr"), writes=("psbcl",))
    p.op("v", lambda e: e.tensor_copy(acol[:], psbc[:, 0:1]), reads=("psbcl",), writes=("acol",))
    p.op("v", lambda e: e.tensor_copy(bcol[:], psbc[:, 1:2]), reads=("psbcl",), writes=("bcol",))

    for wi, (wdap, nchunk, wqdst, wscc, cellix) in enumerate(
            ((w1d, PC, wq1, wsc1c, 0), (w2d, HC, wq2, wsc2c, 1))):
        wid = H2 if wi == 0 else D
        nh_ = max(1, wid // 2048)
        hw_ = wid // nh_
        first = True
        for s in range(nchunk):
            for hh in range(nh_):
                p.op("y", lambda e, s=s, wdap=wdap, hh=hh, hw_=hw_: e.dma_start(
                    out=h2f[:, 0:hw_],
                    in_=wdap[s * 128:(s + 1) * 128, hh * hw_:(hh + 1) * hw_]),
                    writes=("h2f",), dma=True, group=f"wls{wi}")
                p.op("v", lambda e, hw_=hw_: e.tensor_reduce(
                    redt[:], h2f[:, 0:hw_], AX.X, ALU.add, apply_absolute_value=True),
                    reads=("h2f",), writes=("redt",))
                if first:
                    p.op("v", lambda e: e.tensor_copy(absacc[:], redt[:]),
                         reads=("redt",), writes=("absacc",))
                    first = False
                else:
                    p.op("v", lambda e: e.tensor_add(absacc[:], absacc[:], redt[:]),
                         reads=("absacc", "redt"), writes=("absacc",))
        p.op("t", lambda e, cellix=cellix: e.matmul(
            psbc[0:1, 2 + cellix:3 + cellix], absacc[:],
            onec[:], start=True, stop=True),
             reads=("absacc", "onec"), writes=("psbcl",))
        p.op("v", lambda e, cellix=cellix: e.tensor_copy(
            cells[0:1, cellix:cellix + 1], psbc[0:1, 2 + cellix:3 + cellix]),
            reads=("psbcl",), writes=("cells",))
        nel = float(1024 * 4096)
        p.op("v", lambda e, cellix=cellix, nel=nel: e.tensor_scalar(
            cells[0:1, 2 + cellix:3 + cellix], cells[0:1, cellix:cellix + 1],
            1.0 / nel, 1e-5, ALU.mult, ALU.max), reads=("cells",), writes=("cells",))
        p.op("v", lambda e, cellix=cellix: e.reciprocal(
            cells[0:1, 4 + cellix:5 + cellix], cells[0:1, 2 + cellix:3 + cellix]),
            reads=("cells",), writes=("cells",))
        p.op("t", lambda e, cellix=cellix: e.matmul(
            psbc[:, 4 + cellix:5 + cellix], L.onesr[:],
            cells[0:1, 4 + cellix:5 + cellix], start=True, stop=True),
            reads=("cells", "ln_onesr"), writes=("psbcl",))
        p.op("v", lambda e, wscc=wscc, cellix=cellix: e.tensor_copy(
            wscc[:], psbc[:, 4 + cellix:5 + cellix]),
             reads=("psbcl",), writes=("wsc1c" if cellix == 0 else "wsc2c",))
        for s in range(nchunk):
            for hh in range(nh_):
                p.op("y", lambda e, s=s, wdap=wdap, hh=hh, hw_=hw_: e.dma_start(
                    out=h2f[:, 0:hw_],
                    in_=wdap[s * 128:(s + 1) * 128, hh * hw_:(hh + 1) * hw_]),
                    writes=("h2f",), dma=True, group=f"wls{wi}b")
                p.op("v", lambda e, hw_=hw_, wscc=wscc: e.tensor_scalar(
                    h2f[:, 0:hw_], h2f[:, 0:hw_], wscc[:, 0:1], None, ALU.mult),
                    reads=("h2f", "wsc1c" if wi == 0 else "wsc2c"), writes=("h2f",))
                for hf in range(max(1, hw_ // 1024)):
                    w0, w1 = hf * 1024, min(hw_, (hf + 1) * 1024)
                    emit_rneq(p, h2f[:, w0:w1], "h2f", h2i[:, 0:w1 - w0], "h2i",
                              hb1[:, 0:w1 - w0], "hb1", h2g[:, 0:w1 - w0], "h2g",
                              wqdst[:, s * wid + hh * hw_ + w0:
                                    s * wid + hh * hw_ + w1],
                              "wq1" if wi == 0 else "wq2", lo=-1.0, hi=1.0)

    for c in range(NCH):
        p.op("y", lambda e, c=c: e.dma_start(
            out=xs[:], in_=xd[:, c * QT:(c + 1) * QT]
            .rearrange("(pc p) t -> p pc t", p=128)), writes=("xs",), dma=True,
            group="ld")
        p.op("s", lambda e: e.activation(q1g[:], xs[:], AF.Square),
             reads=("xs",), writes=("q1g",))

        def st1(e):
            for k in range(PC):
                mm = e.matmul(psst[:, 0:QT], L.onesc[:], q1g[:, k * QT:(k + 1) * QT],
                              start=(k == 0), stop=(k == PC - 1))
            return mm
        p.op("t", st1, reads=("q1g", "ln_onesc"), writes=("psst",))
        p.op("v", lambda e: e.tensor_copy(r_st[:], psst[0:1, 0:QT]),
             reads=("psst",), writes=("ln_rv",))
        p.op("v", lambda e: e.tensor_scalar_mul(r_st[:], r_st[:], 1.0 / D),
             reads=("ln_rv",), writes=("ln_rv",))
        p.op("s", lambda e: e.activation(r_t[:], r_st[:], AF.Sqrt, bias=eps8[0:1, 0:1]),
             reads=("ln_rv", "eps8"), writes=("ln_rrow",))
        p.op("v", lambda e: e.reciprocal(r_st[:], r_t[:]), reads=("ln_rrow",), writes=("ln_rv",))
        p.op("v", lambda e: e.tensor_tensor(
            q1[:].rearrange("p (pc t) -> p pc t", pc=PC),
            xs[:].rearrange("p (pc t) -> p pc t", pc=PC),
            n1[:].unsqueeze(2).broadcast_to((128, PC, QT)), ALU.mult),
            reads=("xs", "n1"), writes=("q1",))
        for pc in range(PC):
            p.op("t", lambda e, pc=pc: e.transpose(
                pst[pc % 2][:], q1[:, pc * QT:(pc + 1) * QT], ids[:]),
                reads=("q1", "ids"), writes=(f"ptl{pc % 2}",))
            if pc == 0:
                p.op("v", lambda e: e.tensor_reduce(
                    amax[:], pst[0][:], AX.X, ALU.max, apply_absolute_value=True),
                    reads=("ptl0",), writes=("amax",))
            else:
                p.op("v", lambda e, pc=pc: e.tensor_reduce(
                    redt[:], pst[pc % 2][:], AX.X, ALU.max, apply_absolute_value=True),
                    reads=(f"ptl{pc % 2}",), writes=("redt",))
                p.op("v", lambda e: e.tensor_max(amax[:], amax[:], redt[:]),
                     reads=("amax", "redt"), writes=("amax",))
        p.op("t", lambda e: e.transpose(pst[0][0:1, 0:128], amax[:], ids[:]),
             reads=("amax", "ids"), writes=("ptl0",))
        p.op("v", lambda e: e.tensor_copy(r_am[:], pst[0][0:1, 0:128]),
             reads=("ptl0",), writes=("ln_rq",))
        p.op("v", lambda e: e.tensor_mul(r_ax[:], r_am[:], r_st[:]),
             reads=("ln_rq", "ln_rv"), writes=("ln_mrow",))
        p.op("v", lambda e: e.tensor_scalar_max(r_ax[:], r_ax[:], 1e-5),
             reads=("ln_mrow",), writes=("ln_mrow",))
        p.op("v", lambda e: e.reciprocal(r_t[:], r_ax[:]), reads=("ln_mrow",), writes=("ln_rrow",))
        p.op("v", lambda e: e.tensor_mul(r_gw[:], r_t[:], r_st[:]),
             reads=("ln_rrow", "ln_rv"), writes=("ln_var",))
        p.op("v", lambda e: e.tensor_scalar_mul(r_gw[:], r_gw[:], 127.0),
             reads=("ln_var",), writes=("ln_var",))
        p.op("v", lambda e: e.tensor_scalar(r_os[:], r_ax[:], cells[0:1, 2:3],
                                            1.0 / 127.0, ALU.mult, ALU.mult),
             reads=("ln_mrow", "cells"), writes=("ln_srow",))

        def bc1(e):
            e.matmul(psbc[:, 0:QT], L.onesr[:], r_gw[:], start=True, stop=True)
            return e.matmul(psbc[:, QT:2 * QT], L.onesr[:], r_os[:],
                            start=True, stop=True)
        p.op("t", bc1, reads=("ln_var", "ln_srow", "ln_onesr"), writes=("psbcl",))
        p.op("v", lambda e: e.tensor_copy(gwb[:], psbc[:, 0:QT]),
             reads=("psbcl",), writes=("ln_mb",))
        p.op("v", lambda e: e.tensor_copy(osb[:], psbc[:, QT:2 * QT]),
             reads=("psbcl",), writes=("ln_rb",))
        p.op("v", lambda e: e.tensor_tensor(
            q1[:].rearrange("p (pc t) -> p pc t", pc=PC),
            q1[:].rearrange("p (pc t) -> p pc t", pc=PC),
            gwb[:].unsqueeze(1).broadcast_to((128, PC, QT)), ALU.mult),
            reads=("q1", "ln_mb"), writes=("q1",))
        emit_rneq(p, q1[:], "q1", h2i[:, 0:PC * QT], "h2i",
                  h2f[:, 0:PC * QT], "h2f", q1g[:], "q1g", q1b[:], "q1b")
        for dd in range(HC):
            def m1(e, dd=dd):
                for k in range(PC):
                    mm = e.matmul(psq[dd % 2][:],
                                  wq1[:, k * H2 + dd * 128:k * H2 + (dd + 1) * 128],
                                  q1b[:, k * QT:(k + 1) * QT],
                                  start=(k == 0), stop=(k == PC - 1))
                return mm
            p.op("t", m1, reads=("wq1", "q1b"), writes=(f"pql{dd % 2}",))
            p.op("s", lambda e, dd=dd: e.activation(
                hb1[:, dd * QT:(dd + 1) * QT], psq[dd % 2][:], AF.Relu),
                reads=(f"pql{dd % 2}",), writes=("hb1",))
        p.op("v", lambda e: e.tensor_mul(h2g[:], hb1[:], hb1[:]),
             reads=("hb1",), writes=("h2g",))
        p.op("v", lambda e: e.tensor_mul(r_t[:], r_os[:], r_os[:]),
             reads=("ln_srow",), writes=("ln_rrow",))
        p.op("v", lambda e: e.tensor_scalar(r_t[:], r_t[:], acs[0:1, 0:1], None,
                                            ALU.mult),
             reads=("ln_rrow", "acs"), writes=("ln_rrow",))
        p.op("t", lambda e: e.matmul(psbc[:, 0:QT], L.onesr[:], r_t[:],
                                     start=True, stop=True),
             reads=("ln_rrow", "ln_onesr"), writes=("psbcl",))
        p.op("v", lambda e: e.tensor_copy(gwb[:], psbc[:, 0:QT]),
             reads=("psbcl",), writes=("ln_mb",))
        p.op("v", lambda e: e.tensor_tensor(
            hb1[:].rearrange("p (hc t) -> p hc t", hc=HC),
            h2g[:].rearrange("p (hc t) -> p hc t", hc=HC),
            gwb[:].unsqueeze(1).broadcast_to((128, HC, QT)), ALU.mult),
            reads=("h2g", "ln_mb"), writes=("hb1",))
        p.op("v", lambda e: e.tensor_scalar(hb1[:], hb1[:], bcol[:, 0:1], None,
                                            ALU.add),
             reads=("hb1", "bcol"), writes=("hb1",))
        p.op("s", lambda e: e.activation(h2g[:], hb1[:], AF.Square),
             reads=("hb1",), writes=("h2g",))

        def st2(e):
            for k in range(HC):
                mm = e.matmul(psst[:, QT:2 * QT], L.onesc[:],
                              h2g[:, k * QT:(k + 1) * QT],
                              start=(k == 0), stop=(k == HC - 1))
            return mm
        p.op("t", st2, reads=("h2g", "ln_onesc"), writes=("psst",))
        p.op("v", lambda e: e.tensor_copy(r_st[:], psst[0:1, QT:2 * QT]),
             reads=("psst",), writes=("ln_rv",))
        p.op("v", lambda e: e.tensor_scalar_mul(r_st[:], r_st[:], 1.0 / H2),
             reads=("ln_rv",), writes=("ln_rv",))
        p.op("s", lambda e: e.activation(r_t[:], r_st[:], AF.Sqrt, bias=eps8[0:1, 0:1]),
             reads=("ln_rv", "eps8"), writes=("ln_rrow",))
        p.op("v", lambda e: e.reciprocal(r_st[:], r_t[:]), reads=("ln_rrow",), writes=("ln_rv",))
        p.op("v", lambda e: e.tensor_tensor(
            h2g[:].rearrange("p (hc t) -> p hc t", hc=HC),
            hb1[:].rearrange("p (hc t) -> p hc t", hc=HC),
            n2[:].unsqueeze(2).broadcast_to((128, HC, QT)), ALU.mult),
            reads=("hb1", "n2"), writes=("h2g",))
        for hc in range(HC):
            p.op("t", lambda e, hc=hc: e.transpose(
                pst[hc % 2][:], h2g[:, hc * QT:(hc + 1) * QT], ids[:]),
                reads=("h2g", "ids"), writes=(f"ptl{hc % 2}",))
            if hc == 0:
                p.op("v", lambda e: e.tensor_reduce(
                    amax[:], pst[0][:], AX.X, ALU.max, apply_absolute_value=True),
                    reads=("ptl0",), writes=("amax",))
            else:
                p.op("v", lambda e, hc=hc: e.tensor_reduce(
                    redt[:], pst[hc % 2][:], AX.X, ALU.max, apply_absolute_value=True),
                    reads=(f"ptl{hc % 2}",), writes=("redt",))
                p.op("v", lambda e: e.tensor_max(amax[:], amax[:], redt[:]),
                     reads=("amax", "redt"), writes=("amax",))
        p.op("t", lambda e: e.transpose(pst[0][0:1, 0:128], amax[:], ids[:]),
             reads=("amax", "ids"), writes=("ptl0",))
        p.op("v", lambda e: e.tensor_copy(r_am[:], pst[0][0:1, 0:128]),
             reads=("ptl0",), writes=("ln_rq",))
        p.op("v", lambda e: e.tensor_mul(r_ax[:], r_am[:], r_st[:]),
             reads=("ln_rq", "ln_rv"), writes=("ln_mrow",))
        p.op("v", lambda e: e.tensor_scalar_max(r_ax[:], r_ax[:], 1e-5),
             reads=("ln_mrow",), writes=("ln_mrow",))
        p.op("v", lambda e: e.reciprocal(r_t[:], r_ax[:]), reads=("ln_mrow",), writes=("ln_rrow",))
        p.op("v", lambda e: e.tensor_mul(r_gw[:], r_t[:], r_st[:]),
             reads=("ln_rrow", "ln_rv"), writes=("ln_var",))
        p.op("v", lambda e: e.tensor_scalar_mul(r_gw[:], r_gw[:], 127.0),
             reads=("ln_var",), writes=("ln_var",))
        p.op("v", lambda e: e.tensor_scalar(r_os[:], r_ax[:], cells[0:1, 3:4],
                                            1.0 / 127.0, ALU.mult, ALU.mult),
             reads=("ln_mrow", "cells"), writes=("ln_srow",))

        def bc2(e):
            e.matmul(psbc[:, 0:QT], L.onesr[:], r_gw[:], start=True, stop=True)
            return e.matmul(psbc[:, QT:2 * QT], L.onesr[:], r_os[:],
                            start=True, stop=True)
        p.op("t", bc2, reads=("ln_var", "ln_srow", "ln_onesr"), writes=("psbcl",))
        p.op("v", lambda e: e.tensor_copy(gwb[:], psbc[:, 0:QT]),
             reads=("psbcl",), writes=("ln_mb",))
        p.op("v", lambda e: e.tensor_copy(osb[:], psbc[:, QT:2 * QT]),
             reads=("psbcl",), writes=("ln_rb",))
        p.op("v", lambda e: e.tensor_tensor(
            h2g[:].rearrange("p (hc t) -> p hc t", hc=HC),
            h2g[:].rearrange("p (hc t) -> p hc t", hc=HC),
            gwb[:].unsqueeze(1).broadcast_to((128, HC, QT)), ALU.mult),
            reads=("h2g", "ln_mb"), writes=("h2g",))
        HW4 = HC * QT // 4
        for half in range(4):
            emit_rneq(p, h2g[:, half * HW4:(half + 1) * HW4], "h2g",
                      h2i[:], "h2i", h2f[:, 0:HW4], "h2f",
                      hb1[:, 0:HW4], "hb1",
                      h2b[:, half * HW4:(half + 1) * HW4], "h2b")
        for dd in range(PC):
            def m2(e, dd=dd):
                for k in range(HC):
                    mm = e.matmul(psq[dd % 2][:],
                                  wq2[:, k * D + dd * 128:k * D + (dd + 1) * 128],
                                  h2b[:, k * QT:(k + 1) * QT],
                                  start=(k == 0), stop=(k == HC - 1))
                return mm
            p.op("t", m2, reads=("wq2", "h2b"), writes=(f"pql{dd % 2}",))
            p.op("v", lambda e, dd=dd: e.tensor_tensor(
                q1[:, dd * QT:(dd + 1) * QT], psq[dd % 2][:], osb[:], ALU.mult),
                reads=(f"pql{dd % 2}", "ln_rb"), writes=("q1",))
            p.op("v", lambda e, dd=dd: e.tensor_add(
                q1g[:, dd * QT:(dd + 1) * QT], q1[:, dd * QT:(dd + 1) * QT],
                xs[:, dd * QT:(dd + 1) * QT]),
                reads=("q1", "xs"), writes=("q1g",))
        emit_postln(p, L, q1g[:, 0:PC * QT], "q1g", xs[:], "xs")
        p.op("s", lambda e, c=c: e.dma_start(
            out=yd[:, c * QT:(c + 1) * QT].rearrange("(pc p) t -> p pc t", p=128),
            in_=xs[:]), reads=("xs",), writes=("ydone",), dma=True, group="st")



def build_l1f(T):
    nc = bass.Bass("TRN2", target_bir_lowering=False, dynamic_dma_scratch_size=8192)
    io = {}
    io["x"] = nc.dram_tensor("x", [D, T], F32, kind="ExternalInput").ap()
    io["nw1_pm"] = nc.dram_tensor("nw1_pm", [128, PC], F32, kind="ExternalInput").ap()
    io["nw2_pm"] = nc.dram_tensor("nw2_pm", [128, 32], F32, kind="ExternalInput").ap()
    io["w1T"] = nc.dram_tensor("w1T", [D, H2], F32, kind="ExternalInput").ap()
    io["w2T"] = nc.dram_tensor("w2T", [H2, D], F32, kind="ExternalInput").ap()
    io["acts"] = nc.dram_tensor("acts", [1, 2], F32, kind="ExternalInput").ap()
    io["ident"] = nc.dram_tensor("ident", [128, 128], F32, kind="ExternalInput").ap()
    io["y"] = nc.dram_tensor("y", [D, T], F32, kind="ExternalOutput").ap()
    p = Prog()
    emit_l1f(nc, p, io, T)
    p.op("y", lambda e: e.nop(), reads=("ydone",))
    p.emit(nc)
    return nc
def emit_l2a(nc, p, io, T, QT=512, tag=""):
    NCH = T // QT
    H = 16
    DI = 1024
    xd = io["x"]
    gmd = io["gam_pm"]
    wqd = io["wqkv"]
    wod = io["wout"]
    tpd = io["temp16"]
    hmd = io["hm16"]
    h64d = io["h64"]
    idd = io["ident"]
    yd = io["y"]

    wq = nc.alloc_sbuf_tensor(tag + "wq", [128, PC * 3 * DI], BF16)
    wo = nc.alloc_sbuf_tensor(tag + "wo", [128, PC * D], BF16)
    gm = nc.alloc_sbuf_tensor(tag + "gm", [128, PC], F32)
    tps = nc.alloc_sbuf_tensor(tag + "tps", [1, H], F32)
    hms = nc.alloc_sbuf_tensor(tag + "hms", [H, PC], F32)
    h64s = nc.alloc_sbuf_tensor(tag + "h64s", [H, 128], F32)
    ids = nc.alloc_sbuf_tensor(tag + "ids", [128, 128], F32)
    idb = nc.alloc_sbuf_tensor(tag + "idb", [128, 128], BF16)
    xs = nc.alloc_sbuf_tensor(tag + "xs", [128, PC * QT], F32)
    xnb = nc.alloc_sbuf_tensor(tag + "xnb", [128, PC * QT], BF16)
    qv = nc.alloc_sbuf_tensor(tag + "qv", [128, PC * QT], F32)
    kv = nc.alloc_sbuf_tensor(tag + "kv", [128, PC * QT], F32)
    sqs = nc.alloc_sbuf_tensor(tag + "sqs", [128, PC * QT], F32)
    vsb = nc.alloc_sbuf_tensor(tag + "vsb", [128, PC * QT], F32)
    vb16 = nc.alloc_sbuf_tensor(tag + "vb16", [128, PC * QT], BF16)
    qtb = nc.alloc_sbuf_tensor(tag + "qtb", [128, PC * QT], BF16)
    ktb = nc.alloc_sbuf_tensor(tag + "ktb", [128, PC * QT], BF16)
    simacc = nc.alloc_sbuf_tensor(tag + "simacc", [128, PC * 128], F32)
    attnb = nc.alloc_sbuf_tensor(tag + "attnb", [128, PC * 128], BF16)
    qss = nc.alloc_sbuf_tensor(tag + "qss", [128, PC], F32)
    kss = nc.alloc_sbuf_tensor(tag + "kss", [128, PC], F32)
    red1 = nc.alloc_sbuf_tensor(tag + "red1", [128, PC], F32)
    qsc = nc.alloc_sbuf_tensor(tag + "qsc", [128, PC], F32)
    ksc = nc.alloc_sbuf_tensor(tag + "ksc", [128, PC], F32)
    tcol = nc.alloc_sbuf_tensor(tag + "tcol", [128, PC], F32)
    t16 = nc.alloc_sbuf_tensor(tag + "t16", [H, 1], F32)
    r168 = nc.alloc_sbuf_tensor(tag + "r168", [H, PC], F32)
    smax = nc.alloc_sbuf_tensor(tag + "smax", [128, 1], F32)
    ssum = nc.alloc_sbuf_tensor(tag + "ssum", [128, 1], F32)
    nrow = nc.alloc_sbuf_tensor(tag + "nrow", [1, QT], F32)
    srow = nc.alloc_sbuf_tensor(tag + "srow2", [1, QT], F32)
    nb = nc.alloc_sbuf_tensor(tag + "nb", [128, QT], F32)
    pm1 = [nc.alloc_psum_tensor(tag + f"pm1{i}", [128, QT], F32) for i in range(2)]
    pstt = [nc.alloc_psum_tensor(tag + f"pstt{i}", [128, 128], F32) for i in range(2)]
    L = LNT(nc, QT, tag=tag, ext_vsq=sqs, ext_vsq_key="sqs")
    pss = L.psb2[:, 0:QT]


    p = p
    p.op("g", lambda e: e.dma_start(
        out=wq[:].rearrange("p (pc q) -> p pc q", pc=PC),
        in_=wqd.rearrange("(pc p) q -> p pc q", p=128)), writes=("wq",), dma=True)
    p.op("g", lambda e: e.dma_start(
        out=wo[:].rearrange("p (pc d) -> p pc d", pc=PC),
        in_=wod.rearrange("(pc p) d -> p pc d", p=128)), writes=("wo",), dma=True)
    for nm, dst, srcap in (("gm", gm, gmd), ("tps", tps, tpd), ("hms", hms, hmd),
                           ("h64s", h64s, h64d), ("ids", ids, idd)):
        p.op("y", lambda e, dst=dst, srcap=srcap: e.dma_start(out=dst[:], in_=srcap[:]),
             writes=(nm,), dma=True)
    L.init_ops(p)
    p.op("v", lambda e: e.tensor_copy(idb[:], ids[:]), reads=("ids",), writes=("idb",))
    p.op("v", lambda e: e.memset(simacc[:], 0.0), writes=("simacc",))
    p.op("v", lambda e: e.memset(qss[:], 0.0), writes=("qss",))
    p.op("v", lambda e: e.memset(kss[:], 0.0), writes=("kss",))

    def emit_xn(p):
        p.op("s", lambda e: e.activation(sqs[:], xs[:], AF.Square),
             reads=("xs",), writes=("sqs",))

        def stm(e):
            for k in range(PC):
                mm = e.matmul(L.psr[:, 0:QT], L.onesc[:], sqs[:, k * QT:(k + 1) * QT],
                              start=(k == 0), stop=(k == PC - 1))
            return mm
        p.op("t", stm, reads=("sqs", "ln_onesc"), writes=("ln_psr",))
        p.op("v", lambda e: e.tensor_copy(nrow[:], L.psr[0:1, 0:QT]),
             reads=("ln_psr",), writes=("nrow",))
        p.op("s", lambda e: e.activation(srow[:], nrow[:], AF.Sqrt),
             reads=("nrow",), writes=("srow2",))
        p.op("v", lambda e: e.tensor_scalar_max(srow[:], srow[:], 1e-12),
             reads=("srow2",), writes=("srow2",))
        p.op("v", lambda e: e.reciprocal(nrow[:], srow[:]),
             reads=("srow2",), writes=("nrow",))
        p.op("v", lambda e: e.tensor_scalar_mul(nrow[:], nrow[:], 32.0),
             reads=("nrow",), writes=("nrow",))
        p.op("t", lambda e: e.matmul(pss, L.onesr[:], nrow[:],
                                     start=True, stop=True),
             reads=("nrow", "ln_onesr"), writes=("ln_psb2",))
        p.op("v", lambda e: e.tensor_copy(nb[:], pss),
             reads=("ln_psb2",), writes=("nb",))
        p.op("v", lambda e: e.tensor_tensor(
            sqs[:].rearrange("p (pc t) -> p pc t", pc=PC),
            xs[:].rearrange("p (pc t) -> p pc t", pc=PC),
            nb[:].unsqueeze(1).broadcast_to((128, PC, QT)), ALU.mult),
            reads=("xs", "nb"), writes=("sqs",))
        p.op("v", lambda e: e.tensor_tensor(
            xnb[:].rearrange("p (pc t) -> p pc t", pc=PC),
            sqs[:].rearrange("p (pc t) -> p pc t", pc=PC),
            gm[:].unsqueeze(2).broadcast_to((128, PC, QT)), ALU.mult),
            reads=("sqs", "gm"), writes=("xnb",))

    def emit_proj(p, off, out_t, okey):
        for dd in range(PC):
            def qm(e, dd=dd, off=off):
                for k in range(PC):
                    mm = e.matmul(pm1[dd % 2][:],
                                  wq[:, k * 3 * DI + off + dd * 128:
                                     k * 3 * DI + off + (dd + 1) * 128],
                                  xnb[:, k * QT:(k + 1) * QT],
                                  start=(k == 0), stop=(k == PC - 1))
                return mm
            p.op("t", qm, reads=("wq", "xnb"), writes=(f"pm{dd % 2}",))
            p.op("v", lambda e, dd=dd, out_t=out_t, okey=okey: e.tensor_copy(
                out_t[:, dd * QT:(dd + 1) * QT], pm1[dd % 2][:]),
                reads=(f"pm{dd % 2}",), writes=(okey,))

    nTB = QT // 128
    for c in range(NCH):
        p.op("y", lambda e, c=c: e.dma_start(
            out=xs[:], in_=xd[:, c * QT:(c + 1) * QT]
            .rearrange("(pc p) t -> p pc t", p=128)), writes=("xs",), dma=True,
            group="ld")
        emit_xn(p)
        emit_proj(p, 0, qv, "qv")
        emit_proj(p, DI, kv, "kv")
        for src_t, acc, skey, akey in ((qv, qss, "qv", "qss"), (kv, kss, "kv", "kss")):
            p.op("s", lambda e, src_t=src_t, skey=skey: e.activation(
                sqs[:], src_t[:], AF.Square), reads=(skey,), writes=("sqs",))
            for pc in range(PC):
                p.op("v", lambda e, pc=pc: e.tensor_reduce(
                    red1[:, pc:pc + 1], sqs[:, pc * QT:(pc + 1) * QT], AX.X, ALU.add),
                    reads=("sqs",), writes=("red1",))
            p.op("v", lambda e, acc=acc, akey=akey: e.tensor_add(acc[:], acc[:], red1[:]),
                 reads=(akey, "red1"), writes=(akey,))
        for src_t, dstb, skey, dkey in ((qv, qtb, "qv", "qtb"), (kv, ktb, "kv", "ktb")):
            p.op("v", lambda e, src_t=src_t: e.tensor_copy(vb16[:], src_t[:]),
                 reads=(skey,), writes=("vb16",))
            for j in range(PC * nTB):
                p.op("t", lambda e, j=j: e.transpose(
                    pstt[j % 2][:].bitcast(BF16)[:, 0:128],
                    vb16[:, j * 128:(j + 1) * 128], idb[:]),
                    reads=("vb16", "idb"), writes=(f"pstt{j % 2}",))
                p.op("v", lambda e, j=j, dstb=dstb, dkey=dkey: e.tensor_copy(
                    dstb[:, j * 128:(j + 1) * 128],
                    pstt[j % 2][:].bitcast(BF16)[:, 0:128]),
                    reads=(f"pstt{j % 2}",), writes=(dkey,))
        for pc in range(PC):
            def smm(e, pc=pc):
                for tb in range(nTB):
                    j = pc * nTB + tb
                    mm = e.matmul(pstt[0][:],
                                  qtb[:, j * 128:(j + 1) * 128],
                                  ktb[:, j * 128:(j + 1) * 128],
                                  start=(tb == 0), stop=(tb == nTB - 1))
                return mm
            p.op("t", smm, reads=("qtb", "ktb"), writes=("pstt0",))
            p.op("v", lambda e, pc=pc: e.tensor_tensor(
                simacc[:, pc * 128:(pc + 1) * 128], simacc[:, pc * 128:(pc + 1) * 128],
                pstt[0][:], ALU.add), reads=("simacc", "pstt0"), writes=("simacc",))

    p.op("s", lambda e: e.activation(tps[:], tps[:], AF.Exp),
         reads=("tps",), writes=("tps",))
    p.op("t", lambda e: e.transpose(pstt[0][0:H, 0:1], tps[:], ids[0:1, 0:1]),
         reads=("tps", "ids"), writes=("pstt0",))
    p.op("v", lambda e: e.tensor_copy(t16[:], pstt[0][0:H, 0:1]),
         reads=("pstt0",), writes=("t16",))
    for acc, dst, akey, dkey in ((qss, qsc, "qss", "qsc"), (kss, ksc, "kss", "ksc")):
        p.op("s", lambda e, acc=acc, dst=dst, akey=akey: e.activation(
            dst[:], acc[:], AF.Sqrt), reads=(akey,), writes=(dkey,))
        p.op("v", lambda e, dst=dst, dkey=dkey: e.tensor_scalar_max(
            dst[:], dst[:], 1e-12), reads=(dkey,), writes=(dkey,))
        p.op("v", lambda e, dst=dst, dkey=dkey: e.reciprocal(dst[:], dst[:]),
             reads=(dkey,), writes=(dkey,))
    p.op("v", lambda e: e.tensor_tensor(r168[:], hms[:],
                                        t16[:].broadcast_to((H, PC)), ALU.mult),
         reads=("hms", "t16"), writes=("r168",))
    p.op("t", lambda e: e.matmul(pstt[0][:, 0:PC], h64s[:], r168[:],
                                 start=True, stop=True),
         reads=("h64s", "r168"), writes=("pstt0",))
    p.op("v", lambda e: e.tensor_copy(tcol[:], pstt[0][:, 0:PC]),
         reads=("pstt0",), writes=("tcol",))
    p.op("v", lambda e: e.tensor_scalar_mul(tcol[:], tcol[:], 8.0),
         reads=("tcol",), writes=("tcol",))
    p.op("v", lambda e: e.tensor_mul(qsc[:], qsc[:], tcol[:]),
         reads=("qsc", "tcol"), writes=("qsc",))
    for pc in range(PC):
        p.op("v", lambda e, pc=pc: e.tensor_scalar(
            simacc[:, pc * 128:(pc + 1) * 128], simacc[:, pc * 128:(pc + 1) * 128],
            qsc[:, pc:pc + 1], None, ALU.mult),
            reads=("simacc", "qsc"), writes=("simacc",))
        p.op("t", lambda e, pc=pc: e.transpose(
            pstt[0][:], simacc[:, pc * 128:(pc + 1) * 128], ids[:]),
            reads=("simacc", "ids"), writes=("pstt0",))
        p.op("v", lambda e, pc=pc: e.scalar_tensor_tensor(
            vsb[:, 0:128], pstt[0][:], ksc[:, pc:pc + 1], pstt[0][:],
            ALU.mult, ALU.bypass) if False else e.tensor_scalar(
            vsb[:, 0:128], pstt[0][:], ksc[:, pc:pc + 1], None, ALU.mult),
            reads=("pstt0", "ksc"), writes=("vsb",))
        p.op("t", lambda e, pc=pc: e.transpose(pstt[1][:], vsb[:, 0:128], ids[:]),
             reads=("vsb", "ids"), writes=("pstt1",))
        p.op("v", lambda e, pc=pc: e.tensor_copy(
            simacc[:, pc * 128:(pc + 1) * 128], pstt[1][:]),
            reads=("pstt1",), writes=("simacc",))
        for i in range(2):
            sl = simacc[i * 64:(i + 1) * 64, pc * 128 + i * 64:pc * 128 + (i + 1) * 64]
            p.op("v", lambda e, sl=sl, i=i: e.reduce_max(
                smax[i * 64:(i + 1) * 64, 0:1], sl, AX.X),
                reads=("simacc",), writes=("smax",))
            p.op("v", lambda e, sl=sl, i=i: e.tensor_scalar(
                sl, sl, smax[i * 64:(i + 1) * 64, 0:1], None, ALU.subtract),
                reads=("simacc", "smax"), writes=("simacc",))
            p.op("s", lambda e, sl=sl: e.activation(sl, sl, AF.Exp),
                 reads=("simacc",), writes=("simacc",))
            p.op("v", lambda e, sl=sl, i=i: e.reduce_sum(
                ssum[i * 64:(i + 1) * 64, 0:1], sl, AX.X),
                reads=("simacc",), writes=("ssum",))
            p.op("v", lambda e, i=i: e.reciprocal(
                ssum[i * 64:(i + 1) * 64, 0:1], ssum[i * 64:(i + 1) * 64, 0:1]),
                reads=("ssum",), writes=("ssum",))
            p.op("v", lambda e, sl=sl, i=i: e.tensor_scalar(
                sl, sl, ssum[i * 64:(i + 1) * 64, 0:1], None, ALU.mult),
                reads=("simacc", "ssum"), writes=("simacc",))
        p.op("v", lambda e, pc=pc: e.tensor_copy(
            vb16[:, 0:128], simacc[:, pc * 128:(pc + 1) * 128]),
            reads=("simacc",), writes=("vb16",))
        p.op("t", lambda e, pc=pc: e.transpose(
            pstt[0][:].bitcast(BF16)[:, 0:128], vb16[:, 0:128], idb[:]),
            reads=("vb16", "idb"), writes=("pstt0",))
        p.op("v", lambda e, pc=pc: e.tensor_copy(
            attnb[:, pc * 128:(pc + 1) * 128], pstt[0][:].bitcast(BF16)[:, 0:128]),
            reads=("pstt0",), writes=("attnb",))

    for c in range(NCH):
        p.op("y", lambda e, c=c: e.dma_start(
            out=xs[:], in_=xd[:, c * QT:(c + 1) * QT]
            .rearrange("(pc p) t -> p pc t", p=128)), writes=("xs",), dma=True,
            group="ld2")
        emit_xn(p)
        emit_proj(p, 2 * DI, qv, "qv")
        p.op("v", lambda e: e.tensor_copy(vb16[:], qv[:]), reads=("qv",), writes=("vb16",))
        for pc in range(PC):
            for i in range(2):
                def am(e, pc=pc, i=i):
                    return e.matmul(
                        pm1[pc % 2][i * 64:(i + 1) * 64, :],
                        attnb[i * 64:(i + 1) * 64,
                              pc * 128 + i * 64:pc * 128 + (i + 1) * 64],
                        vb16[i * 64:(i + 1) * 64, pc * QT:(pc + 1) * QT],
                        start=True, stop=True)
                p.op("t", am, reads=("attnb", "vb16"), writes=(f"pm{pc % 2}_{i}",))
            p.op("v", lambda e, pc=pc: e.tensor_copy(
                kv[:, pc * QT:(pc + 1) * QT], pm1[pc % 2][:]),
                reads=(f"pm{pc % 2}_0", f"pm{pc % 2}_1"), writes=("kv",))
        p.op("v", lambda e: e.tensor_copy(ktb[:], kv[:]), reads=("kv",), writes=("ktb",))
        for dd in range(PC):
            def om(e, dd=dd):
                for k in range(PC):
                    mm = e.matmul(pm1[dd % 2][:],
                                  wo[:, k * D + dd * 128:k * D + (dd + 1) * 128],
                                  ktb[:, k * QT:(k + 1) * QT],
                                  start=(k == 0), stop=(k == PC - 1))
                return mm
            p.op("t", om, reads=("wo", "ktb"), writes=(f"pm{dd % 2}",))
            p.op("v", lambda e, dd=dd: e.tensor_tensor(
                vsb[:, dd * QT:(dd + 1) * QT], pm1[dd % 2][:],
                xs[:, dd * QT:(dd + 1) * QT], ALU.add),
                reads=(f"pm{dd % 2}", "xs"), writes=("vsb",))
        emit_postln(p, L, vsb, "vsb", xs[:], "xs")
        p.op("s", lambda e, c=c: e.dma_start(
            out=yd[:, c * QT:(c + 1) * QT].rearrange("(pc p) t -> p pc t", p=128),
            in_=xs[:]), reads=("xs",), writes=("ydone",), dma=True, group="st")




def build_l2a(T, QT=512):
    nc = bass.Bass("TRN2", target_bir_lowering=False)
    io = {}
    io["x"] = nc.dram_tensor("x", [D, T], F32, kind="ExternalInput").ap()
    io["gam_pm"] = nc.dram_tensor("gam_pm", [128, PC], F32, kind="ExternalInput").ap()
    io["wqkv"] = nc.dram_tensor("wqkv", [D, 3 * DI], F32, kind="ExternalInput").ap()
    io["wout"] = nc.dram_tensor("wout", [DI, D], F32, kind="ExternalInput").ap()
    io["temp16"] = nc.dram_tensor("temp16", [1, H], F32, kind="ExternalInput").ap()
    io["hm16"] = nc.dram_tensor("hm16", [H, PC], F32, kind="ExternalInput").ap()
    io["h64"] = nc.dram_tensor("h64", [H, 128], F32, kind="ExternalInput").ap()
    io["ident"] = nc.dram_tensor("ident", [128, 128], F32, kind="ExternalInput").ap()
    io["y"] = nc.dram_tensor("y", [D, T], F32, kind="ExternalOutput").ap()
    p = Prog()
    emit_l2a(nc, p, io, T, QT)
    p.op("y", lambda e: e.nop(), reads=("ydone",))
    p.emit(nc)
    return nc
def emit_l0a(nc, p, io, T, tag=""):
    QT = 128
    NCH = T // QT
    NH, HD, DS = 32, 64, 128
    DIP = 4384
    DIN = 2048
    CC = 18
    xd = io["x"]
    wipd = io["wipT"]
    wopd = io["wopT"]
    cwd = io["cw_pm"]
    cbd = io["cb_pm"]
    dtbd = io["dtb_col"]
    ald = io["alog_row"]
    dpd = io["d_pm"]
    nwd = io["nw_pm"]
    mkd = io["maskS"]
    sbd = io["selbig"]
    sod = io["selone"]
    idd = io["ident"]
    yd = io["y"]

    wip = nc.alloc_sbuf_tensor(tag + "wip", [128, PC * DIP], BF16)
    wop = nc.alloc_sbuf_tensor(tag + "wop", [128, 16 * D], BF16)
    cw = nc.alloc_sbuf_tensor(tag + "cw", [128, CC * 4], F32)
    cb = nc.alloc_sbuf_tensor(tag + "cb", [128, CC], F32)
    dtb = nc.alloc_sbuf_tensor(tag + "dtb", [NH, 1], F32)
    alr = nc.alloc_sbuf_tensor(tag + "alr", [1, 96], F32)
    acol = nc.alloc_sbuf_tensor(tag + "acol", [NH, 1], F32)
    dpm = nc.alloc_sbuf_tensor(tag + "dpm", [128, 16], F32)
    nwm = nc.alloc_sbuf_tensor(tag + "nwm", [128, 16], F32)
    mks = nc.alloc_sbuf_tensor(tag + "mks", [128, 128], BF16)
    selb = nc.alloc_sbuf_tensor(tag + "selb", [NH, 16 * 128], BF16)
    ebf = nc.alloc_sbuf_tensor(tag + "ebf", [NH, 128], BF16)
    selo = nc.alloc_sbuf_tensor(tag + "selo", [NH, NH * 128], F32)
    ids = nc.alloc_sbuf_tensor(tag + "ids", [128, 128], F32)
    idb = nc.alloc_sbuf_tensor(tag + "idb", [128, 128], BF16)
    onecell = nc.alloc_sbuf_tensor(tag + "onecell", [1, 1], F32)
    onec32 = nc.alloc_sbuf_tensor(tag + "onec32", [32, 1], F32)
    xs = nc.alloc_sbuf_tensor(tag + "xs", [128, PC * QT], F32)
    xb = nc.alloc_sbuf_tensor(tag + "xb", [128, PC * QT], BF16)
    zs = nc.alloc_sbuf_tensor(tag + "zs", [128, 16 * QT], F32)
    xbce = nc.alloc_sbuf_tensor(tag + "xbce", [128, CC * (QT + 3)], F32)
    xbcs = nc.alloc_sbuf_tensor(tag + "xbcs", [128, CC * QT], BF16)
    convt = nc.alloc_sbuf_tensor(tag + "convt", [128, QT], F32)
    dtraw = nc.alloc_sbuf_tensor(tag + "dtraw", [NH, QT], F32)
    dtt = nc.alloc_sbuf_tensor(tag + "dtt", [NH, QT], F32)
    dts = nc.alloc_sbuf_tensor(tag + "dts", [NH, QT], F32)
    La = nc.alloc_sbuf_tensor(tag + "La", [NH, QT], F32)
    eLa = nc.alloc_sbuf_tensor(tag + "eLa", [NH, QT], F32)
    eEnd = nc.alloc_sbuf_tensor(tag + "eEnd", [NH, QT], F32)
    LaT = nc.alloc_sbuf_tensor(tag + "LaT", [128, NH], F32)
    dteT = nc.alloc_sbuf_tensor(tag + "dteT", [128, NH], F32)
    dtT = nc.alloc_sbuf_tensor(tag + "dtT", [128, NH], F32)
    E32 = nc.alloc_sbuf_tensor(tag + "E32", [128, NH], F32)
    e32r = alr[0:1, 0:NH]
    XTb = nc.alloc_sbuf_tensor(tag + "XTb", [128, 16 * QT], BF16)
    BTb = nc.alloc_sbuf_tensor(tag + "BTb", [128, QT], BF16)
    CBTm = nc.alloc_sbuf_tensor(tag + "CBTm", [128, QT], BF16)
    Esb = nc.alloc_sbuf_tensor(tag + "Esb", [128, 16 * QT], F32)
    ysb = nc.alloc_sbuf_tensor(tag + "ysb", [128, 16 * QT], F32)
    Wg = nc.alloc_sbuf_tensor(tag + "Wg", [128, 16 * QT], BF16)
    hst = nc.alloc_sbuf_tensor(tag + "hst", [128, NH * HD], F32)
    hbf = nc.alloc_sbuf_tensor(tag + "hbf", [128, NH * HD], BF16)
    xds = nc.alloc_sbuf_tensor(tag + "xds", [128, NH * HD], BF16)
    silz = nc.alloc_sbuf_tensor(tag + "silz", [128, 16 * QT], F32)
    pm1 = [nc.alloc_psum_tensor(tag + f"pml{i}", [128, QT], F32) for i in range(2)]
    psW = nc.alloc_psum_tensor(tag + "psW", [128, 16 * QT], F32)
    L = LNT(nc, QT, tag=tag, ext_vsq=silz[:, 0:PC * QT], ext_vsq_key="silz")
    vsb = silz[:, PC * QT:2 * PC * QT]
    rmsr = L.rv
    rmsb = L.mb


    p = p
    p.op("g", lambda e: e.dma_start(
        out=wip[:].rearrange("p (pc q) -> p pc q", pc=PC),
        in_=wipd.rearrange("(pc p) q -> p pc q", p=128)), writes=("wip",), dma=True)
    for hf in range(2):
        p.op("g", lambda e, hf=hf: e.dma_start(
            out=wop[:, hf * 8 * D:(hf + 1) * 8 * D]
            .rearrange("p (kk d) -> p kk d", kk=8),
            in_=wopd[hf * 8 * 128:(hf + 1) * 8 * 128, :]
            .rearrange("(kk p) d -> p kk d", p=128)), writes=("wop",), dma=True)
    for nm, dst, srcap in (("cw", cw, cwd), ("cb", cb, cbd), ("dtb", dtb, dtbd),
                           ("alr", alr, ald), ("dpm", dpm, dpd), ("nwm", nwm, nwd),
                           ("selo", selo, sod),
                           ("ids", ids, idd)):
        p.op("y", lambda e, dst=dst, srcap=srcap: e.dma_start(out=dst[:], in_=srcap[:]),
             writes=(nm,), dma=True)
    p.op("g", lambda e: e.dma_start(out=mks[:], in_=mkd[:]), writes=("mks",), dma=True)
    p.op("g", lambda e: e.dma_start(out=selb[:], in_=sbd[:]), writes=("selb",), dma=True)
    L.init_ops(p)
    p.op("v", lambda e: e.memset(onecell[:], 1.0), writes=("onecell",))
    p.op("v", lambda e: e.memset(onec32[:], 1.0), writes=("onec32",))
    p.op("v", lambda e: e.memset(xbce[:], 0.0), writes=("xbce",))
    p.op("v", lambda e: e.memset(hst[:], 0.0), writes=("hst",))
    p.op("v", lambda e: e.tensor_copy(idb[:], ids[:]), reads=("ids",), writes=("idb",))
    p.op("s", lambda e: e.activation(alr[:], alr[:], AF.Exp),
         reads=("alr",), writes=("alr",))
    p.op("v", lambda e: e.tensor_add(alr[0:1, 0:32], alr[0:1, 0:32], alr[0:1, 32:64]),
         reads=("alr",), writes=("alr",))
    p.op("v", lambda e: e.tensor_add(alr[0:1, 0:32], alr[0:1, 0:32], alr[0:1, 64:96]),
         reads=("alr",), writes=("alr",))
    p.op("v", lambda e: e.tensor_scalar_mul(alr[0:1, 0:32], alr[0:1, 0:32], -1.0 / 3.0),
         reads=("alr",), writes=("alr",))
    p.op("t", lambda e: e.transpose(pm1[0][0:32, 0:1], alr[0:1, 0:32], ids[0:1, 0:1]),
         reads=("alr", "ids"), writes=("pm0",))
    p.op("v", lambda e: e.tensor_copy(acol[:], pm1[0][0:32, 0:1]),
         reads=("pm0",), writes=("acol",))
    for kk in range(16):
        p.op("v", lambda e, kk=kk: e.tensor_scalar(
            wop[:, kk * D:(kk + 1) * D], wop[:, kk * D:(kk + 1) * D],
            nwm[:, kk:kk + 1], None, ALU.mult), reads=("wop", "nwm"), writes=("wop",))

    for c in range(NCH):
        p.op("y", lambda e, c=c: e.dma_start(
            out=xs[:], in_=xd[:, c * QT:(c + 1) * QT]
            .rearrange("(pc p) t -> p pc t", p=128)), writes=("xs",), dma=True,
            group="ld")
        p.op("v", lambda e: e.tensor_copy(xb[:], xs[:]), reads=("xs",), writes=("xb",))
        p.op("v", lambda e: e.tensor_copy(
            xbce[:].rearrange("p (cc t) -> p cc t", cc=CC)[:, :, QT:QT + 3],
            xbce[:].rearrange("p (cc t) -> p cc t", cc=CC)[:, :, 0:3]) if False
            else e.tensor_copy(
            xbce[:].rearrange("p (cc t) -> p cc t", cc=CC)[:, :, 0:3],
            xbce[:].rearrange("p (cc t) -> p cc t", cc=CC)[:, :, QT:QT + 3]),
            reads=("xbce",), writes=("xbce",))
        for d in range(35):
            dw = 128 if d < 34 else 32
            def ipm(e, d=d, dw=dw):
                for k in range(PC):
                    mm = e.matmul(pm1[d % 2][0:dw, :],
                                  wip[:, k * DIP + d * 128:k * DIP + d * 128 + dw],
                                  xb[:, k * QT:(k + 1) * QT],
                                  start=(k == 0), stop=(k == PC - 1))
                return mm
            p.op("t", ipm, reads=("wip", "xb"), writes=(f"pm{d % 2}",))
            if d < 16:
                p.op("v", lambda e, d=d: e.tensor_copy(
                    zs[:, d * QT:(d + 1) * QT], pm1[d % 2][:]),
                    reads=(f"pm{d % 2}",), writes=("zs",))
            elif d < 34:
                cc = d - 16
                p.op("v", lambda e, d=d, cc=cc: e.tensor_copy(
                    xbce[:, cc * (QT + 3) + 3:cc * (QT + 3) + 3 + QT], pm1[d % 2][:]),
                    reads=(f"pm{d % 2}",), writes=("xbce",))
            else:
                p.op("v", lambda e, d=d: e.tensor_copy(
                    dtraw[:], pm1[d % 2][0:32, :]), reads=(f"pm{d % 2}",),
                    writes=("dtraw",))
        for cc in range(CC):
            p.op("v", lambda e, cc=cc: e.tensor_scalar(
                convt[:], xbce[:, cc * (QT + 3):cc * (QT + 3) + QT],
                cw[:, cc * 4:cc * 4 + 1], None, ALU.mult),
                reads=("xbce", "cw"), writes=("convt",))
            for kk in range(1, 4):
                p.op("v", lambda e, cc=cc, kk=kk: e.scalar_tensor_tensor(
                    convt[:], xbce[:, cc * (QT + 3) + kk:cc * (QT + 3) + kk + QT],
                    cw[:, cc * 4 + kk:cc * 4 + kk + 1], convt[:], ALU.mult, ALU.add),
                    reads=("xbce", "cw", "convt"), writes=("convt",))
            p.op("s", lambda e, cc=cc: e.activation(
                xbcs[:, cc * QT:(cc + 1) * QT], convt[:], AF.Silu,
                bias=cb[:, cc:cc + 1]), reads=("convt", "cb"), writes=("xbcs",))
        p.op("s", lambda e: e.activation(dtt[:], dtraw[:], AF.Abs, bias=dtb[:]),
             reads=("dtraw", "dtb"), writes=("dtt",))
        p.op("s", lambda e: e.activation(dtt[:], dtt[:], AF.Exp, scale=-1.0),
             reads=("dtt",), writes=("dtt",))
        p.op("s", lambda e: e.activation(dtt[:], dtt[:], AF.Ln, bias=onec32[:]),
             reads=("dtt", "onec32"), writes=("dtt",))
        p.op("s", lambda e: e.activation(dts[:], dtraw[:], AF.Relu, bias=dtb[:]),
             reads=("dtraw", "dtb"), writes=("dts",))
        p.op("v", lambda e: e.tensor_add(dts[:], dts[:], dtt[:]),
             reads=("dts", "dtt"), writes=("dts",))
        p.op("v", lambda e: e.tensor_scalar(dtt[:], dts[:], acol[:], None, ALU.mult),
             reads=("dts", "acol"), writes=("dtt",))
        p.op("v", lambda e: e.tensor_tensor_scan(La[:], dtt[:], dtt[:], 0.0,
                                                 ALU.add, ALU.bypass),
             reads=("dtt",), writes=("La",))
        p.op("s", lambda e: e.activation(eLa[:], La[:], AF.Exp),
             reads=("La",), writes=("eLa",))
        p.op("v", lambda e: e.tensor_scalar(eEnd[:], La[:], La[:, QT - 1:QT], None,
                                            ALU.subtract),
             reads=("La",), writes=("eEnd",))
        p.op("s", lambda e: e.activation(eEnd[:], eEnd[:], AF.Exp, scale=-1.0),
             reads=("eEnd",), writes=("eEnd",))
        p.op("v", lambda e: e.tensor_mul(eEnd[:], eEnd[:], dts[:]),
             reads=("eEnd", "dts"), writes=("eEnd",))
        p.op("t", lambda e: e.transpose(pm1[0][:, 0:NH], La[:], ids[0:32, 0:32]),
             reads=("La", "ids"), writes=("pm0",))
        p.op("v", lambda e: e.tensor_copy(LaT[:], pm1[0][:, 0:NH]),
             reads=("pm0",), writes=("LaT",))
        p.op("t", lambda e: e.transpose(pm1[1][:, 0:NH], eEnd[:], ids[0:32, 0:32]),
             reads=("eEnd", "ids"), writes=("pm1",))
        p.op("v", lambda e: e.tensor_copy(dteT[:], pm1[1][:, 0:NH]),
             reads=("pm1",), writes=("dteT",))
        p.op("t", lambda e: e.transpose(pm1[0][:, 0:NH], dts[:], ids[0:32, 0:32]),
             reads=("dts", "ids"), writes=("pm0",))
        p.op("v", lambda e: e.tensor_copy(dtT[:], pm1[0][:, 0:NH]),
             reads=("pm0",), writes=("dtT",))
        p.op("t", lambda e: e.matmul(pm1[0][:], xbcs[:, 16 * QT:17 * QT],
                                     xbcs[:, 17 * QT:18 * QT], start=True, stop=True),
             reads=("xbcs",), writes=("pm0",))
        p.op("v", lambda e: e.tensor_tensor(CBTm[:], pm1[0][:], mks[:], ALU.mult),
             reads=("pm0", "mks"), writes=("CBTm",))
        for j in range(16):
            p.op("t", lambda e, j=j: e.transpose(
                pm1[j % 2][:].bitcast(BF16)[:, 0:128],
                xbcs[:, j * QT:(j + 1) * QT], idb[:]),
                reads=("xbcs", "idb"), writes=(f"pm{j % 2}",))
            p.op("v", lambda e, j=j: e.tensor_copy(
                XTb[:, j * QT:(j + 1) * QT], pm1[j % 2][:].bitcast(BF16)[:, 0:128]),
                reads=(f"pm{j % 2}",), writes=("XTb",))
        p.op("v", lambda e: e.tensor_copy(hbf[:], hst[:]), reads=("hst",), writes=("hbf",))
        p.op("v", lambda e: e.tensor_copy(ebf[:], eLa[:]), reads=("eLa",), writes=("ebf",))

        def emm(e):
            for pc2 in range(16):
                mm = e.matmul(psW[:, pc2 * QT:(pc2 + 1) * QT],
                              selb[:, pc2 * 128:(pc2 + 1) * 128], ebf[:],
                              start=True, stop=True)
            return mm
        p.op("t", emm, reads=("selb", "ebf"), writes=("psW",))
        p.op("s", lambda e: e.activation(Esb[:], psW[:], AF.Copy),
             reads=("psW",), writes=("Esb",))
        for gq in range(2):
            def lrm(e, gq=gq):
                for i in range(16):
                    h = gq * 16 + i
                    mm = e.matmul(psW[:, i * QT:(i + 1) * QT],
                                  selo[:, h * 128:(h + 1) * 128], La[:],
                                  start=True, stop=True)
                return mm
            p.op("t", lrm, reads=("selo", "La"), writes=("psW",))
            p.op("v", lambda e, gq=gq: e.tensor_tensor(
                silz[:].rearrange("p (g t) -> p g t", g=16),
                psW[:].rearrange("p (g t) -> p g t", g=16),
                LaT[:, gq * 16:(gq + 1) * 16].unsqueeze(2).broadcast_to((128, 16, QT)),
                ALU.min), reads=("psW", "LaT"), writes=("silz",))
            p.op("v", lambda e, gq=gq: e.tensor_tensor(
                silz[:].rearrange("p (g t) -> p g t", g=16),
                silz[:].rearrange("p (g t) -> p g t", g=16),
                LaT[:, gq * 16:(gq + 1) * 16].unsqueeze(2).broadcast_to((128, 16, QT)),
                ALU.subtract), reads=("silz", "LaT"), writes=("silz",))
            p.op("s", lambda e: e.activation(Wg[:], silz[:], AF.Exp),
                 reads=("silz",), writes=("Wg",))
            p.op("v", lambda e, gq=gq: e.tensor_tensor(
                Wg[:].rearrange("p (g t) -> p g t", g=16),
                Wg[:].rearrange("p (g t) -> p g t", g=16),
                dtT[:, gq * 16:(gq + 1) * 16].unsqueeze(2).broadcast_to((128, 16, QT)),
                ALU.mult), reads=("Wg", "dtT"), writes=("Wg",))
            p.op("v", lambda e: e.tensor_tensor(
                Wg[:].rearrange("p (g t) -> p g t", g=16),
                Wg[:].rearrange("p (g t) -> p g t", g=16),
                CBTm[:].unsqueeze(1).broadcast_to((128, 16, QT)), ALU.mult),
                reads=("Wg", "CBTm"), writes=("Wg",))
            def ymm(e, gq=gq):
                for i in range(16):
                    h = gq * 16 + i
                    sl = psW[(h % 2) * 64:(h % 2) * 64 + 64,
                             (h // 2) * QT:(h // 2) * QT + QT]
                    mm = e.matmul(sl, XTb[:, h * 64:(h + 1) * 64],
                                  Wg[:, i * QT:(i + 1) * QT], start=True, stop=True)
                return mm
            p.op("t", ymm, reads=("XTb", "Wg", "ysbF" if gq == 0 else "ysbG"),
                 writes=("psW",))
            if gq == 0:
                p.op("v", lambda e: e.tensor_copy(
                    ysb[:, 0:8 * QT], psW[:, 0:8 * QT]),
                    reads=("psW",), writes=("ysbG",))
            else:
                p.op("v", lambda e: e.tensor_copy(
                    ysb[:, 8 * QT:16 * QT], psW[:, 8 * QT:16 * QT]),
                    reads=("psW",), writes=("ysbG",))

        def yim(e):
            for h in range(NH):
                sl = psW[(h % 2) * 64:(h % 2) * 64 + 64,
                         (h // 2) * QT:(h // 2) * QT + QT]
                mm = e.matmul(sl, hbf[:, h * 64:(h + 1) * 64],
                              xbcs[:, 17 * QT:18 * QT], start=True, stop=True)
            return mm
        p.op("t", yim, reads=("hbf", "xbcs", "ysbG"), writes=("psW",))
        p.op("v", lambda e: e.tensor_mul(silz[:], psW[:], Esb[:]),
             reads=("psW", "Esb"), writes=("silz",))
        p.op("v", lambda e: e.tensor_add(ysb[:], ysb[:], silz[:]),
             reads=("ysbG", "silz"), writes=("ysbF",))
        p.op("v", lambda e: e.tensor_tensor(
            silz[:].rearrange("p (g t) -> p g t", g=16),
            xbcs[:, 0:16 * QT].rearrange("p (g t) -> p g t", g=16),
            dpm[:].unsqueeze(2).broadcast_to((128, 16, QT)), ALU.mult),
            reads=("xbcs", "dpm"), writes=("silz",))
        p.op("v", lambda e: e.tensor_add(ysb[:], ysb[:], silz[:]),
             reads=("ysbF", "silz"), writes=("ysbF",))
        p.op("t", lambda e: e.transpose(pm1[0][:].bitcast(BF16)[:, 0:128],
                                        xbcs[:, 16 * QT:17 * QT], idb[:]),
             reads=("xbcs", "idb"), writes=("pm0",))
        p.op("v", lambda e: e.tensor_copy(BTb[:], pm1[0][:].bitcast(BF16)[:, 0:128]),
             reads=("pm0",), writes=("BTb",))
        p.op("v", lambda e: e.tensor_tensor(
            xds[:].rearrange("p (h q) -> p h q", h=NH),
            XTb[:].rearrange("p (h q) -> p h q", h=NH),
            dteT[:].unsqueeze(2).broadcast_to((128, NH, HD)), ALU.mult),
            reads=("XTb", "dteT"), writes=("xds",))

        def dhm(e):
            for j in range(4):
                mm = e.matmul(psW[:, j * 512:(j + 1) * 512], BTb[:],
                              xds[:, j * 512:(j + 1) * 512], start=True, stop=True)
            return mm
        p.op("t", dhm, reads=("BTb", "xds", "ysbF"), writes=("psW",))
        p.op("t", lambda e: e.transpose(pm1[0][0:1, 0:NH], eLa[:, QT - 1:QT],
                                        ids[0:32, 0:32]),
             reads=("eLa", "ids"), writes=("pm0",))
        p.op("v", lambda e: e.tensor_copy(e32r, pm1[0][0:1, 0:NH]),
             reads=("pm0",), writes=("alr",))
        p.op("t", lambda e: e.matmul(pm1[1][:, 0:NH], L.onesr[:], e32r,
                                     start=True, stop=True),
             reads=("alr", "ln_onesr"), writes=("pm1",))
        p.op("v", lambda e: e.tensor_copy(E32[:], pm1[1][:, 0:NH]),
             reads=("pm1",), writes=("E32",))
        p.op("v", lambda e: e.tensor_tensor(
            hst[:].rearrange("n (h q) -> n h q", h=NH),
            hst[:].rearrange("n (h q) -> n h q", h=NH),
            E32[:].unsqueeze(2).broadcast_to((128, NH, HD)), ALU.mult),
            reads=("hst", "E32", "hbf"), writes=("hst",))
        p.op("v", lambda e: e.tensor_add(hst[:], hst[:], psW[:]),
             reads=("hst", "psW"), writes=("hst",))
        p.op("s", lambda e: e.activation(silz[:], zs[:], AF.Silu),
             reads=("zs",), writes=("silz",))
        p.op("s", lambda e: e.activation(Esb[:], ysb[:], AF.Square),
             reads=("ysbF",), writes=("Esb",))

        def rmm(e):
            for k in range(16):
                mm = e.matmul(L.psr[:, 0:QT], L.onesc[:], Esb[:, k * QT:(k + 1) * QT],
                              start=(k == 0), stop=(k == 15))
            return mm
        p.op("t", rmm, reads=("Esb", "ln_onesc"), writes=("ln_psr",))
        p.op("v", lambda e: e.tensor_copy(rmsr[:], L.psr[0:1, 0:QT]),
             reads=("ln_psr",), writes=("ln_rv",))
        p.op("v", lambda e: e.tensor_scalar_mul(rmsr[:], rmsr[:], 1.0 / DIN),
             reads=("ln_rv",), writes=("ln_rv",))
        p.op("s", lambda e: e.activation(rmsr[:], rmsr[:], AF.Sqrt,
                                         bias=L.epsb[0:1, 0:1]),
             reads=("ln_rv", "ln_epsb"), writes=("ln_rv",))
        p.op("v", lambda e: e.reciprocal(rmsr[:], rmsr[:]),
             reads=("ln_rv",), writes=("ln_rv",))
        p.op("t", lambda e: e.matmul(L.psb2[:, 0:QT], L.onesr[:], rmsr[:],
                                     start=True, stop=True),
             reads=("ln_rv", "ln_onesr"), writes=("ln_psb2",))
        p.op("v", lambda e: e.tensor_copy(rmsb[:], L.psb2[:, 0:QT]),
             reads=("ln_psb2",), writes=("ln_mb",))
        p.op("v", lambda e: e.tensor_tensor(
            ysb[:].rearrange("p (g t) -> p g t", g=16),
            ysb[:].rearrange("p (g t) -> p g t", g=16),
            rmsb[:].unsqueeze(1).broadcast_to((128, 16, QT)), ALU.mult),
            reads=("ysbF", "ln_mb"), writes=("ysbF",))
        p.op("v", lambda e: e.tensor_mul(xds[:, 0:16 * QT], ysb[:], silz[:]),
             reads=("ysbF", "silz"), writes=("xds",))
        for dd in range(PC):
            def opm(e, dd=dd):
                for k in range(16):
                    mm = e.matmul(pm1[dd % 2][:],
                                  wop[:, k * D + dd * 128:k * D + (dd + 1) * 128],
                                  xds[:, k * QT:(k + 1) * QT],
                                  start=(k == 0), stop=(k == 15))
                return mm
            p.op("t", opm, reads=("wop", "xds"), writes=(f"pm{dd % 2}",))
            p.op("v", lambda e, dd=dd: e.tensor_tensor(
                vsb[:, dd * QT:(dd + 1) * QT], pm1[dd % 2][:],
                xs[:, dd * QT:(dd + 1) * QT], ALU.add),
                reads=(f"pm{dd % 2}", "xs"), writes=("silz",))
        emit_postln(p, L, vsb, "silz", xs[:], "xs")
        p.op("s", lambda e, c=c: e.dma_start(
            out=yd[:, c * QT:(c + 1) * QT].rearrange("(pc p) t -> p pc t", p=128),
            in_=xs[:]), reads=("xs",), writes=("ydone",), dma=True, group="st")



def build_l0a(T):
    nc = bass.Bass("TRN2", target_bir_lowering=False, dynamic_dma_scratch_size=8192)
    io = {}
    io["x"] = nc.dram_tensor("x", [D, T], F32, kind="ExternalInput").ap()
    io["wipT"] = nc.dram_tensor("wipT", [D, DIP], F32, kind="ExternalInput").ap()
    io["wopT"] = nc.dram_tensor("wopT", [DIN, D], F32, kind="ExternalInput").ap()
    io["cw_pm"] = nc.dram_tensor("cw_pm", [128, CC * 4], F32, kind="ExternalInput").ap()
    io["cb_pm"] = nc.dram_tensor("cb_pm", [128, CC], F32, kind="ExternalInput").ap()
    io["dtb_col"] = nc.dram_tensor("dtb_col", [NH, 1], F32, kind="ExternalInput").ap()
    io["alog_row"] = nc.dram_tensor("alog_row", [1, 96], F32, kind="ExternalInput").ap()
    io["d_pm"] = nc.dram_tensor("d_pm", [128, 16], F32, kind="ExternalInput").ap()
    io["nw_pm"] = nc.dram_tensor("nw_pm", [128, 16], F32, kind="ExternalInput").ap()
    io["maskS"] = nc.dram_tensor("maskS", [128, 128], F32, kind="ExternalInput").ap()
    io["selbig"] = nc.dram_tensor("selbig", [NH, 16 * 128], F32, kind="ExternalInput").ap()
    io["selone"] = nc.dram_tensor("selone", [NH, NH * 128], F32, kind="ExternalInput").ap()
    io["ident"] = nc.dram_tensor("ident", [128, 128], F32, kind="ExternalInput").ap()
    io["y"] = nc.dram_tensor("y", [D, T], F32, kind="ExternalOutput").ap()
    p = Prog()
    emit_l0a(nc, p, io, T)
    p.op("y", lambda e: e.nop(), reads=("ydone",))
    p.emit(nc)
    return nc
def prep_l23f(x_fm, w_gate, w_up, w_down):
    return {"x": x_fm,
            "wgT": np.ascontiguousarray(w_gate.T),
            "wuT": np.ascontiguousarray(w_up.T),
            "wdT": np.ascontiguousarray(w_down.T)}


BLOCK_DECLS = {'build_l23f': [('x', None, 'F32', 'ExternalInput'), ('wgT', (1024, 3072), 'I32', 'ExternalInput'), ('wuT', (1024, 3072), 'I32', 'ExternalInput'), ('wdT', (3072, 1024), 'I32', 'ExternalInput'), ('y', None, 'F32', 'ExternalOutput')], 'build_l3a': [('x', None, 'F32', 'ExternalInput'), ('w_in', (8, 128, 128), 'F32', 'ExternalInput'), ('w_a', (8, 128, 128), 'F32', 'ExternalInput'), ('ap_pm', (128, 8), 'F32', 'ExternalInput'), ('y', None, 'F32', 'ExternalOutput')], 'build_l1a': [('x', None, 'F32', 'ExternalInput'), ('q_pm', (128, 8), 'F32', 'ExternalInput'), ('k_pm', (128, 8), 'F32', 'ExternalInput'), ('nw_pm', (128, 8), 'F32', 'ExternalInput'), ('ld_row', (1, 16), 'F32', 'ExternalInput'), ('owT', (1024, 1024), 'F32', 'ExternalInput'), ('halves', (128, 2), 'F32', 'ExternalInput'), ('hm16', (16, 8), 'F32', 'ExternalInput'), ('h64', (16, 128), 'F32', 'ExternalInput'), ('ident', (128, 128), 'F32', 'ExternalInput'), ('y', None, 'F32', 'ExternalOutput')], 'build_l0f': [('x', None, 'F32', 'ExternalInput'), ('w', (1024, 1024), 'F32', 'ExternalInput'), ('bias_pm', (128, 8), 'F32', 'ExternalInput'), ('ident', (128, 128), 'F32', 'ExternalInput'), ('y', None, 'F32', 'ExternalOutput')], 'build_l1f': [('x', None, 'F32', 'ExternalInput'), ('nw1_pm', (128, 8), 'F32', 'ExternalInput'), ('nw2_pm', (128, 32), 'F32', 'ExternalInput'), ('w1T', (1024, 4096), 'F32', 'ExternalInput'), ('w2T', (4096, 1024), 'F32', 'ExternalInput'), ('acts', (1, 2), 'F32', 'ExternalInput'), ('ident', (128, 128), 'F32', 'ExternalInput'), ('y', None, 'F32', 'ExternalOutput')], 'build_l2a': [('x', None, 'F32', 'ExternalInput'), ('gam_pm', (128, 8), 'F32', 'ExternalInput'), ('wqkv', (1024, 3072), 'F32', 'ExternalInput'), ('wout', (1024, 1024), 'F32', 'ExternalInput'), ('temp16', (1, 16), 'F32', 'ExternalInput'), ('hm16', (16, 8), 'F32', 'ExternalInput'), ('h64', (16, 128), 'F32', 'ExternalInput'), ('ident', (128, 128), 'F32', 'ExternalInput'), ('y', None, 'F32', 'ExternalOutput')], 'build_l0a': [('x', None, 'F32', 'ExternalInput'), ('wipT', (1024, 4384), 'F32', 'ExternalInput'), ('wopT', (2048, 1024), 'F32', 'ExternalInput'), ('cw_pm', (128, 72), 'F32', 'ExternalInput'), ('cb_pm', (128, 18), 'F32', 'ExternalInput'), ('dtb_col', (32, 1), 'F32', 'ExternalInput'), ('alog_row', (1, 96), 'F32', 'ExternalInput'), ('d_pm', (128, 16), 'F32', 'ExternalInput'), ('nw_pm', (128, 16), 'F32', 'ExternalInput'), ('maskS', (128, 128), 'F32', 'ExternalInput'), ('selbig', (32, 2048), 'F32', 'ExternalInput'), ('selone', (32, 4096), 'F32', 'ExternalInput'), ('ident', (128, 128), 'F32', 'ExternalInput'), ('y', None, 'F32', 'ExternalOutput')]}


_DT = {"F32": F32, "BF16": BF16, "I32": I32}
FUSED_BLOCKS = [("l0a", "build_l0a"), ("l0f", "build_l0f"), ("l1a", "build_l1a"),
                ("l1f", "build_l1f"), ("l2a", "build_l2a"), ("l2f", "build_l23f"),
                ("l3a", "build_l3a"), ("l3f", "build_l23f")]


def build_fused(T, upto=None, reset=True):
    nc = bass.Bass("TRN2", target_bir_lowering=False)
    p = Prog()
    xd = nc.dram_tensor("x", [D, T], F32, kind="ExternalInput").ap()
    yd = nc.dram_tensor("y", [D, T], F32, kind="ExternalOutput").ap()
    xscr = ([nc.dram_tensor(f"xscr{i}", [D, T], F32, kind="Internal").ap()
             for i in range(2)] if (upto is None or upto > 1) else [None, None])
    emits = dict(build_l0a=emit_l0a, build_l0f=emit_l0f, build_l1a=emit_l1a,
                 build_l1f=emit_l1f, build_l2a=emit_l2a, build_l23f=emit_l23f,
                 build_l3a=emit_l3a)

    blocks = FUSED_BLOCKS if upto is None else FUSED_BLOCKS[:upto]
    for bi, (tag, bn) in enumerate(blocks):
        io = {}
        for (name, shape, dt, kind) in BLOCK_DECLS[bn]:
            if name == "x":
                io["x"] = xd if bi == 0 else xscr[bi % 2]
            elif name == "y":
                io["y"] = yd if bi == len(blocks) - 1 else xscr[(bi + 1) % 2]
            else:
                io[name] = nc.dram_tensor(tag + "_" + name, list(shape),
                                          _DT[dt], kind="ExternalInput").ap()
        saved_sb = nc.sbuf_base
        saved_ps = nc.psum_base
        emits[bn](nc, p, io, T, tag=tag + "_")
        if reset:
            nc.sbuf_base = saved_sb
            nc.psum_base = saved_ps
        p.barrier_all()
    p.op("y", lambda e: e.nop(), reads=("ydone",))
    p.emit(nc)
    return nc


def fused_in_map(g, xarr):
    NHl = 32
    halves, hm16, h64, selb, selo, mask, ident = _consts()
    cw = np.asarray(g["L0_attn_conv_w"], np.float32).reshape(2304, 4) \
        .reshape(18, 128, 4).transpose(1, 0, 2)
    m = {"x": xarr}
    m.update({
        "l0a_wipT": np.ascontiguousarray(np.asarray(g["L0_attn_in_proj_w"], np.float32).T),
        "l0a_wopT": np.ascontiguousarray(np.asarray(g["L0_attn_out_proj_w"], np.float32).T),
        "l0a_cw_pm": np.ascontiguousarray(cw.reshape(128, 72)),
        "l0a_cb_pm": np.ascontiguousarray(np.asarray(g["L0_attn_conv_b"], np.float32)
                                          .reshape(18, 128).T),
        "l0a_dtb_col": np.asarray(g["L0_attn_dt_bias"], np.float32).reshape(NHl, 1),
        "l0a_alog_row": np.asarray(g["L0_attn_A_log"], np.float32).reshape(1, 96),
        "l0a_d_pm": np.ascontiguousarray(
            np.repeat(np.asarray(g["L0_attn_D"], np.float32), 64).reshape(16, 128).T),
        "l0a_nw_pm": np.ascontiguousarray(
            np.asarray(g["L0_attn_norm_w"], np.float32).reshape(16, 128).T),
        "l0a_maskS": mask, "l0a_selbig": selb, "l0a_selone": selo, "l0a_ident": ident,
        "l0f_w": np.asarray(g["L0_ffn_weight"], np.float32),
        "l0f_bias_pm": pm(np.asarray(g["L0_ffn_bias"], np.float32)),
        "l0f_ident": ident,
        "l1a_q_pm": pm(np.asarray(g["L1_attn_q"], np.float32)),
        "l1a_k_pm": pm(np.asarray(g["L1_attn_k"], np.float32)),
        "l1a_nw_pm": pm(np.asarray(g["L1_attn_norm_weight"], np.float32)),
        "l1a_ld_row": np.asarray(g["L1_attn_log_decay"], np.float32)[None, :],
        "l1a_owT": np.ascontiguousarray(np.asarray(g["L1_attn_o_w"], np.float32).T),
        "l1a_halves": halves, "l1a_hm16": hm16, "l1a_h64": h64, "l1a_ident": ident,
        "l1f_nw1_pm": pm(np.asarray(g["L1_ffn_norm1_w"], np.float32)),
        "l1f_nw2_pm": np.ascontiguousarray(
            np.asarray(g["L1_ffn_norm2_w"], np.float32).reshape(32, 128).T),
        "l1f_w1T": np.ascontiguousarray(np.asarray(g["L1_ffn_w1"], np.float32).T),
        "l1f_w2T": np.ascontiguousarray(np.asarray(g["L1_ffn_w2"], np.float32).T),
        "l1f_acts": np.array([[np.asarray(g["L1_ffn_act_scale"]).ravel()[0],
                               np.asarray(g["L1_ffn_act_bias"]).ravel()[0]]], np.float32),
        "l1f_ident": ident,
        "l2a_gam_pm": pm(np.asarray(g["L2_attn_gamma"], np.float32)),
        "l2a_wqkv": np.asarray(g["L2_attn_w_qkv"], np.float32),
        "l2a_wout": np.asarray(g["L2_attn_w_out"], np.float32),
        "l2a_temp16": np.asarray(g["L2_attn_temperature"], np.float32).reshape(1, 16),
        "l2a_hm16": hm16, "l2a_h64": h64, "l2a_ident": ident,
        "l2f_wgT": np.ascontiguousarray(np.asarray(g["L2_ffn_w_gate"], np.int32).T),
        "l2f_wuT": np.ascontiguousarray(np.asarray(g["L2_ffn_w_up"], np.int32).T),
        "l2f_wdT": np.ascontiguousarray(np.asarray(g["L2_ffn_w_down"], np.int32).T),
        "l3a_w_in": np.asarray(g["L3_attn_w_in"], np.float32),
        "l3a_w_a": np.asarray(g["L3_attn_w_a"], np.float32),
        "l3a_ap_pm": pm(np.asarray(g["L3_attn_a_param"], np.float32)),
        "l3f_wgT": np.ascontiguousarray(np.asarray(g["L3_ffn_w_gate"], np.int32).T),
        "l3f_wuT": np.ascontiguousarray(np.asarray(g["L3_ffn_w_up"], np.int32).T),
        "l3f_wdT": np.ascontiguousarray(np.asarray(g["L3_ffn_w_down"], np.int32).T),
    })
    return m


def _consts():
    H = 16
    NH = 32
    halves = np.zeros((128, 2), np.float32)
    halves[:64, 0] = 1
    halves[64:, 1] = 1
    hm16 = np.zeros((H, 8), np.float32)
    for h in range(H):
        hm16[h, h // 2] = 1
    h64 = np.zeros((H, 128), np.float32)
    for h in range(H):
        h64[h, (h % 2) * 64:(h % 2 + 1) * 64] = 1
    selb = np.zeros((NH, 16 * 128), np.float32)
    for pc in range(16):
        for pp in range(128):
            selb[2 * pc + pp // 64, pc * 128 + pp] = 1
    selo = np.zeros((NH, NH * 128), np.float32)
    for h in range(NH):
        selo[h, h * 128:(h + 1) * 128] = 1
    mask = np.triu(np.ones((128, 128), np.float32))
    ident = np.eye(128, dtype=np.float32)
    return halves, hm16, h64, selb, selo, mask, ident


def kernel(**g):
    T = 8192
    nc = build_fused(T)
    base = fused_in_map(g, None)
    del base["x"]
    u = np.asarray(g["L0_attn_u"], dtype=np.float32)
    zbase = {k: np.zeros_like(v) for k, v in base.items()}
    zx = np.zeros((D, T), np.float32)
    in_maps = []
    for c in range(8):
        if c < 4:
            m = dict(base)
            m["x"] = fm(u[c])
        else:
            m = dict(zbase)
            m["x"] = zx
        in_maps.append(m)
    res = run_bass_kernel_spmd(nc, in_maps, core_ids=list(range(8)))
    out = np.stack([np.ascontiguousarray(np.asarray(res.results[b]["y"]).T)
                    for b in range(4)])
    return np.ascontiguousarray(out.astype(np.float32))


# version 06
# speedup vs baseline: 1.0300x; 1.0250x over previous
import contextlib
import numpy as np
import concourse.bass as bass
import concourse.mybir as mybir
from concourse.bass_utils import run_bass_kernel_spmd

F32 = mybir.dt.float32
BF16 = mybir.dt.bfloat16
I32 = mybir.dt.int32
AF = mybir.ActivationFunctionType
ALU = mybir.AluOpType
AX = mybir.AxisListType

D = 1024
PC = D // 128
LNEPS = 1e-5
HID = 3072
H2 = 4096
DIP = 4384
DIN = 2048
CC = 18
NH = 32
H = 16
DI = 1024
HD = 64
DS = 128


def pm(v):
    v = np.asarray(v)
    return np.ascontiguousarray(v.reshape(-1, 128).T)


def fm(x):
    return np.ascontiguousarray(np.asarray(x).T)


class Prog:

    ENGS = ("t", "v", "s", "g", "y", "sd", "gd", "yd")

    def __init__(self):
        self.ops = []
        self.last_w = {}
        self.readers = {}
        self.fence = None
        self.bar = {}

    def op(self, eng, fn, reads=(), writes=(), dma=False, group=None):
        if dma:
            eng = {"y": "yd", "g": "gd", "s": "sd"}[eng]
            if group is None:
                group = "dg_" + (writes[0] if writes else "anon")
        idx = len(self.ops)
        if self.fence is not None:
            reads = tuple(reads) + (self.fence,)
        deps = set(self.bar.values())
        for b in reads:
            w = self.last_w.get(b)
            if w is not None:
                deps.add(w)
        for b in writes:
            w = self.last_w.get(b)
            if w is not None:
                deps.add(w)
            deps.update(self.readers.get(b, ()))
        deps.discard(idx)
        self.ops.append(dict(eng=eng, fn=fn, reads=reads, writes=writes,
                             deps=deps, group=group if dma else None))
        for b in reads:
            self.readers.setdefault(b, []).append(idx)
        for b in writes:
            self.last_w[b] = idx
            self.readers[b] = []
        return idx

    def barrier_all(self):
        last = {}
        for i, o in enumerate(self.ops):
            last[o["eng"]] = i
        self.bar = last

    def emit(self, nc):
        import contextlib
        DMAQ = ("sd", "gd", "yd")
        needed = set()
        for o in self.ops:
            needed.update(o["deps"])
        ord_ = {}
        cnt = {e: 0 for e in self.ENGS if e not in DMAQ}
        dma_sem_ids = {}
        n_dma = 0
        groups = {}
        gmembers = {}
        for i, o in enumerate(self.ops):
            if o.get("group") is not None and o["eng"] in DMAQ:
                gmembers.setdefault(o["group"], []).append(i)
        for i, o in enumerate(self.ops):
            gname = o.get("group")
            if o["eng"] in DMAQ and gname is not None:
                if any(m in needed for m in gmembers[gname]):
                    if gname not in groups:
                        groups[gname] = n_dma
                        n_dma += 1
                    dma_sem_ids[i] = groups[gname]
                continue
            if i in needed:
                if o["eng"] in DMAQ:
                    dma_sem_ids[i] = n_dma
                    n_dma += 1
                else:
                    cnt[o["eng"]] += 1
                    ord_[i] = cnt[o["eng"]]
        import bisect
        per_eng = {e: [] for e in self.ENGS}
        for i, o in enumerate(self.ops):
            per_eng[o["eng"]].append(i)
        with contextlib.ExitStack() as es:
            sems = {e: es.enter_context(nc.semaphore("sem_" + e))
                    for e in self.ENGS if e not in DMAQ}
            dsems = [es.enter_context(nc.semaphore(f"dma{j}"))
                     for j in range(n_dma)]
            block = es.enter_context(nc.Block())
            emap = dict(t="tensor", v="vector", s="scalar", g="gpsimd",
                        y="sync", sd="scalar", gd="gpsimd", yd="sync")
            phys = {}
            for e in self.ENGS:
                phys.setdefault(emap[e], []).append(e)

            def mk(pname, eng_keys):
                def body(eng):
                    watermark = {}
                    seen_insts = {}
                    mine = sorted([i for e in eng_keys for i in per_eng[e]])
                    for i in mine:
                        o = self.ops[i]
                        targets = {}
                        for d in o["deps"]:
                            od = self.ops[d]
                            if od["eng"] in DMAQ:
                                gname = od.get("group")
                                if gname is not None:
                                    nmem = bisect.bisect_left(gmembers[gname], i)
                                    k = ("d", dma_sem_ids[d])
                                    targets[k] = max(targets.get(k, 0), 16 * nmem)
                                else:
                                    k = ("d", dma_sem_ids[d])
                                    targets[k] = max(targets.get(k, 0), 16)
                            else:
                                k = ("e", od["eng"])
                                targets[k] = max(targets.get(k, 0), ord_[d])
                        for k, val in sorted(targets.items()):
                            if watermark.get(k, 0) >= val:
                                continue
                            sem = dsems[k[1]] if k[0] == "d" else sems[k[1]]
                            eng.wait_ge(sem, val)
                            watermark[k] = val
                        inst = o["fn"](eng)
                        ik = id(inst)
                        assert not (ik in seen_insts and seen_insts[ik] is inst), \
                            f"op {i} returned an instruction already inc'd"
                        if i in dma_sem_ids:
                            inst.then_inc(dsems[dma_sem_ids[i]], 16)
                            seen_insts[ik] = inst
                        elif i in needed:
                            inst.then_inc(sems[o["eng"]], 1)
                            seen_insts[ik] = inst
                    return
                return body

            for pname, eng_keys in phys.items():
                getattr(block, pname)(mk(pname, eng_keys))
        return nc


class Emit:

    def __init__(self, nc, es, names):
        self.nc = nc
        self.sems = {}
        self.cnt = {}
        for n in names:
            self.sems[n] = es.enter_context(nc.semaphore(n))
            self.cnt[n] = 0

    def inc(self, inst, name, n=1):
        inst.then_inc(self.sems[name], n)
        self.cnt[name] += n

    def w(self, eng, name, val):
        eng.wait_ge(self.sems[name], val)

    def sw(self, eng, name):
        eng.wait_ge(self.sems[name], self.cnt[name])


def emit_rneq(p, src, skey, ii, ikey, ff, fkey, gg, gkey, out, okey, lo=-128.0, hi=127.0):
    p.op("v", lambda e: e.tensor_copy(ii, src), reads=(skey,), writes=(ikey,))
    p.op("v", lambda e: e.tensor_copy(ff, ii), reads=(ikey,), writes=(fkey,))
    p.op("v", lambda e: e.tensor_sub(gg, src, ff), reads=(skey, fkey), writes=(gkey,))
    p.op("v", lambda e: e.tensor_single_scalar(src, gg, 0.5, ALU.is_ge),
         reads=(gkey,), writes=(skey,))
    p.op("v", lambda e: e.tensor_add(ff, ff, src), reads=(fkey, skey), writes=(fkey,))
    p.op("v", lambda e: e.tensor_single_scalar(gg, gg, -0.5, ALU.is_le),
         reads=(gkey,), writes=(gkey,))
    p.op("v", lambda e: e.tensor_sub(ff, ff, gg), reads=(fkey, gkey), writes=(fkey,))
    p.op("v", lambda e: e.tensor_scalar(out, ff, lo, hi, ALU.max, ALU.min),
         reads=(fkey,), writes=(okey,))


class LNT:
    def __init__(self, nc, QT, tag="", ext_vsq=None, ext_vsq_key=None):
        self.QT = QT
        a = nc.alloc_sbuf_tensor
        self.onesr = a(f"ln_onesr{tag}", [1, 128], F32)
        self.onesc = a(f"ln_onesc{tag}", [128, 2], F32)
        self.epsb = a(f"ln_epsb{tag}", [1, 1], F32)
        if ext_vsq is not None:
            self.vsq = ext_vsq
            self.vsq_key = ext_vsq_key
        else:
            self.vsq = a(f"ln_vsq{tag}", [128, PC * QT], F32)
            self.vsq_key = "ln_vsq"
        self.rv = a(f"ln_rv{tag}", [1, QT], F32)
        self.rq = a(f"ln_rq{tag}", [1, QT], F32)
        self.mrow = a(f"ln_mrow{tag}", [1, QT], F32)
        self.varrow = a(f"ln_var{tag}", [1, QT], F32)
        self.srow = a(f"ln_srow{tag}", [1, QT], F32)
        self.rrow = a(f"ln_rrow{tag}", [1, QT], F32)
        self.mb = a(f"ln_mb{tag}", [128, QT], F32)
        self.rb = a(f"ln_rb{tag}", [128, QT], F32)
        self.psr = nc.alloc_psum_tensor(f"ln_psr{tag}", [2, 2 * QT], F32)
        self.psb2 = nc.alloc_psum_tensor(f"ln_psb2{tag}", [128, 2 * QT], F32)

    def init_ops(self, p):
        p.op("v", lambda e: e.memset(self.onesr[:], 1.0), writes=("ln_onesr",))
        p.op("v", lambda e: e.memset(self.onesc[:], 1.0), writes=("ln_onesc",))
        p.op("v", lambda e: e.memset(self.epsb[:], LNEPS), writes=("ln_epsb",))


def emit_postln(p, L, vsb, vkey, out_ap, okey, eps_val=None):
    QT = L.QT
    VK = L.vsq_key
    p.op("s", lambda e: e.activation(L.vsq[:], vsb[:], AF.Square),
         reads=(vkey,), writes=(VK,))

    def statmm(e):
        for k in range(PC):
            e.matmul(L.psr[:, 0:QT], L.onesc[:], vsb[:, k * QT:(k + 1) * QT],
                     start=(k == 0), stop=(k == PC - 1))
        for k in range(PC):
            mm = e.matmul(L.psr[:, QT:2 * QT], L.onesc[:],
                          L.vsq[:, k * QT:(k + 1) * QT],
                          start=(k == 0), stop=(k == PC - 1))
        return mm
    p.op("t", statmm, reads=(vkey, VK, "ln_onesc"), writes=("ln_psr",))
    p.op("v", lambda e: e.tensor_copy(L.rv[:], L.psr[0:1, 0:QT]),
         reads=("ln_psr",), writes=("ln_rv",))
    p.op("v", lambda e: e.tensor_copy(L.rq[:], L.psr[0:1, QT:2 * QT]),
         reads=("ln_psr",), writes=("ln_rq",))
    p.op("v", lambda e: e.tensor_scalar_mul(L.mrow[:], L.rv[:], 1.0 / D),
         reads=("ln_rv",), writes=("ln_mrow",))
    p.op("v", lambda e: e.tensor_mul(L.rv[:], L.mrow[:], L.mrow[:]),
         reads=("ln_mrow",), writes=("ln_rv",))
    p.op("v", lambda e: e.tensor_scalar_mul(L.rq[:], L.rq[:], 1.0 / D),
         reads=("ln_rq",), writes=("ln_rq",))
    p.op("v", lambda e: e.tensor_sub(L.varrow[:], L.rq[:], L.rv[:]),
         reads=("ln_rq", "ln_rv"), writes=("ln_var",))
    p.op("s", lambda e: e.activation(L.srow[:], L.varrow[:], AF.Sqrt,
                                     bias=L.epsb[0:1, 0:1]),
         reads=("ln_var", "ln_epsb"), writes=("ln_srow",))
    p.op("v", lambda e: e.reciprocal(L.rrow[:], L.srow[:]),
         reads=("ln_srow",), writes=("ln_rrow",))

    def bcast(e):
        e.matmul(L.psb2[:, 0:QT], L.onesr[:], L.mrow[:], start=True, stop=True)
        return e.matmul(L.psb2[:, QT:2 * QT], L.onesr[:], L.rrow[:],
                        start=True, stop=True)
    p.op("t", bcast, reads=("ln_mrow", "ln_rrow", "ln_onesr"), writes=("ln_psb2",))
    p.op("v", lambda e: e.tensor_copy(L.mb[:], L.psb2[:, 0:QT]),
         reads=("ln_psb2",), writes=("ln_mb",))
    p.op("v", lambda e: e.tensor_copy(L.rb[:], L.psb2[:, QT:2 * QT]),
         reads=("ln_psb2",), writes=("ln_rb",))
    p.op("v", lambda e: e.tensor_tensor(
        L.vsq[:].rearrange("p (pc t) -> p pc t", pc=PC),
        vsb[:].rearrange("p (pc t) -> p pc t", pc=PC),
        L.mb[:].unsqueeze(1).broadcast_to((128, PC, QT)), ALU.subtract),
        reads=(vkey, "ln_mb"), writes=(VK,))
    p.op("v", lambda e: e.tensor_tensor(
        out_ap.rearrange("p (pc t) -> p pc t", pc=PC),
        L.vsq[:].rearrange("p (pc t) -> p pc t", pc=PC),
        L.rb[:].unsqueeze(1).broadcast_to((128, PC, QT)), ALU.mult),
        reads=(VK, "ln_rb"), writes=(okey,))


def emit_l23f(nc, p, io, T, QT=128, tag=""):
    HID = 3072
    HC = HID // 128
    NCH = T // QT
    xd = io["x"]
    wgd = io["wgT"]
    wud = io["wuT"]
    wdd = io["wdT"]
    yd = io["y"]

    wg = nc.alloc_sbuf_tensor(tag + "wg", [128, PC * HID], BF16)
    wu = nc.alloc_sbuf_tensor(tag + "wu", [128, PC * HID], BF16)
    wdb = nc.alloc_sbuf_tensor(tag + "wdb", [128, HC * D], BF16)
    xs = nc.alloc_sbuf_tensor(tag + "xs", [128, PC * QT], F32)
    xb = nc.alloc_sbuf_tensor(tag + "xb", [128, PC * QT], BF16)
    gb = nc.alloc_sbuf_tensor(tag + "gb", [128, 2 * QT], BF16)
    hb = nc.alloc_sbuf_tensor(tag + "hb", [128, HC * QT], BF16)
    vsb = nc.alloc_sbuf_tensor(tag + "vsb", [128, PC * QT], F32)
    psg = [nc.alloc_psum_tensor(tag + f"psg{i}", [128, QT], F32) for i in range(4)]
    L = LNT(nc, QT, tag=tag)


    p = p
    p.op("g", lambda e: e.dma_start(
        out=wg[:].rearrange("p (pc h) -> p pc h", pc=PC),
        in_=wgd.rearrange("(pc p) h -> p pc h", p=128)), writes=("wg",), dma=True)
    p.op("g", lambda e: e.dma_start(
        out=wu[:].rearrange("p (pc h) -> p pc h", pc=PC),
        in_=wud.rearrange("(pc p) h -> p pc h", p=128)), writes=("wu",), dma=True)
    for hf in range(2):
        p.op("g", lambda e, hf=hf: e.dma_start(
            out=wdb[:, hf * 12 * D:(hf + 1) * 12 * D]
            .rearrange("p (hc d) -> p hc d", hc=12),
            in_=wdd[hf * 12 * 128:(hf + 1) * 12 * 128, :]
            .rearrange("(hc p) d -> p hc d", p=128)), writes=("wdb",), dma=True)
    L.init_ops(p)

    for c in range(NCH):
        p.op("y", lambda e, c=c: e.dma_start(
            out=xs[:], in_=xd[:, c * QT:(c + 1) * QT]
            .rearrange("(pc p) t -> p pc t", p=128)), writes=("xs",), dma=True, group="ld")
        p.op("v", lambda e: e.tensor_copy(xb[:], xs[:]), reads=("xs",), writes=("xb",))
        for d in range(HC):
            def gmm(e, d=d):
                for k in range(PC):
                    mm = e.matmul(psg[d % 2][:],
                                  wg[:, k * HID + d * 128:k * HID + (d + 1) * 128],
                                  xb[:, k * QT:(k + 1) * QT],
                                  start=(k == 0), stop=(k == PC - 1))
                return mm
            p.op("t", gmm, reads=("wg", "xb"), writes=(f"psA{d % 2}",))
            p.op("s", lambda e, d=d: e.activation(
                gb[:, (d % 2) * QT:(d % 2 + 1) * QT], psg[d % 2][:], AF.Silu,
                scale=1.0 / 127.0), reads=(f"psA{d % 2}",), writes=(f"gb{d % 2}",))

            def umm(e, d=d):
                for k in range(PC):
                    mm = e.matmul(psg[2 + d % 2][:],
                                  wu[:, k * HID + d * 128:k * HID + (d + 1) * 128],
                                  xb[:, k * QT:(k + 1) * QT],
                                  start=(k == 0), stop=(k == PC - 1))
                return mm
            p.op("t", umm, reads=("wu", "xb"), writes=(f"psB{d % 2}",))
            p.op("v", lambda e, d=d: e.tensor_tensor(
                hb[:, d * QT:(d + 1) * QT], psg[2 + d % 2][:],
                gb[:, (d % 2) * QT:(d % 2 + 1) * QT], ALU.mult),
                reads=(f"psB{d % 2}", f"gb{d % 2}"), writes=(f"hb{d}",))
        hkeys = tuple(f"hb{d}" for d in range(HC))
        for dd in range(PC):
            def dmm(e, dd=dd):
                for k in range(HC):
                    mm = e.matmul(psg[dd % 4][:],
                                  wdb[:, k * D + dd * 128:k * D + (dd + 1) * 128],
                                  hb[:, k * QT:(k + 1) * QT],
                                  start=(k == 0), stop=(k == HC - 1))
                return mm
            p.op("t", dmm, reads=("wdb",) + hkeys,
                 writes=(f"ps{'AB'[dd % 4 // 2]}{dd % 2}",))
            p.op("v", lambda e, dd=dd: e.scalar_tensor_tensor(
                vsb[:, dd * QT:(dd + 1) * QT], psg[dd % 4][:], 1.0 / 16129.0,
                xs[:, dd * QT:(dd + 1) * QT], ALU.mult, ALU.add),
                reads=(f"ps{'AB'[dd % 4 // 2]}{dd % 2}", "xs"), writes=("vsb",))
        emit_postln(p, L, vsb, "vsb", xs[:], "xs")
        p.op("s", lambda e, c=c: e.dma_start(
            out=yd[:, c * QT:(c + 1) * QT].rearrange("(pc p) t -> p pc t", p=128),
            in_=xs[:]), reads=("xs",), writes=("ydone",), dma=True, group="st")



def build_l23f(T, QT=128):
    nc = bass.Bass("TRN2", target_bir_lowering=False)
    io = {}
    io["x"] = nc.dram_tensor("x", [D, T], F32, kind="ExternalInput").ap()
    io["wgT"] = nc.dram_tensor("wgT", [D, HID], I32, kind="ExternalInput").ap()
    io["wuT"] = nc.dram_tensor("wuT", [D, HID], I32, kind="ExternalInput").ap()
    io["wdT"] = nc.dram_tensor("wdT", [HID, D], I32, kind="ExternalInput").ap()
    io["y"] = nc.dram_tensor("y", [D, T], F32, kind="ExternalOutput").ap()
    p = Prog()
    emit_l23f(nc, p, io, T, QT)
    p.op("y", lambda e: e.nop(), reads=("ydone",))
    p.emit(nc)
    return nc
def emit_l3a(nc, p, io, T, QT=512, tag=""):
    NCH = T // QT
    xd = io["x"]
    wid = io["w_in"]
    wad = io["w_a"]
    apd = io["ap_pm"]
    yd = io["y"]

    wib = nc.alloc_sbuf_tensor(tag + "wib", [128, PC * 128], BF16)
    wab = nc.alloc_sbuf_tensor(tag + "wab", [128, PC * 128], BF16)
    aps = nc.alloc_sbuf_tensor(tag + "aps", [128, PC], F32)
    sp8 = nc.alloc_sbuf_tensor(tag + "sp8", [128, PC], F32)
    spt = nc.alloc_sbuf_tensor(tag + "spt", [128, PC], F32)
    onec = nc.alloc_sbuf_tensor(tag + "onec", [128, 1], F32)
    carry = nc.alloc_sbuf_tensor(tag + "carry3", [128, PC], F32)
    xs = nc.alloc_sbuf_tensor(tag + "xs", [128, PC * QT], F32)
    xb = nc.alloc_sbuf_tensor(tag + "xb", [128, PC * QT], BF16)
    gx = nc.alloc_sbuf_tensor(tag + "gx", [128, PC * QT], F32)
    ga = nc.alloc_sbuf_tensor(tag + "ga", [128, PC * QT], F32)
    av = nc.alloc_sbuf_tensor(tag + "av", [128, PC * QT], F32)
    bv = nc.alloc_sbuf_tensor(tag + "bv", [128, PC * QT], F32)
    sc = nc.alloc_sbuf_tensor(tag + "sc", [128, PC * QT], F32)
    vsb = nc.alloc_sbuf_tensor(tag + "vsb", [128, PC * QT], F32)
    psm = [nc.alloc_psum_tensor(tag + f"psm{i}", [128, QT], F32) for i in range(4)]
    L = LNT(nc, QT, tag=tag)


    p = p
    for h in range(PC):
        p.op("g", lambda e, h=h: e.dma_start(
            out=wib[:, h * 128:(h + 1) * 128], in_=wid[h]), writes=("wib",), dma=True)
        p.op("g", lambda e, h=h: e.dma_start(
            out=wab[:, h * 128:(h + 1) * 128], in_=wad[h]), writes=("wab",), dma=True)
    p.op("y", lambda e: e.dma_start(out=aps[:], in_=apd[:]), writes=("aps",), dma=True)
    L.init_ops(p)
    p.op("v", lambda e: e.memset(onec[:], 1.0), writes=("onec",))
    p.op("v", lambda e: e.memset(carry[:], 0.0), writes=("carry",))
    p.op("s", lambda e: e.activation(spt[:], aps[:], AF.Abs),
         reads=("aps",), writes=("spt",))
    p.op("s", lambda e: e.activation(spt[:], spt[:], AF.Exp, scale=-1.0),
         reads=("spt",), writes=("spt",))
    p.op("s", lambda e: e.activation(spt[:], spt[:], AF.Ln, bias=onec[:]),
         reads=("spt", "onec"), writes=("spt",))
    p.op("s", lambda e: e.activation(sp8[:], aps[:], AF.Relu),
         reads=("aps",), writes=("sp8",))
    p.op("v", lambda e: e.tensor_add(sp8[:], sp8[:], spt[:]),
         reads=("sp8", "spt"), writes=("sp8",))
    p.op("v", lambda e: e.tensor_scalar_mul(sp8[:], sp8[:], -8.0),
         reads=("sp8",), writes=("sp8",))

    for c in range(NCH):
        p.op("y", lambda e, c=c: e.dma_start(
            out=xs[:], in_=xd[:, c * QT:(c + 1) * QT]
            .rearrange("(pc p) t -> p pc t", p=128)), writes=("xs",), dma=True,
            group="ld")
        p.op("v", lambda e: e.tensor_copy(xb[:], xs[:]), reads=("xs",), writes=("xb",))
        for h in range(PC):
            p.op("t", lambda e, h=h: e.matmul(
                psm[h % 2][:], wib[:, h * 128:(h + 1) * 128],
                xb[:, h * QT:(h + 1) * QT], start=True, stop=True),
                reads=("wib", "xb"), writes=(f"pA{h % 2}",))
            p.op("s", lambda e, h=h: e.activation(
                gx[:, h * QT:(h + 1) * QT], psm[h % 2][:], AF.Sigmoid),
                reads=(f"pA{h % 2}",), writes=("gx",))
            p.op("t", lambda e, h=h: e.matmul(
                psm[2 + h % 2][:], wab[:, h * 128:(h + 1) * 128],
                xb[:, h * QT:(h + 1) * QT], start=True, stop=True),
                reads=("wab", "xb"), writes=(f"pB{h % 2}",))
            p.op("s", lambda e, h=h: e.activation(
                ga[:, h * QT:(h + 1) * QT], psm[2 + h % 2][:], AF.Sigmoid),
                reads=(f"pB{h % 2}",), writes=("ga",))
        p.op("v", lambda e: e.tensor_tensor(
            ga[:].rearrange("p (pc t) -> p pc t", pc=PC),
            ga[:].rearrange("p (pc t) -> p pc t", pc=PC),
            sp8[:].unsqueeze(2).broadcast_to((128, PC, QT)), ALU.mult),
            reads=("ga", "sp8"), writes=("ga",))
        p.op("s", lambda e: e.activation(av[:], ga[:], AF.Exp),
             reads=("ga",), writes=("av",))
        p.op("s", lambda e: e.activation(ga[:], ga[:], AF.Exp, scale=2.0),
             reads=("ga",), writes=("ga",))
        p.op("v", lambda e: e.tensor_scalar(ga[:], ga[:], 1.0, -1.0,
                                            ALU.subtract, ALU.mult),
             reads=("ga",), writes=("ga",))
        p.op("s", lambda e: e.activation(ga[:], ga[:], AF.Sqrt),
             reads=("ga",), writes=("ga",))
        p.op("v", lambda e: e.tensor_mul(bv[:], gx[:], xs[:]),
             reads=("gx", "xs"), writes=("bv",))
        p.op("v", lambda e: e.tensor_mul(bv[:], bv[:], ga[:]),
             reads=("bv", "ga"), writes=("bv",))
        for pc in range(PC):
            p.op("v", lambda e, pc=pc: e.tensor_tensor_scan(
                sc[:, pc * QT:(pc + 1) * QT], av[:, pc * QT:(pc + 1) * QT],
                bv[:, pc * QT:(pc + 1) * QT], carry[:, pc:pc + 1],
                ALU.mult, ALU.add), reads=("av", "bv", "carry"), writes=("sc", f"sc{pc}"))
        p.op("v", lambda e: e.tensor_copy(
            carry[:], sc[:].rearrange("p (pc t) -> p pc t", pc=PC)[:, :, QT - 1]),
            reads=("sc",), writes=("carry",))
        p.op("v", lambda e: e.tensor_add(vsb[:], sc[:], xs[:]),
             reads=("sc", "xs"), writes=("vsb",))
        emit_postln(p, L, vsb, "vsb", xs[:], "xs")
        p.op("s", lambda e, c=c: e.dma_start(
            out=yd[:, c * QT:(c + 1) * QT].rearrange("(pc p) t -> p pc t", p=128),
            in_=xs[:]), reads=("xs",), writes=("ydone",), dma=True, group="st")



def build_l3a(T, QT=512):
    nc = bass.Bass("TRN2", target_bir_lowering=False)
    io = {}
    io["x"] = nc.dram_tensor("x", [D, T], F32, kind="ExternalInput").ap()
    io["w_in"] = nc.dram_tensor("w_in", [8, 128, 128], F32, kind="ExternalInput").ap()
    io["w_a"] = nc.dram_tensor("w_a", [8, 128, 128], F32, kind="ExternalInput").ap()
    io["ap_pm"] = nc.dram_tensor("ap_pm", [128, PC], F32, kind="ExternalInput").ap()
    io["y"] = nc.dram_tensor("y", [D, T], F32, kind="ExternalOutput").ap()
    p = Prog()
    emit_l3a(nc, p, io, T, QT)
    p.op("y", lambda e: e.nop(), reads=("ydone",))
    p.emit(nc)
    return nc
def emit_l1a(nc, p, io, T, QT=512, tag=""):
    NCH = T // QT
    H = 16
    xd = io["x"]
    qd = io["q_pm"]
    kd = io["k_pm"]
    nwd = io["nw_pm"]
    ldd = io["ld_row"]
    owd = io["owT"]
    hvd = io["halves"]
    hmd = io["hm16"]
    h64d = io["h64"]
    idd = io["ident"]
    yd = io["y"]

    ow = nc.alloc_sbuf_tensor(tag + "ow", [128, PC * D], BF16)
    qs = nc.alloc_sbuf_tensor(tag + "qs", [128, PC], F32)
    ks = nc.alloc_sbuf_tensor(tag + "ks", [128, PC], F32)
    nw = nc.alloc_sbuf_tensor(tag + "nw", [128, PC], F32)
    lds = nc.alloc_sbuf_tensor(tag + "lds", [1, H], F32)
    hvs = nc.alloc_sbuf_tensor(tag + "hvs", [128, 2], F32)
    hms = nc.alloc_sbuf_tensor(tag + "hms", [H, PC], F32)
    h64s = nc.alloc_sbuf_tensor(tag + "h64s", [H, 128], F32)
    ids = nc.alloc_sbuf_tensor(tag + "ids", [128, 128], F32)
    qkp = nc.alloc_sbuf_tensor(tag + "qkp", [128, PC], F32)
    QH = nc.alloc_sbuf_tensor(tag + "QH", [128, H], F32)
    onec = nc.alloc_sbuf_tensor(tag + "onec", [128, 1], F32)
    col16 = nc.alloc_sbuf_tensor(tag + "col16", [H, 1], F32)
    lamcol16 = nc.alloc_sbuf_tensor(tag + "lamcol16", [H, 1], F32)
    rhs168 = nc.alloc_sbuf_tensor(tag + "rhs168", [H, PC], F32)
    qkc = nc.alloc_sbuf_tensor(tag + "qkc", [128, PC], F32)
    lamc = nc.alloc_sbuf_tensor(tag + "lamc", [128, PC], F32)
    eps2 = nc.alloc_sbuf_tensor(tag + "eps2", [1, 1], F32)
    carry = nc.alloc_sbuf_tensor(tag + "carry1", [128, PC], F32)
    xs = nc.alloc_sbuf_tensor(tag + "xs", [128, PC * QT], F32)
    sc = nc.alloc_sbuf_tensor(tag + "sc", [128, PC * QT], F32)
    ov = nc.alloc_sbuf_tensor(tag + "ov", [128, PC * QT], F32)
    osq = nc.alloc_sbuf_tensor(tag + "osq", [128, PC * QT], F32)
    ob = nc.alloc_sbuf_tensor(tag + "ob", [128, PC * QT], BF16)
    vsb = nc.alloc_sbuf_tensor(tag + "vsb", [128, PC * QT], F32)
    ror = nc.alloc_sbuf_tensor(tag + "ror", [1, QT], F32)
    sor = nc.alloc_sbuf_tensor(tag + "sor", [1, QT], F32)
    rob = nc.alloc_sbuf_tensor(tag + "rob", [128, QT], F32)
    psm = [nc.alloc_psum_tensor(tag + f"psm{i}", [128, QT], F32) for i in range(2)]
    psro = nc.alloc_psum_tensor(tag + "psro", [2, QT], F32)
    L = LNT(nc, QT, tag=tag)
    psb1 = L.psb2[:, 0:QT]
    psi = psm[0][:, 0:2 + PC]


    p = p
    p.op("g", lambda e: e.dma_start(
        out=ow[:].rearrange("p (pc d) -> p pc d", pc=PC),
        in_=owd.rearrange("(pc p) d -> p pc d", p=128)), writes=("ow",), dma=True)
    for nm, dst, srcap in (("qs", qs, qd), ("ks", ks, kd), ("nw", nw, nwd),
                           ("lds", lds, ldd), ("hvs", hvs, hvd), ("hms", hms, hmd),
                           ("h64s", h64s, h64d), ("ids", ids, idd)):
        p.op("y", lambda e, dst=dst, srcap=srcap: e.dma_start(out=dst[:], in_=srcap[:]),
             writes=(nm,), dma=True)
    L.init_ops(p)
    p.op("v", lambda e: e.memset(onec[:], 1.0), writes=("onec",))
    p.op("v", lambda e: e.memset(eps2[:], 1e-6), writes=("eps2",))
    p.op("v", lambda e: e.memset(carry[:], 0.0), writes=("carry",))
    p.op("v", lambda e: e.tensor_mul(qkp[:], qs[:], ks[:]),
         reads=("qs", "ks"), writes=("qkp",))
    p.op("v", lambda e: e.tensor_tensor(
        QH[:].rearrange("p (a b) -> p a b", a=PC),
        qkp[:].unsqueeze(2).broadcast_to((128, PC, 2)),
        hvs[:].unsqueeze(1).broadcast_to((128, PC, 2)), ALU.mult),
        reads=("qkp", "hvs"), writes=("QH",))
    p.op("t", lambda e: e.matmul(psi[0:H, 0:1], QH[:], onec[:],
                                 start=True, stop=True),
         reads=("QH", "onec"), writes=("pA0",))
    p.op("v", lambda e: e.tensor_copy(col16[:], psi[0:H, 0:1]),
         reads=("pA0",), writes=("col16",))
    p.op("s", lambda e: e.activation(lds[:], lds[:], AF.Sigmoid),
         reads=("lds",), writes=("lds",))
    p.op("t", lambda e: e.transpose(psi[0:H, 1:2], lds[:], ids[0:1, 0:1]),
         reads=("lds", "ids"), writes=("pA0",))
    p.op("v", lambda e: e.tensor_copy(lamcol16[:], psi[0:H, 1:2]),
         reads=("pA0",), writes=("lamcol16",))
    for nm, col, out in (("qkc", col16, qkc), ("lamc", lamcol16, lamc)):
        p.op("v", lambda e, col=col: e.tensor_tensor(
            rhs168[:], hms[:], col[:].broadcast_to((H, PC)), ALU.mult),
            reads=("hms", nm + "_src" if False else ("col16" if col is col16 else "lamcol16")),
            writes=("rhs168",))
        p.op("t", lambda e: e.matmul(psi[:, 2:2 + PC], h64s[:], rhs168[:],
                                     start=True, stop=True),
             reads=("rhs168", "h64s"), writes=("pA0",))
        p.op("v", lambda e, out=out: e.tensor_copy(out[:], psi[:, 2:2 + PC]),
             reads=("pA0",), writes=(nm,))

    for c in range(NCH):
        p.op("y", lambda e, c=c: e.dma_start(
            out=xs[:], in_=xd[:, c * QT:(c + 1) * QT]
            .rearrange("(pc p) t -> p pc t", p=128)), writes=("xs",), dma=True,
            group="ld")
        for pc in range(PC):
            p.op("v", lambda e, pc=pc: e.tensor_tensor_scan(
                sc[:, pc * QT:(pc + 1) * QT],
                lamc[:, pc:pc + 1].broadcast_to((128, QT)),
                xs[:, pc * QT:(pc + 1) * QT], carry[:, pc:pc + 1],
                ALU.mult, ALU.add), reads=("lamc", "xs", "carry"), writes=("sc",))
        p.op("v", lambda e: e.tensor_copy(
            carry[:], sc[:].rearrange("p (pc t) -> p pc t", pc=PC)[:, :, QT - 1]),
            reads=("sc",), writes=("carry",))
        p.op("v", lambda e: e.tensor_tensor(
            ov[:].rearrange("p (pc t) -> p pc t", pc=PC),
            sc[:].rearrange("p (pc t) -> p pc t", pc=PC),
            qkc[:].unsqueeze(2).broadcast_to((128, PC, QT)), ALU.mult),
            reads=("sc", "qkc"), writes=("ov",))
        p.op("s", lambda e: e.activation(osq[:], ov[:], AF.Square),
             reads=("ov",), writes=("osq",))

        def rmsmm(e):
            for k in range(PC):
                mm = e.matmul(psro[:], L.onesc[:], osq[:, k * QT:(k + 1) * QT],
                              start=(k == 0), stop=(k == PC - 1))
            return mm
        p.op("t", rmsmm, reads=("osq", "ln_onesc"), writes=("psro",))
        p.op("v", lambda e: e.tensor_copy(ror[:], psro[0:1, :]),
             reads=("psro",), writes=("ror",))
        p.op("v", lambda e: e.tensor_scalar_mul(ror[:], ror[:], 1.0 / D),
             reads=("ror",), writes=("ror",))
        p.op("s", lambda e: e.activation(sor[:], ror[:], AF.Sqrt, bias=eps2[0:1, 0:1]),
             reads=("ror", "eps2"), writes=("sor",))
        p.op("v", lambda e: e.reciprocal(ror[:], sor[:]), reads=("sor",), writes=("ror",))
        p.op("t", lambda e: e.matmul(psb1, L.onesr[:], ror[:], start=True, stop=True),
             reads=("ror", "ln_onesr"), writes=("ln_psb2",))
        p.op("v", lambda e: e.tensor_copy(rob[:], psb1), reads=("ln_psb2",), writes=("rob",))
        p.op("v", lambda e: e.tensor_tensor(
            osq[:].rearrange("p (pc t) -> p pc t", pc=PC),
            ov[:].rearrange("p (pc t) -> p pc t", pc=PC),
            rob[:].unsqueeze(1).broadcast_to((128, PC, QT)), ALU.mult),
            reads=("ov", "rob"), writes=("osq",))
        p.op("v", lambda e: e.tensor_tensor(
            ob[:].rearrange("p (pc t) -> p pc t", pc=PC),
            osq[:].rearrange("p (pc t) -> p pc t", pc=PC),
            nw[:].unsqueeze(2).broadcast_to((128, PC, QT)), ALU.mult),
            reads=("osq", "nw"), writes=("ob",))
        for dd in range(PC):
            def omm(e, dd=dd):
                for k in range(PC):
                    mm = e.matmul(psm[dd % 2][:],
                                  ow[:, k * D + dd * 128:k * D + (dd + 1) * 128],
                                  ob[:, k * QT:(k + 1) * QT],
                                  start=(k == 0), stop=(k == PC - 1))
                return mm
            p.op("t", omm, reads=("ow", "ob"), writes=(f"pO{dd % 2}",))
            p.op("v", lambda e, dd=dd: e.tensor_tensor(
                vsb[:, dd * QT:(dd + 1) * QT], psm[dd % 2][:],
                xs[:, dd * QT:(dd + 1) * QT], ALU.add),
                reads=(f"pO{dd % 2}", "xs"), writes=("vsb",))
        emit_postln(p, L, vsb, "vsb", xs[:], "xs")
        p.op("s", lambda e, c=c: e.dma_start(
            out=yd[:, c * QT:(c + 1) * QT].rearrange("(pc p) t -> p pc t", p=128),
            in_=xs[:]), reads=("xs",), writes=("ydone",), dma=True, group="st")



def build_l1a(T, QT=512):
    nc = bass.Bass("TRN2", target_bir_lowering=False)
    io = {}
    io["x"] = nc.dram_tensor("x", [D, T], F32, kind="ExternalInput").ap()
    io["q_pm"] = nc.dram_tensor("q_pm", [128, PC], F32, kind="ExternalInput").ap()
    io["k_pm"] = nc.dram_tensor("k_pm", [128, PC], F32, kind="ExternalInput").ap()
    io["nw_pm"] = nc.dram_tensor("nw_pm", [128, PC], F32, kind="ExternalInput").ap()
    io["ld_row"] = nc.dram_tensor("ld_row", [1, H], F32, kind="ExternalInput").ap()
    io["owT"] = nc.dram_tensor("owT", [D, D], F32, kind="ExternalInput").ap()
    io["halves"] = nc.dram_tensor("halves", [128, 2], F32, kind="ExternalInput").ap()
    io["hm16"] = nc.dram_tensor("hm16", [H, PC], F32, kind="ExternalInput").ap()
    io["h64"] = nc.dram_tensor("h64", [H, 128], F32, kind="ExternalInput").ap()
    io["ident"] = nc.dram_tensor("ident", [128, 128], F32, kind="ExternalInput").ap()
    io["y"] = nc.dram_tensor("y", [D, T], F32, kind="ExternalOutput").ap()
    p = Prog()
    emit_l1a(nc, p, io, T, QT)
    p.op("y", lambda e: e.nop(), reads=("ydone",))
    p.emit(nc)
    return nc
def emit_l0f(nc, p, io, T, tag=""):
    QT = 128
    NCH = T // QT
    xd = io["x"]
    wd = io["w"]
    bd = io["bias_pm"]
    idd = io["ident"]
    yd = io["y"]

    wf = nc.alloc_sbuf_tensor(tag + "wf", [128, PC * D], F32)
    wqn = nc.alloc_sbuf_tensor(tag + "wqn", [128, PC * D], BF16)
    wqT = nc.alloc_sbuf_tensor(tag + "wqT", [128, PC * D], BF16)
    wsc = nc.alloc_sbuf_tensor(tag + "wsc", [128, PC], F32)
    wrs = nc.alloc_sbuf_tensor(tag + "wrs", [128, PC], F32)
    wi32 = nc.alloc_sbuf_tensor(tag + "wi32", [128, D], I32)
    bias = nc.alloc_sbuf_tensor(tag + "bias", [128, PC], F32)
    ids = nc.alloc_sbuf_tensor(tag + "ids", [128, 128], F32)
    idb = nc.alloc_sbuf_tensor(tag + "idb", [128, 128], BF16)
    xs = nc.alloc_sbuf_tensor(tag + "xs", [128, PC * QT], F32)
    amax = nc.alloc_sbuf_tensor(tag + "amax", [128, 1], F32)
    am2 = nc.alloc_sbuf_tensor(tag + "am2", [128, 2], F32)
    amr1 = nc.alloc_sbuf_tensor(tag + "amr1", [1, 128], F32)
    amr2 = nc.alloc_sbuf_tensor(tag + "amr2", [1, 128], F32)
    rsb = nc.alloc_sbuf_tensor(tag + "rsb", [128, 2 * QT], F32)
    xq1 = nc.alloc_sbuf_tensor(tag + "xq1", [128, PC * QT], F32)
    xqi = nc.alloc_sbuf_tensor(tag + "xqi", [128, PC * QT], I32)
    xqf = nc.alloc_sbuf_tensor(tag + "xqf", [128, PC * QT], F32)
    xqb = nc.alloc_sbuf_tensor(tag + "xqb", [128, PC * QT], BF16)
    vsb = nc.alloc_sbuf_tensor(tag + "vsb", [128, PC * QT], F32)
    t1s = nc.alloc_sbuf_tensor(tag + "t1s", [128, QT], F32)
    pst = [nc.alloc_psum_tensor(tag + f"pst{i}", [128, 128], F32) for i in range(2)]
    psq = [nc.alloc_psum_tensor(tag + f"psq{i}", [128, QT], F32) for i in range(2)]
    psb = nc.alloc_psum_tensor(tag + "psbc", [128, 2 * QT], F32)
    L = LNT(nc, QT, tag=tag)


    p = p
    p.op("y", lambda e: e.dma_start(out=ids[:], in_=idd[:]), writes=("ids",), dma=True)
    p.op("y", lambda e: e.dma_start(out=bias[:], in_=bd[:]), writes=("bias",), dma=True)
    p.op("y", lambda e: e.dma_start(
        out=wf[:].rearrange("p (pc d) -> p pc d", pc=PC),
        in_=wd.rearrange("(pc p) d -> p pc d", p=128)), writes=("wf",), dma=True)
    L.init_ops(p)
    p.op("v", lambda e: e.tensor_copy(idb[:], ids[:]), reads=("ids",), writes=("idb",))
    for pc in range(PC):
        p.op("v", lambda e, pc=pc: e.tensor_reduce(
            wsc[:, pc:pc + 1], wf[:, pc * D:(pc + 1) * D], AX.X, ALU.max,
            apply_absolute_value=True), reads=("wf",), writes=("wsc",))
    p.op("v", lambda e: e.tensor_scalar_max(wsc[:], wsc[:], 1e-8),
         reads=("wsc",), writes=("wsc",))
    p.op("v", lambda e: e.tensor_scalar_mul(wsc[:], wsc[:], 1.0 / 127.0),
         reads=("wsc",), writes=("wsc",))
    p.op("v", lambda e: e.reciprocal(wrs[:], wsc[:]), reads=("wsc",), writes=("wrs",))
    for pc in range(PC):
        p.op("v", lambda e, pc=pc: e.tensor_scalar(
            wf[:, pc * D:(pc + 1) * D], wf[:, pc * D:(pc + 1) * D],
            wrs[:, pc:pc + 1], None, ALU.mult), reads=("wf", "wrs"), writes=("wf",))
    for pc in range(PC):
        emit_rneq(p, wf[:, pc * D:(pc + 1) * D], "wf", wi32[:], "wi32",
                  xq1[:, 0:D], "xq1", vsb[:, 0:D], "vsb",
                  wqn[:, pc * D:(pc + 1) * D], "wqn")
    for a in range(PC):
        for bb in range(PC):
            p.op("t", lambda e, a=a, bb=bb: e.transpose(
                pst[(a * PC + bb) % 2][:].bitcast(BF16)[:, 0:128],
                wqn[:, a * D + bb * 128:a * D + (bb + 1) * 128], idb[:]),
                reads=("wqn", "idb"), writes=(f"pt{(a * PC + bb) % 2}",))
            p.op("v", lambda e, a=a, bb=bb: e.tensor_copy(
                wqT[:, bb * D + a * 128:bb * D + (a + 1) * 128],
                pst[(a * PC + bb) % 2][:].bitcast(BF16)[:, 0:128]),
                reads=(f"pt{(a * PC + bb) % 2}",), writes=("wqT",))

    for c in range(NCH):
        p.op("y", lambda e, c=c: e.dma_start(
            out=xs[:], in_=xd[:, c * QT:(c + 1) * QT]
            .rearrange("(pc p) t -> p pc t", p=128)), writes=("xs",), dma=True,
            group="ld")
        for pc in range(PC):
            p.op("t", lambda e, pc=pc: e.transpose(
                pst[pc % 2][:], xs[:, pc * QT:(pc + 1) * QT], ids[:]),
                reads=("xs", "ids"), writes=(f"pt{pc % 2}",))
            if pc == 0:
                p.op("v", lambda e: e.tensor_reduce(
                    amax[:], pst[0][:], AX.X, ALU.max, apply_absolute_value=True),
                    reads=("pt0",), writes=("amax",))
            else:
                p.op("v", lambda e, pc=pc: e.tensor_reduce(
                    t1s[:, 0:1], pst[pc % 2][:], AX.X, ALU.max,
                    apply_absolute_value=True), reads=(f"pt{pc % 2}",), writes=("t1s",))
                p.op("v", lambda e: e.tensor_max(amax[:], amax[:], t1s[:, 0:1]),
                     reads=("amax", "t1s"), writes=("amax",))
        p.op("v", lambda e: e.tensor_scalar_max(amax[:], amax[:], 1e-8),
             reads=("amax",), writes=("amax",))
        p.op("v", lambda e: e.reciprocal(am2[:, 0:1], amax[:]),
             reads=("amax",), writes=("am2",))
        p.op("v", lambda e: e.tensor_scalar_mul(am2[:, 0:1], am2[:, 0:1], 127.0),
             reads=("am2",), writes=("am2",))
        p.op("v", lambda e: e.tensor_scalar_mul(am2[:, 1:2], amax[:], 1.0 / 127.0),
             reads=("amax",), writes=("am2",))
        p.op("t", lambda e: e.transpose(pst[0][0:1, 0:128], am2[:, 0:1], ids[:]),
             reads=("am2", "ids"), writes=("pt0",))
        p.op("t", lambda e: e.transpose(pst[1][0:1, 0:128], am2[:, 1:2], ids[:]),
             reads=("am2", "ids"), writes=("pt1",))
        p.op("v", lambda e: e.tensor_copy(amr1[:], pst[0][0:1, 0:128]),
             reads=("pt0",), writes=("amr1",))
        p.op("v", lambda e: e.tensor_copy(amr2[:], pst[1][0:1, 0:128]),
             reads=("pt1",), writes=("amr2",))
        def bc2(e):
            e.matmul(psb[:, 0:QT], L.onesr[:], amr1[:], start=True, stop=True)
            return e.matmul(psb[:, QT:2 * QT], L.onesr[:], amr2[:],
                            start=True, stop=True)
        p.op("t", bc2, reads=("amr1", "amr2", "ln_onesr"), writes=("psbc",))
        p.op("v", lambda e: e.tensor_copy(rsb[:], psb[:]), reads=("psbc",), writes=("rsb",))
        p.op("v", lambda e: e.tensor_tensor(
            xq1[:].rearrange("p (pc t) -> p pc t", pc=PC),
            xs[:].rearrange("p (pc t) -> p pc t", pc=PC),
            rsb[:, 0:QT].unsqueeze(1).broadcast_to((128, PC, QT)), ALU.mult),
            reads=("xs", "rsb"), writes=("xq1",))
        emit_rneq(p, xq1[:], "xq1", xqi[:], "xqi", xqf[:], "xqf",
                  vsb[:], "vsb", xqb[:], "xqb")
        for dd in range(PC):
            def qmm(e, dd=dd):
                for k in range(PC):
                    mm = e.matmul(psq[dd % 2][:],
                                  wqT[:, k * D + dd * 128:k * D + (dd + 1) * 128],
                                  xqb[:, k * QT:(k + 1) * QT],
                                  start=(k == 0), stop=(k == PC - 1))
                return mm
            p.op("t", qmm, reads=("wqT", "xqb"), writes=(f"pq{dd % 2}",))
            p.op("v", lambda e, dd=dd: e.scalar_tensor_tensor(
                xq1[:, dd * QT:(dd + 1) * QT], psq[dd % 2][:], wsc[:, dd:dd + 1],
                rsb[:, QT:2 * QT], ALU.mult, ALU.mult),
                reads=(f"pq{dd % 2}", "wsc", "rsb"), writes=("xq1",))
            p.op("v", lambda e, dd=dd: e.scalar_tensor_tensor(
                vsb[:, dd * QT:(dd + 1) * QT], xq1[:, dd * QT:(dd + 1) * QT],
                bias[:, dd:dd + 1], xs[:, dd * QT:(dd + 1) * QT],
                ALU.add, ALU.add),
                reads=("xq1", "bias", "xs"), writes=("vsb",))
        emit_postln(p, L, vsb, "vsb", xs[:], "xs")
        p.op("s", lambda e, c=c: e.dma_start(
            out=yd[:, c * QT:(c + 1) * QT].rearrange("(pc p) t -> p pc t", p=128),
            in_=xs[:]), reads=("xs",), writes=("ydone",), dma=True, group="st")



def build_l0f(T):
    nc = bass.Bass("TRN2", target_bir_lowering=False)
    io = {}
    io["x"] = nc.dram_tensor("x", [D, T], F32, kind="ExternalInput").ap()
    io["w"] = nc.dram_tensor("w", [D, D], F32, kind="ExternalInput").ap()
    io["bias_pm"] = nc.dram_tensor("bias_pm", [128, PC], F32, kind="ExternalInput").ap()
    io["ident"] = nc.dram_tensor("ident", [128, 128], F32, kind="ExternalInput").ap()
    io["y"] = nc.dram_tensor("y", [D, T], F32, kind="ExternalOutput").ap()
    p = Prog()
    emit_l0f(nc, p, io, T)
    p.op("y", lambda e: e.nop(), reads=("ydone",))
    p.emit(nc)
    return nc
def emit_l1f(nc, p, io, T, tag=""):
    QT = 128
    H2 = 4096
    HC = H2 // 128
    NCH = T // QT
    xd = io["x"]
    n1d = io["nw1_pm"]
    n2d = io["nw2_pm"]
    w1d = io["w1T"]
    w2d = io["w2T"]
    acd = io["acts"]
    idd = io["ident"]
    yd = io["y"]

    wq1 = nc.alloc_sbuf_tensor(tag + "wq1", [128, PC * H2], BF16)
    wq2 = nc.alloc_sbuf_tensor(tag + "wq2", [128, HC * D], BF16)
    L = LNT(nc, QT, tag=tag)
    absacc = nc.alloc_sbuf_tensor(tag + "absacc", [128, 1], F32)
    redt = nc.alloc_sbuf_tensor(tag + "redt", [128, 1], F32)
    onec = nc.alloc_sbuf_tensor(tag + "onec", [128, 1], F32)
    cells = nc.alloc_sbuf_tensor(tag + "cells", [1, 8], F32)
    wsc1c = nc.alloc_sbuf_tensor(tag + "wsc1c", [128, 1], F32)
    wsc2c = nc.alloc_sbuf_tensor(tag + "wsc2c", [128, 1], F32)
    n1 = nc.alloc_sbuf_tensor(tag + "n1", [128, PC], F32)
    n2 = nc.alloc_sbuf_tensor(tag + "n2", [128, HC], F32)
    acs = nc.alloc_sbuf_tensor(tag + "acs", [1, 2], F32)
    acol = nc.alloc_sbuf_tensor(tag + "acol", [128, 1], F32)
    bcol = nc.alloc_sbuf_tensor(tag + "bcol", [128, 1], F32)
    ids = nc.alloc_sbuf_tensor(tag + "ids", [128, 128], F32)
    eps8 = nc.alloc_sbuf_tensor(tag + "eps8", [1, 1], F32)
    xs = nc.alloc_sbuf_tensor(tag + "xs", [128, PC * QT], F32)
    q1 = nc.alloc_sbuf_tensor(tag + "q1", [128, PC * QT], F32)
    q1g = nc.alloc_sbuf_tensor(tag + "q1g", [128, PC * QT], F32)
    q1b = nc.alloc_sbuf_tensor(tag + "q1b", [128, PC * QT], BF16)
    hb1 = nc.alloc_sbuf_tensor(tag + "hb1", [128, HC * QT], F32)
    h2i = nc.alloc_sbuf_tensor(tag + "h2i", [128, HC * QT // 4], I32)
    h2f = nc.alloc_sbuf_tensor(tag + "h2f", [128, HC * QT // 2], F32)
    h2g = nc.alloc_sbuf_tensor(tag + "h2g", [128, HC * QT], F32)
    h2b = nc.alloc_sbuf_tensor(tag + "h2b", [128, HC * QT], BF16)
    amax = nc.alloc_sbuf_tensor(tag + "amax", [128, 1], F32)
    gwb = L.mb
    osb = L.rb
    r_am = L.rq
    r_t = L.rrow
    r_st = L.rv
    r_ax = L.mrow
    r_gw = L.varrow
    r_os = L.srow
    pst = [nc.alloc_psum_tensor(tag + f"pstl{i}", [128, 128], F32) for i in range(2)]
    psq = [nc.alloc_psum_tensor(tag + f"psql{i}", [128, QT], F32) for i in range(2)]
    psst = nc.alloc_psum_tensor(tag + "psst", [2, 2 * QT], F32)
    psbc = nc.alloc_psum_tensor(tag + "psbcl", [128, 2 * QT], F32)


    p = p
    p.op("y", lambda e: e.dma_start(out=ids[:], in_=idd[:]), writes=("ids",), dma=True)
    p.op("y", lambda e: e.dma_start(out=n1[:], in_=n1d[:]), writes=("n1",), dma=True)
    p.op("y", lambda e: e.dma_start(out=n2[:], in_=n2d[:]), writes=("n2",), dma=True)
    p.op("y", lambda e: e.dma_start(out=acs[:], in_=acd[:]), writes=("acs",), dma=True)
    L.init_ops(p)
    p.op("v", lambda e: e.memset(onec[:], 1.0), writes=("onec",))
    p.op("v", lambda e: e.memset(eps8[:], 1e-8), writes=("eps8",))
    def expa(e):
        e.matmul(psbc[:, 0:1], L.onesr[:], acs[0:1, 0:1], start=True, stop=True)
        return e.matmul(psbc[:, 1:2], L.onesr[:], acs[0:1, 1:2], start=True, stop=True)
    p.op("t", expa, reads=("acs", "ln_onesr"), writes=("psbcl",))
    p.op("v", lambda e: e.tensor_copy(acol[:], psbc[:, 0:1]), reads=("psbcl",), writes=("acol",))
    p.op("v", lambda e: e.tensor_copy(bcol[:], psbc[:, 1:2]), reads=("psbcl",), writes=("bcol",))

    for wi, (wdap, nchunk, wqdst, wscc, cellix) in enumerate(
            ((w1d, PC, wq1, wsc1c, 0), (w2d, HC, wq2, wsc2c, 1))):
        wid = H2 if wi == 0 else D
        nh_ = max(1, wid // 2048)
        hw_ = wid // nh_
        first = True
        for s in range(nchunk):
            for hh in range(nh_):
                p.op("y", lambda e, s=s, wdap=wdap, hh=hh, hw_=hw_: e.dma_start(
                    out=h2f[:, 0:hw_],
                    in_=wdap[s * 128:(s + 1) * 128, hh * hw_:(hh + 1) * hw_]),
                    writes=("h2f",), dma=True, group=f"wls{wi}")
                p.op("v", lambda e, hw_=hw_: e.tensor_reduce(
                    redt[:], h2f[:, 0:hw_], AX.X, ALU.add, apply_absolute_value=True),
                    reads=("h2f",), writes=("redt",))
                if first:
                    p.op("v", lambda e: e.tensor_copy(absacc[:], redt[:]),
                         reads=("redt",), writes=("absacc",))
                    first = False
                else:
                    p.op("v", lambda e: e.tensor_add(absacc[:], absacc[:], redt[:]),
                         reads=("absacc", "redt"), writes=("absacc",))
        p.op("t", lambda e, cellix=cellix: e.matmul(
            psbc[0:1, 2 + cellix:3 + cellix], absacc[:],
            onec[:], start=True, stop=True),
             reads=("absacc", "onec"), writes=("psbcl",))
        p.op("v", lambda e, cellix=cellix: e.tensor_copy(
            cells[0:1, cellix:cellix + 1], psbc[0:1, 2 + cellix:3 + cellix]),
            reads=("psbcl",), writes=("cells",))
        nel = float(1024 * 4096)
        p.op("v", lambda e, cellix=cellix, nel=nel: e.tensor_scalar(
            cells[0:1, 2 + cellix:3 + cellix], cells[0:1, cellix:cellix + 1],
            1.0 / nel, 1e-5, ALU.mult, ALU.max), reads=("cells",), writes=("cells",))
        p.op("v", lambda e, cellix=cellix: e.reciprocal(
            cells[0:1, 4 + cellix:5 + cellix], cells[0:1, 2 + cellix:3 + cellix]),
            reads=("cells",), writes=("cells",))
        p.op("t", lambda e, cellix=cellix: e.matmul(
            psbc[:, 4 + cellix:5 + cellix], L.onesr[:],
            cells[0:1, 4 + cellix:5 + cellix], start=True, stop=True),
            reads=("cells", "ln_onesr"), writes=("psbcl",))
        p.op("v", lambda e, wscc=wscc, cellix=cellix: e.tensor_copy(
            wscc[:], psbc[:, 4 + cellix:5 + cellix]),
             reads=("psbcl",), writes=("wsc1c" if cellix == 0 else "wsc2c",))
        for s in range(nchunk):
            for hh in range(nh_):
                p.op("y", lambda e, s=s, wdap=wdap, hh=hh, hw_=hw_: e.dma_start(
                    out=h2f[:, 0:hw_],
                    in_=wdap[s * 128:(s + 1) * 128, hh * hw_:(hh + 1) * hw_]),
                    writes=("h2f",), dma=True, group=f"wls{wi}b")
                p.op("v", lambda e, hw_=hw_, wscc=wscc: e.tensor_scalar(
                    h2f[:, 0:hw_], h2f[:, 0:hw_], wscc[:, 0:1], None, ALU.mult),
                    reads=("h2f", "wsc1c" if wi == 0 else "wsc2c"), writes=("h2f",))
                for hf in range(max(1, hw_ // 1024)):
                    w0, w1 = hf * 1024, min(hw_, (hf + 1) * 1024)
                    emit_rneq(p, h2f[:, w0:w1], "h2f", h2i[:, 0:w1 - w0], "h2i",
                              hb1[:, 0:w1 - w0], "hb1", h2g[:, 0:w1 - w0], "h2g",
                              wqdst[:, s * wid + hh * hw_ + w0:
                                    s * wid + hh * hw_ + w1],
                              "wq1" if wi == 0 else "wq2", lo=-1.0, hi=1.0)

    for c in range(NCH):
        p.op("y", lambda e, c=c: e.dma_start(
            out=xs[:], in_=xd[:, c * QT:(c + 1) * QT]
            .rearrange("(pc p) t -> p pc t", p=128)), writes=("xs",), dma=True,
            group="ld")
        p.op("s", lambda e: e.activation(q1g[:], xs[:], AF.Square),
             reads=("xs",), writes=("q1g",))

        def st1(e):
            for k in range(PC):
                mm = e.matmul(psst[:, 0:QT], L.onesc[:], q1g[:, k * QT:(k + 1) * QT],
                              start=(k == 0), stop=(k == PC - 1))
            return mm
        p.op("t", st1, reads=("q1g", "ln_onesc"), writes=("psst",))
        p.op("v", lambda e: e.tensor_copy(r_st[:], psst[0:1, 0:QT]),
             reads=("psst",), writes=("ln_rv",))
        p.op("v", lambda e: e.tensor_scalar_mul(r_st[:], r_st[:], 1.0 / D),
             reads=("ln_rv",), writes=("ln_rv",))
        p.op("s", lambda e: e.activation(r_t[:], r_st[:], AF.Sqrt, bias=eps8[0:1, 0:1]),
             reads=("ln_rv", "eps8"), writes=("ln_rrow",))
        p.op("v", lambda e: e.reciprocal(r_st[:], r_t[:]), reads=("ln_rrow",), writes=("ln_rv",))
        p.op("v", lambda e: e.tensor_tensor(
            q1[:].rearrange("p (pc t) -> p pc t", pc=PC),
            xs[:].rearrange("p (pc t) -> p pc t", pc=PC),
            n1[:].unsqueeze(2).broadcast_to((128, PC, QT)), ALU.mult),
            reads=("xs", "n1"), writes=("q1",))
        for pc in range(PC):
            p.op("t", lambda e, pc=pc: e.transpose(
                pst[pc % 2][:], q1[:, pc * QT:(pc + 1) * QT], ids[:]),
                reads=("q1", "ids"), writes=(f"ptl{pc % 2}",))
            if pc == 0:
                p.op("v", lambda e: e.tensor_reduce(
                    amax[:], pst[0][:], AX.X, ALU.max, apply_absolute_value=True),
                    reads=("ptl0",), writes=("amax",))
            else:
                p.op("v", lambda e, pc=pc: e.tensor_reduce(
                    redt[:], pst[pc % 2][:], AX.X, ALU.max, apply_absolute_value=True),
                    reads=(f"ptl{pc % 2}",), writes=("redt",))
                p.op("v", lambda e: e.tensor_max(amax[:], amax[:], redt[:]),
                     reads=("amax", "redt"), writes=("amax",))
        p.op("t", lambda e: e.transpose(pst[0][0:1, 0:128], amax[:], ids[:]),
             reads=("amax", "ids"), writes=("ptl0",))
        p.op("v", lambda e: e.tensor_copy(r_am[:], pst[0][0:1, 0:128]),
             reads=("ptl0",), writes=("ln_rq",))
        p.op("v", lambda e: e.tensor_mul(r_ax[:], r_am[:], r_st[:]),
             reads=("ln_rq", "ln_rv"), writes=("ln_mrow",))
        p.op("v", lambda e: e.tensor_scalar_max(r_ax[:], r_ax[:], 1e-5),
             reads=("ln_mrow",), writes=("ln_mrow",))
        p.op("v", lambda e: e.reciprocal(r_t[:], r_ax[:]), reads=("ln_mrow",), writes=("ln_rrow",))
        p.op("v", lambda e: e.tensor_mul(r_gw[:], r_t[:], r_st[:]),
             reads=("ln_rrow", "ln_rv"), writes=("ln_var",))
        p.op("v", lambda e: e.tensor_scalar_mul(r_gw[:], r_gw[:], 127.0),
             reads=("ln_var",), writes=("ln_var",))
        p.op("v", lambda e: e.tensor_scalar(r_os[:], r_ax[:], cells[0:1, 2:3],
                                            1.0 / 127.0, ALU.mult, ALU.mult),
             reads=("ln_mrow", "cells"), writes=("ln_srow",))

        def bc1(e):
            e.matmul(psbc[:, 0:QT], L.onesr[:], r_gw[:], start=True, stop=True)
            return e.matmul(psbc[:, QT:2 * QT], L.onesr[:], r_os[:],
                            start=True, stop=True)
        p.op("t", bc1, reads=("ln_var", "ln_srow", "ln_onesr"), writes=("psbcl",))
        p.op("v", lambda e: e.tensor_copy(gwb[:], psbc[:, 0:QT]),
             reads=("psbcl",), writes=("ln_mb",))
        p.op("v", lambda e: e.tensor_copy(osb[:], psbc[:, QT:2 * QT]),
             reads=("psbcl",), writes=("ln_rb",))
        p.op("v", lambda e: e.tensor_tensor(
            q1[:].rearrange("p (pc t) -> p pc t", pc=PC),
            q1[:].rearrange("p (pc t) -> p pc t", pc=PC),
            gwb[:].unsqueeze(1).broadcast_to((128, PC, QT)), ALU.mult),
            reads=("q1", "ln_mb"), writes=("q1",))
        emit_rneq(p, q1[:], "q1", h2i[:, 0:PC * QT], "h2i",
                  h2f[:, 0:PC * QT], "h2f", q1g[:], "q1g", q1b[:], "q1b")
        for dd in range(HC):
            def m1(e, dd=dd):
                for k in range(PC):
                    mm = e.matmul(psq[dd % 2][:],
                                  wq1[:, k * H2 + dd * 128:k * H2 + (dd + 1) * 128],
                                  q1b[:, k * QT:(k + 1) * QT],
                                  start=(k == 0), stop=(k == PC - 1))
                return mm
            p.op("t", m1, reads=("wq1", "q1b"), writes=(f"pql{dd % 2}",))
            p.op("s", lambda e, dd=dd: e.activation(
                hb1[:, dd * QT:(dd + 1) * QT], psq[dd % 2][:], AF.Relu),
                reads=(f"pql{dd % 2}",), writes=("hb1",))
        p.op("v", lambda e: e.tensor_mul(h2g[:], hb1[:], hb1[:]),
             reads=("hb1",), writes=("h2g",))
        p.op("v", lambda e: e.tensor_mul(r_t[:], r_os[:], r_os[:]),
             reads=("ln_srow",), writes=("ln_rrow",))
        p.op("v", lambda e: e.tensor_scalar(r_t[:], r_t[:], acs[0:1, 0:1], None,
                                            ALU.mult),
             reads=("ln_rrow", "acs"), writes=("ln_rrow",))
        p.op("t", lambda e: e.matmul(psbc[:, 0:QT], L.onesr[:], r_t[:],
                                     start=True, stop=True),
             reads=("ln_rrow", "ln_onesr"), writes=("psbcl",))
        p.op("v", lambda e: e.tensor_copy(gwb[:], psbc[:, 0:QT]),
             reads=("psbcl",), writes=("ln_mb",))
        p.op("v", lambda e: e.tensor_tensor(
            hb1[:].rearrange("p (hc t) -> p hc t", hc=HC),
            h2g[:].rearrange("p (hc t) -> p hc t", hc=HC),
            gwb[:].unsqueeze(1).broadcast_to((128, HC, QT)), ALU.mult),
            reads=("h2g", "ln_mb"), writes=("hb1",))
        p.op("v", lambda e: e.tensor_scalar(hb1[:], hb1[:], bcol[:, 0:1], None,
                                            ALU.add),
             reads=("hb1", "bcol"), writes=("hb1",))
        p.op("s", lambda e: e.activation(h2g[:], hb1[:], AF.Square),
             reads=("hb1",), writes=("h2g",))

        def st2(e):
            for k in range(HC):
                mm = e.matmul(psst[:, QT:2 * QT], L.onesc[:],
                              h2g[:, k * QT:(k + 1) * QT],
                              start=(k == 0), stop=(k == HC - 1))
            return mm
        p.op("t", st2, reads=("h2g", "ln_onesc"), writes=("psst",))
        p.op("v", lambda e: e.tensor_copy(r_st[:], psst[0:1, QT:2 * QT]),
             reads=("psst",), writes=("ln_rv",))
        p.op("v", lambda e: e.tensor_scalar_mul(r_st[:], r_st[:], 1.0 / H2),
             reads=("ln_rv",), writes=("ln_rv",))
        p.op("s", lambda e: e.activation(r_t[:], r_st[:], AF.Sqrt, bias=eps8[0:1, 0:1]),
             reads=("ln_rv", "eps8"), writes=("ln_rrow",))
        p.op("v", lambda e: e.reciprocal(r_st[:], r_t[:]), reads=("ln_rrow",), writes=("ln_rv",))
        p.op("v", lambda e: e.tensor_tensor(
            h2g[:].rearrange("p (hc t) -> p hc t", hc=HC),
            hb1[:].rearrange("p (hc t) -> p hc t", hc=HC),
            n2[:].unsqueeze(2).broadcast_to((128, HC, QT)), ALU.mult),
            reads=("hb1", "n2"), writes=("h2g",))
        for hc in range(HC):
            p.op("t", lambda e, hc=hc: e.transpose(
                pst[hc % 2][:], h2g[:, hc * QT:(hc + 1) * QT], ids[:]),
                reads=("h2g", "ids"), writes=(f"ptl{hc % 2}",))
            if hc == 0:
                p.op("v", lambda e: e.tensor_reduce(
                    amax[:], pst[0][:], AX.X, ALU.max, apply_absolute_value=True),
                    reads=("ptl0",), writes=("amax",))
            else:
                p.op("v", lambda e, hc=hc: e.tensor_reduce(
                    redt[:], pst[hc % 2][:], AX.X, ALU.max, apply_absolute_value=True),
                    reads=(f"ptl{hc % 2}",), writes=("redt",))
                p.op("v", lambda e: e.tensor_max(amax[:], amax[:], redt[:]),
                     reads=("amax", "redt"), writes=("amax",))
        p.op("t", lambda e: e.transpose(pst[0][0:1, 0:128], amax[:], ids[:]),
             reads=("amax", "ids"), writes=("ptl0",))
        p.op("v", lambda e: e.tensor_copy(r_am[:], pst[0][0:1, 0:128]),
             reads=("ptl0",), writes=("ln_rq",))
        p.op("v", lambda e: e.tensor_mul(r_ax[:], r_am[:], r_st[:]),
             reads=("ln_rq", "ln_rv"), writes=("ln_mrow",))
        p.op("v", lambda e: e.tensor_scalar_max(r_ax[:], r_ax[:], 1e-5),
             reads=("ln_mrow",), writes=("ln_mrow",))
        p.op("v", lambda e: e.reciprocal(r_t[:], r_ax[:]), reads=("ln_mrow",), writes=("ln_rrow",))
        p.op("v", lambda e: e.tensor_mul(r_gw[:], r_t[:], r_st[:]),
             reads=("ln_rrow", "ln_rv"), writes=("ln_var",))
        p.op("v", lambda e: e.tensor_scalar_mul(r_gw[:], r_gw[:], 127.0),
             reads=("ln_var",), writes=("ln_var",))
        p.op("v", lambda e: e.tensor_scalar(r_os[:], r_ax[:], cells[0:1, 3:4],
                                            1.0 / 127.0, ALU.mult, ALU.mult),
             reads=("ln_mrow", "cells"), writes=("ln_srow",))

        def bc2(e):
            e.matmul(psbc[:, 0:QT], L.onesr[:], r_gw[:], start=True, stop=True)
            return e.matmul(psbc[:, QT:2 * QT], L.onesr[:], r_os[:],
                            start=True, stop=True)
        p.op("t", bc2, reads=("ln_var", "ln_srow", "ln_onesr"), writes=("psbcl",))
        p.op("v", lambda e: e.tensor_copy(gwb[:], psbc[:, 0:QT]),
             reads=("psbcl",), writes=("ln_mb",))
        p.op("v", lambda e: e.tensor_copy(osb[:], psbc[:, QT:2 * QT]),
             reads=("psbcl",), writes=("ln_rb",))
        p.op("v", lambda e: e.tensor_tensor(
            h2g[:].rearrange("p (hc t) -> p hc t", hc=HC),
            h2g[:].rearrange("p (hc t) -> p hc t", hc=HC),
            gwb[:].unsqueeze(1).broadcast_to((128, HC, QT)), ALU.mult),
            reads=("h2g", "ln_mb"), writes=("h2g",))
        HW4 = HC * QT // 4
        for half in range(4):
            emit_rneq(p, h2g[:, half * HW4:(half + 1) * HW4], "h2g",
                      h2i[:], "h2i", h2f[:, 0:HW4], "h2f",
                      hb1[:, 0:HW4], "hb1",
                      h2b[:, half * HW4:(half + 1) * HW4], "h2b")
        for dd in range(PC):
            def m2(e, dd=dd):
                for k in range(HC):
                    mm = e.matmul(psq[dd % 2][:],
                                  wq2[:, k * D + dd * 128:k * D + (dd + 1) * 128],
                                  h2b[:, k * QT:(k + 1) * QT],
                                  start=(k == 0), stop=(k == HC - 1))
                return mm
            p.op("t", m2, reads=("wq2", "h2b"), writes=(f"pql{dd % 2}",))
            p.op("v", lambda e, dd=dd: e.tensor_tensor(
                q1[:, dd * QT:(dd + 1) * QT], psq[dd % 2][:], osb[:], ALU.mult),
                reads=(f"pql{dd % 2}", "ln_rb"), writes=("q1",))
            p.op("v", lambda e, dd=dd: e.tensor_add(
                q1g[:, dd * QT:(dd + 1) * QT], q1[:, dd * QT:(dd + 1) * QT],
                xs[:, dd * QT:(dd + 1) * QT]),
                reads=("q1", "xs"), writes=("q1g",))
        emit_postln(p, L, q1g[:, 0:PC * QT], "q1g", xs[:], "xs")
        p.op("s", lambda e, c=c: e.dma_start(
            out=yd[:, c * QT:(c + 1) * QT].rearrange("(pc p) t -> p pc t", p=128),
            in_=xs[:]), reads=("xs",), writes=("ydone",), dma=True, group="st")



def build_l1f(T):
    nc = bass.Bass("TRN2", target_bir_lowering=False, dynamic_dma_scratch_size=8192)
    io = {}
    io["x"] = nc.dram_tensor("x", [D, T], F32, kind="ExternalInput").ap()
    io["nw1_pm"] = nc.dram_tensor("nw1_pm", [128, PC], F32, kind="ExternalInput").ap()
    io["nw2_pm"] = nc.dram_tensor("nw2_pm", [128, 32], F32, kind="ExternalInput").ap()
    io["w1T"] = nc.dram_tensor("w1T", [D, H2], F32, kind="ExternalInput").ap()
    io["w2T"] = nc.dram_tensor("w2T", [H2, D], F32, kind="ExternalInput").ap()
    io["acts"] = nc.dram_tensor("acts", [1, 2], F32, kind="ExternalInput").ap()
    io["ident"] = nc.dram_tensor("ident", [128, 128], F32, kind="ExternalInput").ap()
    io["y"] = nc.dram_tensor("y", [D, T], F32, kind="ExternalOutput").ap()
    p = Prog()
    emit_l1f(nc, p, io, T)
    p.op("y", lambda e: e.nop(), reads=("ydone",))
    p.emit(nc)
    return nc
def emit_l2a(nc, p, io, T, QT=512, tag=""):
    NCH = T // QT
    H = 16
    DI = 1024
    xd = io["x"]
    gmd = io["gam_pm"]
    wqd = io["wqkv"]
    wod = io["wout"]
    tpd = io["temp16"]
    hmd = io["hm16"]
    h64d = io["h64"]
    idd = io["ident"]
    yd = io["y"]

    wq = nc.alloc_sbuf_tensor(tag + "wq", [128, PC * 3 * DI], BF16)
    wo = nc.alloc_sbuf_tensor(tag + "wo", [128, PC * D], BF16)
    gm = nc.alloc_sbuf_tensor(tag + "gm", [128, PC], F32)
    tps = nc.alloc_sbuf_tensor(tag + "tps", [1, H], F32)
    hms = nc.alloc_sbuf_tensor(tag + "hms", [H, PC], F32)
    h64s = nc.alloc_sbuf_tensor(tag + "h64s", [H, 128], F32)
    ids = nc.alloc_sbuf_tensor(tag + "ids", [128, 128], F32)
    idb = nc.alloc_sbuf_tensor(tag + "idb", [128, 128], BF16)
    xs = nc.alloc_sbuf_tensor(tag + "xs", [128, PC * QT], F32)
    xnb = nc.alloc_sbuf_tensor(tag + "xnb", [128, PC * QT], BF16)
    qv = nc.alloc_sbuf_tensor(tag + "qv", [128, PC * QT], F32)
    kv = nc.alloc_sbuf_tensor(tag + "kv", [128, PC * QT], F32)
    sqs = nc.alloc_sbuf_tensor(tag + "sqs", [128, PC * QT], F32)
    vsb = nc.alloc_sbuf_tensor(tag + "vsb", [128, PC * QT], F32)
    vb16 = nc.alloc_sbuf_tensor(tag + "vb16", [128, PC * QT], BF16)
    qtb = nc.alloc_sbuf_tensor(tag + "qtb", [128, PC * QT], BF16)
    ktb = nc.alloc_sbuf_tensor(tag + "ktb", [128, PC * QT], BF16)
    simacc = nc.alloc_sbuf_tensor(tag + "simacc", [128, PC * 128], F32)
    attnb = nc.alloc_sbuf_tensor(tag + "attnb", [128, PC * 128], BF16)
    qss = nc.alloc_sbuf_tensor(tag + "qss", [128, PC], F32)
    kss = nc.alloc_sbuf_tensor(tag + "kss", [128, PC], F32)
    red1 = nc.alloc_sbuf_tensor(tag + "red1", [128, PC], F32)
    qsc = nc.alloc_sbuf_tensor(tag + "qsc", [128, PC], F32)
    ksc = nc.alloc_sbuf_tensor(tag + "ksc", [128, PC], F32)
    tcol = nc.alloc_sbuf_tensor(tag + "tcol", [128, PC], F32)
    t16 = nc.alloc_sbuf_tensor(tag + "t16", [H, 1], F32)
    r168 = nc.alloc_sbuf_tensor(tag + "r168", [H, PC], F32)
    smax = nc.alloc_sbuf_tensor(tag + "smax", [128, 1], F32)
    ssum = nc.alloc_sbuf_tensor(tag + "ssum", [128, 1], F32)
    nrow = nc.alloc_sbuf_tensor(tag + "nrow", [1, QT], F32)
    srow = nc.alloc_sbuf_tensor(tag + "srow2", [1, QT], F32)
    nb = nc.alloc_sbuf_tensor(tag + "nb", [128, QT], F32)
    pm1 = [nc.alloc_psum_tensor(tag + f"pm1{i}", [128, QT], F32) for i in range(2)]
    pstt = [nc.alloc_psum_tensor(tag + f"pstt{i}", [128, 128], F32) for i in range(2)]
    L = LNT(nc, QT, tag=tag, ext_vsq=sqs, ext_vsq_key="sqs")
    pss = L.psb2[:, 0:QT]


    p = p
    p.op("g", lambda e: e.dma_start(
        out=wq[:].rearrange("p (pc q) -> p pc q", pc=PC),
        in_=wqd.rearrange("(pc p) q -> p pc q", p=128)), writes=("wq",), dma=True)
    p.op("g", lambda e: e.dma_start(
        out=wo[:].rearrange("p (pc d) -> p pc d", pc=PC),
        in_=wod.rearrange("(pc p) d -> p pc d", p=128)), writes=("wo",), dma=True)
    for nm, dst, srcap in (("gm", gm, gmd), ("tps", tps, tpd), ("hms", hms, hmd),
                           ("h64s", h64s, h64d), ("ids", ids, idd)):
        p.op("y", lambda e, dst=dst, srcap=srcap: e.dma_start(out=dst[:], in_=srcap[:]),
             writes=(nm,), dma=True)
    L.init_ops(p)
    p.op("v", lambda e: e.tensor_copy(idb[:], ids[:]), reads=("ids",), writes=("idb",))
    p.op("v", lambda e: e.memset(simacc[:], 0.0), writes=("simacc",))
    p.op("v", lambda e: e.memset(qss[:], 0.0), writes=("qss",))
    p.op("v", lambda e: e.memset(kss[:], 0.0), writes=("kss",))

    def emit_xn(p):
        p.op("s", lambda e: e.activation(sqs[:], xs[:], AF.Square),
             reads=("xs",), writes=("sqs",))

        def stm(e):
            for k in range(PC):
                mm = e.matmul(L.psr[:, 0:QT], L.onesc[:], sqs[:, k * QT:(k + 1) * QT],
                              start=(k == 0), stop=(k == PC - 1))
            return mm
        p.op("t", stm, reads=("sqs", "ln_onesc"), writes=("ln_psr",))
        p.op("v", lambda e: e.tensor_copy(nrow[:], L.psr[0:1, 0:QT]),
             reads=("ln_psr",), writes=("nrow",))
        p.op("s", lambda e: e.activation(srow[:], nrow[:], AF.Sqrt),
             reads=("nrow",), writes=("srow2",))
        p.op("v", lambda e: e.tensor_scalar_max(srow[:], srow[:], 1e-12),
             reads=("srow2",), writes=("srow2",))
        p.op("v", lambda e: e.reciprocal(nrow[:], srow[:]),
             reads=("srow2",), writes=("nrow",))
        p.op("v", lambda e: e.tensor_scalar_mul(nrow[:], nrow[:], 32.0),
             reads=("nrow",), writes=("nrow",))
        p.op("t", lambda e: e.matmul(pss, L.onesr[:], nrow[:],
                                     start=True, stop=True),
             reads=("nrow", "ln_onesr"), writes=("ln_psb2",))
        p.op("v", lambda e: e.tensor_copy(nb[:], pss),
             reads=("ln_psb2",), writes=("nb",))
        p.op("v", lambda e: e.tensor_tensor(
            sqs[:].rearrange("p (pc t) -> p pc t", pc=PC),
            xs[:].rearrange("p (pc t) -> p pc t", pc=PC),
            nb[:].unsqueeze(1).broadcast_to((128, PC, QT)), ALU.mult),
            reads=("xs", "nb"), writes=("sqs",))
        p.op("v", lambda e: e.tensor_tensor(
            xnb[:].rearrange("p (pc t) -> p pc t", pc=PC),
            sqs[:].rearrange("p (pc t) -> p pc t", pc=PC),
            gm[:].unsqueeze(2).broadcast_to((128, PC, QT)), ALU.mult),
            reads=("sqs", "gm"), writes=("xnb",))

    def emit_proj(p, off, out_t, okey):
        for dd in range(PC):
            def qm(e, dd=dd, off=off):
                for k in range(PC):
                    mm = e.matmul(pm1[dd % 2][:],
                                  wq[:, k * 3 * DI + off + dd * 128:
                                     k * 3 * DI + off + (dd + 1) * 128],
                                  xnb[:, k * QT:(k + 1) * QT],
                                  start=(k == 0), stop=(k == PC - 1))
                return mm
            p.op("t", qm, reads=("wq", "xnb"), writes=(f"pm{dd % 2}",))
            p.op("v", lambda e, dd=dd, out_t=out_t, okey=okey: e.tensor_copy(
                out_t[:, dd * QT:(dd + 1) * QT], pm1[dd % 2][:]),
                reads=(f"pm{dd % 2}",), writes=(okey,))

    nTB = QT // 128
    for c in range(NCH):
        p.op("y", lambda e, c=c: e.dma_start(
            out=xs[:], in_=xd[:, c * QT:(c + 1) * QT]
            .rearrange("(pc p) t -> p pc t", p=128)), writes=("xs",), dma=True,
            group="ld")
        emit_xn(p)
        emit_proj(p, 0, qv, "qv")
        emit_proj(p, DI, kv, "kv")
        for src_t, acc, skey, akey in ((qv, qss, "qv", "qss"), (kv, kss, "kv", "kss")):
            p.op("s", lambda e, src_t=src_t, skey=skey: e.activation(
                sqs[:], src_t[:], AF.Square), reads=(skey,), writes=("sqs",))
            for pc in range(PC):
                p.op("v", lambda e, pc=pc: e.tensor_reduce(
                    red1[:, pc:pc + 1], sqs[:, pc * QT:(pc + 1) * QT], AX.X, ALU.add),
                    reads=("sqs",), writes=("red1",))
            p.op("v", lambda e, acc=acc, akey=akey: e.tensor_add(acc[:], acc[:], red1[:]),
                 reads=(akey, "red1"), writes=(akey,))
        for src_t, dstb, skey, dkey in ((qv, qtb, "qv", "qtb"), (kv, ktb, "kv", "ktb")):
            p.op("v", lambda e, src_t=src_t: e.tensor_copy(vb16[:], src_t[:]),
                 reads=(skey,), writes=("vb16",))
            for j in range(PC * nTB):
                p.op("t", lambda e, j=j: e.transpose(
                    pstt[j % 2][:].bitcast(BF16)[:, 0:128],
                    vb16[:, j * 128:(j + 1) * 128], idb[:]),
                    reads=("vb16", "idb"), writes=(f"pstt{j % 2}",))
                p.op("v", lambda e, j=j, dstb=dstb, dkey=dkey: e.tensor_copy(
                    dstb[:, j * 128:(j + 1) * 128],
                    pstt[j % 2][:].bitcast(BF16)[:, 0:128]),
                    reads=(f"pstt{j % 2}",), writes=(dkey,))
        for pc in range(PC):
            def smm(e, pc=pc):
                for tb in range(nTB):
                    j = pc * nTB + tb
                    mm = e.matmul(pstt[0][:],
                                  qtb[:, j * 128:(j + 1) * 128],
                                  ktb[:, j * 128:(j + 1) * 128],
                                  start=(tb == 0), stop=(tb == nTB - 1))
                return mm
            p.op("t", smm, reads=("qtb", "ktb"), writes=("pstt0",))
            p.op("v", lambda e, pc=pc: e.tensor_tensor(
                simacc[:, pc * 128:(pc + 1) * 128], simacc[:, pc * 128:(pc + 1) * 128],
                pstt[0][:], ALU.add), reads=("simacc", "pstt0"), writes=("simacc",))

    p.op("s", lambda e: e.activation(tps[:], tps[:], AF.Exp),
         reads=("tps",), writes=("tps",))
    p.op("t", lambda e: e.transpose(pstt[0][0:H, 0:1], tps[:], ids[0:1, 0:1]),
         reads=("tps", "ids"), writes=("pstt0",))
    p.op("v", lambda e: e.tensor_copy(t16[:], pstt[0][0:H, 0:1]),
         reads=("pstt0",), writes=("t16",))
    for acc, dst, akey, dkey in ((qss, qsc, "qss", "qsc"), (kss, ksc, "kss", "ksc")):
        p.op("s", lambda e, acc=acc, dst=dst, akey=akey: e.activation(
            dst[:], acc[:], AF.Sqrt), reads=(akey,), writes=(dkey,))
        p.op("v", lambda e, dst=dst, dkey=dkey: e.tensor_scalar_max(
            dst[:], dst[:], 1e-12), reads=(dkey,), writes=(dkey,))
        p.op("v", lambda e, dst=dst, dkey=dkey: e.reciprocal(dst[:], dst[:]),
             reads=(dkey,), writes=(dkey,))
    p.op("v", lambda e: e.tensor_tensor(r168[:], hms[:],
                                        t16[:].broadcast_to((H, PC)), ALU.mult),
         reads=("hms", "t16"), writes=("r168",))
    p.op("t", lambda e: e.matmul(pstt[0][:, 0:PC], h64s[:], r168[:],
                                 start=True, stop=True),
         reads=("h64s", "r168"), writes=("pstt0",))
    p.op("v", lambda e: e.tensor_copy(tcol[:], pstt[0][:, 0:PC]),
         reads=("pstt0",), writes=("tcol",))
    p.op("v", lambda e: e.tensor_scalar_mul(tcol[:], tcol[:], 8.0),
         reads=("tcol",), writes=("tcol",))
    p.op("v", lambda e: e.tensor_mul(qsc[:], qsc[:], tcol[:]),
         reads=("qsc", "tcol"), writes=("qsc",))
    for pc in range(PC):
        p.op("v", lambda e, pc=pc: e.tensor_scalar(
            simacc[:, pc * 128:(pc + 1) * 128], simacc[:, pc * 128:(pc + 1) * 128],
            qsc[:, pc:pc + 1], None, ALU.mult),
            reads=("simacc", "qsc"), writes=("simacc",))
        p.op("t", lambda e, pc=pc: e.transpose(
            pstt[0][:], simacc[:, pc * 128:(pc + 1) * 128], ids[:]),
            reads=("simacc", "ids"), writes=("pstt0",))
        p.op("v", lambda e, pc=pc: e.scalar_tensor_tensor(
            vsb[:, 0:128], pstt[0][:], ksc[:, pc:pc + 1], pstt[0][:],
            ALU.mult, ALU.bypass) if False else e.tensor_scalar(
            vsb[:, 0:128], pstt[0][:], ksc[:, pc:pc + 1], None, ALU.mult),
            reads=("pstt0", "ksc"), writes=("vsb",))
        p.op("t", lambda e, pc=pc: e.transpose(pstt[1][:], vsb[:, 0:128], ids[:]),
             reads=("vsb", "ids"), writes=("pstt1",))
        p.op("v", lambda e, pc=pc: e.tensor_copy(
            simacc[:, pc * 128:(pc + 1) * 128], pstt[1][:]),
            reads=("pstt1",), writes=("simacc",))
        for i in range(2):
            sl = simacc[i * 64:(i + 1) * 64, pc * 128 + i * 64:pc * 128 + (i + 1) * 64]
            p.op("v", lambda e, sl=sl, i=i: e.reduce_max(
                smax[i * 64:(i + 1) * 64, 0:1], sl, AX.X),
                reads=("simacc",), writes=("smax",))
            p.op("v", lambda e, sl=sl, i=i: e.tensor_scalar(
                sl, sl, smax[i * 64:(i + 1) * 64, 0:1], None, ALU.subtract),
                reads=("simacc", "smax"), writes=("simacc",))
            p.op("s", lambda e, sl=sl: e.activation(sl, sl, AF.Exp),
                 reads=("simacc",), writes=("simacc",))
            p.op("v", lambda e, sl=sl, i=i: e.reduce_sum(
                ssum[i * 64:(i + 1) * 64, 0:1], sl, AX.X),
                reads=("simacc",), writes=("ssum",))
            p.op("v", lambda e, i=i: e.reciprocal(
                ssum[i * 64:(i + 1) * 64, 0:1], ssum[i * 64:(i + 1) * 64, 0:1]),
                reads=("ssum",), writes=("ssum",))
            p.op("v", lambda e, sl=sl, i=i: e.tensor_scalar(
                sl, sl, ssum[i * 64:(i + 1) * 64, 0:1], None, ALU.mult),
                reads=("simacc", "ssum"), writes=("simacc",))
        p.op("v", lambda e, pc=pc: e.tensor_copy(
            vb16[:, 0:128], simacc[:, pc * 128:(pc + 1) * 128]),
            reads=("simacc",), writes=("vb16",))
        p.op("t", lambda e, pc=pc: e.transpose(
            pstt[0][:].bitcast(BF16)[:, 0:128], vb16[:, 0:128], idb[:]),
            reads=("vb16", "idb"), writes=("pstt0",))
        p.op("v", lambda e, pc=pc: e.tensor_copy(
            attnb[:, pc * 128:(pc + 1) * 128], pstt[0][:].bitcast(BF16)[:, 0:128]),
            reads=("pstt0",), writes=("attnb",))

    for c in range(NCH):
        p.op("y", lambda e, c=c: e.dma_start(
            out=xs[:], in_=xd[:, c * QT:(c + 1) * QT]
            .rearrange("(pc p) t -> p pc t", p=128)), writes=("xs",), dma=True,
            group="ld2")
        emit_xn(p)
        emit_proj(p, 2 * DI, qv, "qv")
        p.op("v", lambda e: e.tensor_copy(vb16[:], qv[:]), reads=("qv",), writes=("vb16",))
        for pc in range(PC):
            for i in range(2):
                def am(e, pc=pc, i=i):
                    return e.matmul(
                        pm1[pc % 2][i * 64:(i + 1) * 64, :],
                        attnb[i * 64:(i + 1) * 64,
                              pc * 128 + i * 64:pc * 128 + (i + 1) * 64],
                        vb16[i * 64:(i + 1) * 64, pc * QT:(pc + 1) * QT],
                        start=True, stop=True)
                p.op("t", am, reads=("attnb", "vb16"), writes=(f"pm{pc % 2}_{i}",))
            p.op("v", lambda e, pc=pc: e.tensor_copy(
                kv[:, pc * QT:(pc + 1) * QT], pm1[pc % 2][:]),
                reads=(f"pm{pc % 2}_0", f"pm{pc % 2}_1"), writes=("kv",))
        p.op("v", lambda e: e.tensor_copy(ktb[:], kv[:]), reads=("kv",), writes=("ktb",))
        for dd in range(PC):
            def om(e, dd=dd):
                for k in range(PC):
                    mm = e.matmul(pm1[dd % 2][:],
                                  wo[:, k * D + dd * 128:k * D + (dd + 1) * 128],
                                  ktb[:, k * QT:(k + 1) * QT],
                                  start=(k == 0), stop=(k == PC - 1))
                return mm
            p.op("t", om, reads=("wo", "ktb"), writes=(f"pm{dd % 2}",))
            p.op("v", lambda e, dd=dd: e.tensor_tensor(
                vsb[:, dd * QT:(dd + 1) * QT], pm1[dd % 2][:],
                xs[:, dd * QT:(dd + 1) * QT], ALU.add),
                reads=(f"pm{dd % 2}", "xs"), writes=("vsb",))
        emit_postln(p, L, vsb, "vsb", xs[:], "xs")
        p.op("s", lambda e, c=c: e.dma_start(
            out=yd[:, c * QT:(c + 1) * QT].rearrange("(pc p) t -> p pc t", p=128),
            in_=xs[:]), reads=("xs",), writes=("ydone",), dma=True, group="st")




def build_l2a(T, QT=512):
    nc = bass.Bass("TRN2", target_bir_lowering=False)
    io = {}
    io["x"] = nc.dram_tensor("x", [D, T], F32, kind="ExternalInput").ap()
    io["gam_pm"] = nc.dram_tensor("gam_pm", [128, PC], F32, kind="ExternalInput").ap()
    io["wqkv"] = nc.dram_tensor("wqkv", [D, 3 * DI], F32, kind="ExternalInput").ap()
    io["wout"] = nc.dram_tensor("wout", [DI, D], F32, kind="ExternalInput").ap()
    io["temp16"] = nc.dram_tensor("temp16", [1, H], F32, kind="ExternalInput").ap()
    io["hm16"] = nc.dram_tensor("hm16", [H, PC], F32, kind="ExternalInput").ap()
    io["h64"] = nc.dram_tensor("h64", [H, 128], F32, kind="ExternalInput").ap()
    io["ident"] = nc.dram_tensor("ident", [128, 128], F32, kind="ExternalInput").ap()
    io["y"] = nc.dram_tensor("y", [D, T], F32, kind="ExternalOutput").ap()
    p = Prog()
    emit_l2a(nc, p, io, T, QT)
    p.op("y", lambda e: e.nop(), reads=("ydone",))
    p.emit(nc)
    return nc
def emit_l0a(nc, p, io, T, tag=""):
    QT = 128
    NCH = T // QT
    NH, HD, DS = 32, 64, 128
    DIP = 4384
    DIN = 2048
    CC = 18
    xd = io["x"]
    wipd = io["wipT"]
    wopd = io["wopT"]
    cwd = io["cw_pm"]
    cbd = io["cb_pm"]
    dtbd = io["dtb_col"]
    ald = io["alog_row"]
    dpd = io["d_pm"]
    nwd = io["nw_pm"]
    mkd = io["maskS"]
    sbd = io["selbig"]
    sod = io["selone"]
    idd = io["ident"]
    yd = io["y"]

    wip = nc.alloc_sbuf_tensor(tag + "wip", [128, PC * DIP], BF16)
    wop = nc.alloc_sbuf_tensor(tag + "wop", [128, 16 * D], BF16)
    cw = nc.alloc_sbuf_tensor(tag + "cw", [128, CC * 4], F32)
    cb = nc.alloc_sbuf_tensor(tag + "cb", [128, CC], F32)
    dtb = nc.alloc_sbuf_tensor(tag + "dtb", [NH, 1], F32)
    alr = nc.alloc_sbuf_tensor(tag + "alr", [1, 96], F32)
    acol = nc.alloc_sbuf_tensor(tag + "acol", [NH, 1], F32)
    dpm = nc.alloc_sbuf_tensor(tag + "dpm", [128, 16], F32)
    nwm = nc.alloc_sbuf_tensor(tag + "nwm", [128, 16], F32)
    mks = nc.alloc_sbuf_tensor(tag + "mks", [128, 128], BF16)
    selb = nc.alloc_sbuf_tensor(tag + "selb", [NH, 16 * 128], BF16)
    ebf = nc.alloc_sbuf_tensor(tag + "ebf", [NH, 128], BF16)
    selo = nc.alloc_sbuf_tensor(tag + "selo", [NH, NH * 128], F32)
    ids = nc.alloc_sbuf_tensor(tag + "ids", [128, 128], F32)
    idb = nc.alloc_sbuf_tensor(tag + "idb", [128, 128], BF16)
    onecell = nc.alloc_sbuf_tensor(tag + "onecell", [1, 1], F32)
    onec32 = nc.alloc_sbuf_tensor(tag + "onec32", [32, 1], F32)
    xs = nc.alloc_sbuf_tensor(tag + "xs", [128, PC * QT], F32)
    xb = nc.alloc_sbuf_tensor(tag + "xb", [128, PC * QT], BF16)
    zs = nc.alloc_sbuf_tensor(tag + "zs", [128, 16 * QT], F32)
    xbce = nc.alloc_sbuf_tensor(tag + "xbce", [128, CC * (QT + 3)], F32)
    xbcs = nc.alloc_sbuf_tensor(tag + "xbcs", [128, CC * QT], BF16)
    convt = nc.alloc_sbuf_tensor(tag + "convt", [128, QT], F32)
    dtraw = nc.alloc_sbuf_tensor(tag + "dtraw", [NH, QT], F32)
    dtt = nc.alloc_sbuf_tensor(tag + "dtt", [NH, QT], F32)
    dts = nc.alloc_sbuf_tensor(tag + "dts", [NH, QT], F32)
    La = nc.alloc_sbuf_tensor(tag + "La", [NH, QT], F32)
    eLa = nc.alloc_sbuf_tensor(tag + "eLa", [NH, QT], F32)
    eEnd = nc.alloc_sbuf_tensor(tag + "eEnd", [NH, QT], F32)
    LaT = nc.alloc_sbuf_tensor(tag + "LaT", [128, NH], F32)
    dteT = nc.alloc_sbuf_tensor(tag + "dteT", [128, NH], F32)
    dtT = nc.alloc_sbuf_tensor(tag + "dtT", [128, NH], F32)
    E32 = nc.alloc_sbuf_tensor(tag + "E32", [128, NH], F32)
    e32r = alr[0:1, 0:NH]
    XTb = nc.alloc_sbuf_tensor(tag + "XTb", [128, 16 * QT], BF16)
    BTb = nc.alloc_sbuf_tensor(tag + "BTb", [128, QT], BF16)
    CBTm = nc.alloc_sbuf_tensor(tag + "CBTm", [128, QT], BF16)
    Esb = nc.alloc_sbuf_tensor(tag + "Esb", [128, 16 * QT], F32)
    ysb = nc.alloc_sbuf_tensor(tag + "ysb", [128, 16 * QT], F32)
    Wg = nc.alloc_sbuf_tensor(tag + "Wg", [128, 16 * QT], BF16)
    hst = nc.alloc_sbuf_tensor(tag + "hst", [128, NH * HD], F32)
    hbf = nc.alloc_sbuf_tensor(tag + "hbf", [128, NH * HD], BF16)
    xds = nc.alloc_sbuf_tensor(tag + "xds", [128, NH * HD], BF16)
    silz = nc.alloc_sbuf_tensor(tag + "silz", [128, 16 * QT], F32)
    pm1 = [nc.alloc_psum_tensor(tag + f"pml{i}", [128, QT], F32) for i in range(2)]
    psW = nc.alloc_psum_tensor(tag + "psW", [128, 16 * QT], F32)
    L = LNT(nc, QT, tag=tag, ext_vsq=silz[:, 0:PC * QT], ext_vsq_key="silz")
    vsb = silz[:, PC * QT:2 * PC * QT]
    rmsr = L.rv
    rmsb = L.mb


    p = p
    p.op("g", lambda e: e.dma_start(
        out=wip[:].rearrange("p (pc q) -> p pc q", pc=PC),
        in_=wipd.rearrange("(pc p) q -> p pc q", p=128)), writes=("wip",), dma=True)
    for hf in range(2):
        p.op("g", lambda e, hf=hf: e.dma_start(
            out=wop[:, hf * 8 * D:(hf + 1) * 8 * D]
            .rearrange("p (kk d) -> p kk d", kk=8),
            in_=wopd[hf * 8 * 128:(hf + 1) * 8 * 128, :]
            .rearrange("(kk p) d -> p kk d", p=128)), writes=("wop",), dma=True)
    for nm, dst, srcap in (("cw", cw, cwd), ("cb", cb, cbd), ("dtb", dtb, dtbd),
                           ("alr", alr, ald), ("dpm", dpm, dpd), ("nwm", nwm, nwd),
                           ("selo", selo, sod),
                           ("ids", ids, idd)):
        p.op("y", lambda e, dst=dst, srcap=srcap: e.dma_start(out=dst[:], in_=srcap[:]),
             writes=(nm,), dma=True)
    p.op("g", lambda e: e.dma_start(out=mks[:], in_=mkd[:]), writes=("mks",), dma=True)
    p.op("g", lambda e: e.dma_start(out=selb[:], in_=sbd[:]), writes=("selb",), dma=True)
    L.init_ops(p)
    p.op("v", lambda e: e.memset(onecell[:], 1.0), writes=("onecell",))
    p.op("v", lambda e: e.memset(onec32[:], 1.0), writes=("onec32",))
    p.op("v", lambda e: e.memset(xbce[:], 0.0), writes=("xbce",))
    p.op("v", lambda e: e.memset(hst[:], 0.0), writes=("hst",))
    p.op("v", lambda e: e.tensor_copy(idb[:], ids[:]), reads=("ids",), writes=("idb",))
    p.op("s", lambda e: e.activation(alr[:], alr[:], AF.Exp),
         reads=("alr",), writes=("alr",))
    p.op("v", lambda e: e.tensor_add(alr[0:1, 0:32], alr[0:1, 0:32], alr[0:1, 32:64]),
         reads=("alr",), writes=("alr",))
    p.op("v", lambda e: e.tensor_add(alr[0:1, 0:32], alr[0:1, 0:32], alr[0:1, 64:96]),
         reads=("alr",), writes=("alr",))
    p.op("v", lambda e: e.tensor_scalar_mul(alr[0:1, 0:32], alr[0:1, 0:32], -1.0 / 3.0),
         reads=("alr",), writes=("alr",))
    p.op("t", lambda e: e.transpose(pm1[0][0:32, 0:1], alr[0:1, 0:32], ids[0:1, 0:1]),
         reads=("alr", "ids"), writes=("pm0",))
    p.op("v", lambda e: e.tensor_copy(acol[:], pm1[0][0:32, 0:1]),
         reads=("pm0",), writes=("acol",))
    for kk in range(16):
        p.op("v", lambda e, kk=kk: e.tensor_scalar(
            wop[:, kk * D:(kk + 1) * D], wop[:, kk * D:(kk + 1) * D],
            nwm[:, kk:kk + 1], None, ALU.mult), reads=("wop", "nwm"), writes=("wop",))

    for c in range(NCH):
        p.op("y", lambda e, c=c: e.dma_start(
            out=xs[:], in_=xd[:, c * QT:(c + 1) * QT]
            .rearrange("(pc p) t -> p pc t", p=128)), writes=("xs",), dma=True,
            group="ld")
        p.op("v", lambda e: e.tensor_copy(xb[:], xs[:]), reads=("xs",), writes=("xb",))
        p.op("v", lambda e: e.tensor_copy(
            xbce[:].rearrange("p (cc t) -> p cc t", cc=CC)[:, :, QT:QT + 3],
            xbce[:].rearrange("p (cc t) -> p cc t", cc=CC)[:, :, 0:3]) if False
            else e.tensor_copy(
            xbce[:].rearrange("p (cc t) -> p cc t", cc=CC)[:, :, 0:3],
            xbce[:].rearrange("p (cc t) -> p cc t", cc=CC)[:, :, QT:QT + 3]),
            reads=("xbce",), writes=("xbce",))
        for d in range(35):
            dw = 128 if d < 34 else 32
            def ipm(e, d=d, dw=dw):
                for k in range(PC):
                    mm = e.matmul(pm1[d % 2][0:dw, :],
                                  wip[:, k * DIP + d * 128:k * DIP + d * 128 + dw],
                                  xb[:, k * QT:(k + 1) * QT],
                                  start=(k == 0), stop=(k == PC - 1))
                return mm
            p.op("t", ipm, reads=("wip", "xb"), writes=(f"pm{d % 2}",))
            if d < 16:
                p.op("v", lambda e, d=d: e.tensor_copy(
                    zs[:, d * QT:(d + 1) * QT], pm1[d % 2][:]),
                    reads=(f"pm{d % 2}",), writes=("zs",))
            elif d < 34:
                cc = d - 16
                p.op("v", lambda e, d=d, cc=cc: e.tensor_copy(
                    xbce[:, cc * (QT + 3) + 3:cc * (QT + 3) + 3 + QT], pm1[d % 2][:]),
                    reads=(f"pm{d % 2}",), writes=("xbce",))
            else:
                p.op("v", lambda e, d=d: e.tensor_copy(
                    dtraw[:], pm1[d % 2][0:32, :]), reads=(f"pm{d % 2}",),
                    writes=("dtraw",))
        for cc in range(CC):
            p.op("v", lambda e, cc=cc: e.tensor_scalar(
                convt[:], xbce[:, cc * (QT + 3):cc * (QT + 3) + QT],
                cw[:, cc * 4:cc * 4 + 1], None, ALU.mult),
                reads=("xbce", "cw"), writes=("convt",))
            for kk in range(1, 4):
                p.op("v", lambda e, cc=cc, kk=kk: e.scalar_tensor_tensor(
                    convt[:], xbce[:, cc * (QT + 3) + kk:cc * (QT + 3) + kk + QT],
                    cw[:, cc * 4 + kk:cc * 4 + kk + 1], convt[:], ALU.mult, ALU.add),
                    reads=("xbce", "cw", "convt"), writes=("convt",))
            p.op("s", lambda e, cc=cc: e.activation(
                xbcs[:, cc * QT:(cc + 1) * QT], convt[:], AF.Silu,
                bias=cb[:, cc:cc + 1]), reads=("convt", "cb"), writes=("xbcs",))
        p.op("s", lambda e: e.activation(dtt[:], dtraw[:], AF.Abs, bias=dtb[:]),
             reads=("dtraw", "dtb"), writes=("dtt",))
        p.op("s", lambda e: e.activation(dtt[:], dtt[:], AF.Exp, scale=-1.0),
             reads=("dtt",), writes=("dtt",))
        p.op("s", lambda e: e.activation(dtt[:], dtt[:], AF.Ln, bias=onec32[:]),
             reads=("dtt", "onec32"), writes=("dtt",))
        p.op("s", lambda e: e.activation(dts[:], dtraw[:], AF.Relu, bias=dtb[:]),
             reads=("dtraw", "dtb"), writes=("dts",))
        p.op("v", lambda e: e.tensor_add(dts[:], dts[:], dtt[:]),
             reads=("dts", "dtt"), writes=("dts",))
        p.op("v", lambda e: e.tensor_scalar(dtt[:], dts[:], acol[:], None, ALU.mult),
             reads=("dts", "acol"), writes=("dtt",))
        p.op("v", lambda e: e.tensor_tensor_scan(La[:], dtt[:], dtt[:], 0.0,
                                                 ALU.add, ALU.bypass),
             reads=("dtt",), writes=("La",))
        p.op("s", lambda e: e.activation(eLa[:], La[:], AF.Exp),
             reads=("La",), writes=("eLa",))
        p.op("v", lambda e: e.tensor_scalar(eEnd[:], La[:], La[:, QT - 1:QT], None,
                                            ALU.subtract),
             reads=("La",), writes=("eEnd",))
        p.op("s", lambda e: e.activation(eEnd[:], eEnd[:], AF.Exp, scale=-1.0),
             reads=("eEnd",), writes=("eEnd",))
        p.op("v", lambda e: e.tensor_mul(eEnd[:], eEnd[:], dts[:]),
             reads=("eEnd", "dts"), writes=("eEnd",))
        p.op("t", lambda e: e.transpose(pm1[0][:, 0:NH], La[:], ids[0:32, 0:32]),
             reads=("La", "ids"), writes=("pm0",))
        p.op("v", lambda e: e.tensor_copy(LaT[:], pm1[0][:, 0:NH]),
             reads=("pm0",), writes=("LaT",))
        p.op("t", lambda e: e.transpose(pm1[1][:, 0:NH], eEnd[:], ids[0:32, 0:32]),
             reads=("eEnd", "ids"), writes=("pm1",))
        p.op("v", lambda e: e.tensor_copy(dteT[:], pm1[1][:, 0:NH]),
             reads=("pm1",), writes=("dteT",))
        p.op("t", lambda e: e.transpose(pm1[0][:, 0:NH], dts[:], ids[0:32, 0:32]),
             reads=("dts", "ids"), writes=("pm0",))
        p.op("v", lambda e: e.tensor_copy(dtT[:], pm1[0][:, 0:NH]),
             reads=("pm0",), writes=("dtT",))
        p.op("t", lambda e: e.matmul(pm1[0][:], xbcs[:, 16 * QT:17 * QT],
                                     xbcs[:, 17 * QT:18 * QT], start=True, stop=True),
             reads=("xbcs",), writes=("pm0",))
        p.op("v", lambda e: e.tensor_tensor(CBTm[:], pm1[0][:], mks[:], ALU.mult),
             reads=("pm0", "mks"), writes=("CBTm",))
        for j in range(16):
            p.op("t", lambda e, j=j: e.transpose(
                pm1[j % 2][:].bitcast(BF16)[:, 0:128],
                xbcs[:, j * QT:(j + 1) * QT], idb[:]),
                reads=("xbcs", "idb"), writes=(f"pm{j % 2}",))
            p.op("v", lambda e, j=j: e.tensor_copy(
                XTb[:, j * QT:(j + 1) * QT], pm1[j % 2][:].bitcast(BF16)[:, 0:128]),
                reads=(f"pm{j % 2}",), writes=("XTb",))
        p.op("v", lambda e: e.tensor_copy(hbf[:], hst[:]), reads=("hst",), writes=("hbf",))
        p.op("v", lambda e: e.tensor_copy(ebf[:], eLa[:]), reads=("eLa",), writes=("ebf",))

        def emm(e):
            for pc2 in range(16):
                mm = e.matmul(psW[:, pc2 * QT:(pc2 + 1) * QT],
                              selb[:, pc2 * 128:(pc2 + 1) * 128], ebf[:],
                              start=True, stop=True)
            return mm
        p.op("t", emm, reads=("selb", "ebf"), writes=("psW",))
        p.op("s", lambda e: e.activation(Esb[:], psW[:], AF.Copy),
             reads=("psW",), writes=("Esb",))
        for gq in range(2):
            def lrm(e, gq=gq):
                for i in range(16):
                    h = gq * 16 + i
                    mm = e.matmul(psW[:, i * QT:(i + 1) * QT],
                                  selo[:, h * 128:(h + 1) * 128], La[:],
                                  start=True, stop=True)
                return mm
            p.op("t", lrm, reads=("selo", "La"), writes=("psW",))
            p.op("v", lambda e, gq=gq: e.tensor_tensor(
                silz[:].rearrange("p (g t) -> p g t", g=16),
                psW[:].rearrange("p (g t) -> p g t", g=16),
                LaT[:, gq * 16:(gq + 1) * 16].unsqueeze(2).broadcast_to((128, 16, QT)),
                ALU.min), reads=("psW", "LaT"), writes=("silz",))
            p.op("v", lambda e, gq=gq: e.tensor_tensor(
                silz[:].rearrange("p (g t) -> p g t", g=16),
                silz[:].rearrange("p (g t) -> p g t", g=16),
                LaT[:, gq * 16:(gq + 1) * 16].unsqueeze(2).broadcast_to((128, 16, QT)),
                ALU.subtract), reads=("silz", "LaT"), writes=("silz",))
            p.op("s", lambda e: e.activation(Wg[:], silz[:], AF.Exp),
                 reads=("silz",), writes=("Wg",))
            p.op("v", lambda e, gq=gq: e.tensor_tensor(
                Wg[:].rearrange("p (g t) -> p g t", g=16),
                Wg[:].rearrange("p (g t) -> p g t", g=16),
                dtT[:, gq * 16:(gq + 1) * 16].unsqueeze(2).broadcast_to((128, 16, QT)),
                ALU.mult), reads=("Wg", "dtT"), writes=("Wg",))
            p.op("v", lambda e: e.tensor_tensor(
                Wg[:].rearrange("p (g t) -> p g t", g=16),
                Wg[:].rearrange("p (g t) -> p g t", g=16),
                CBTm[:].unsqueeze(1).broadcast_to((128, 16, QT)), ALU.mult),
                reads=("Wg", "CBTm"), writes=("Wg",))
            def ymm(e, gq=gq):
                for i in range(16):
                    h = gq * 16 + i
                    sl = psW[(h % 2) * 64:(h % 2) * 64 + 64,
                             (h // 2) * QT:(h // 2) * QT + QT]
                    mm = e.matmul(sl, XTb[:, h * 64:(h + 1) * 64],
                                  Wg[:, i * QT:(i + 1) * QT], start=True, stop=True)
                return mm
            p.op("t", ymm, reads=("XTb", "Wg", "ysbF" if gq == 0 else "ysbG"),
                 writes=("psW",))
            if gq == 0:
                p.op("v", lambda e: e.tensor_copy(
                    ysb[:, 0:8 * QT], psW[:, 0:8 * QT]),
                    reads=("psW",), writes=("ysbG",))
            else:
                p.op("v", lambda e: e.tensor_copy(
                    ysb[:, 8 * QT:16 * QT], psW[:, 8 * QT:16 * QT]),
                    reads=("psW",), writes=("ysbG",))

        def yim(e):
            for h in range(NH):
                sl = psW[(h % 2) * 64:(h % 2) * 64 + 64,
                         (h // 2) * QT:(h // 2) * QT + QT]
                mm = e.matmul(sl, hbf[:, h * 64:(h + 1) * 64],
                              xbcs[:, 17 * QT:18 * QT], start=True, stop=True)
            return mm
        p.op("t", yim, reads=("hbf", "xbcs", "ysbG"), writes=("psW",))
        p.op("v", lambda e: e.tensor_mul(silz[:], psW[:], Esb[:]),
             reads=("psW", "Esb"), writes=("silz",))
        p.op("v", lambda e: e.tensor_add(ysb[:], ysb[:], silz[:]),
             reads=("ysbG", "silz"), writes=("ysbF",))
        p.op("v", lambda e: e.tensor_tensor(
            silz[:].rearrange("p (g t) -> p g t", g=16),
            xbcs[:, 0:16 * QT].rearrange("p (g t) -> p g t", g=16),
            dpm[:].unsqueeze(2).broadcast_to((128, 16, QT)), ALU.mult),
            reads=("xbcs", "dpm"), writes=("silz",))
        p.op("v", lambda e: e.tensor_add(ysb[:], ysb[:], silz[:]),
             reads=("ysbF", "silz"), writes=("ysbF",))
        p.op("t", lambda e: e.transpose(pm1[0][:].bitcast(BF16)[:, 0:128],
                                        xbcs[:, 16 * QT:17 * QT], idb[:]),
             reads=("xbcs", "idb"), writes=("pm0",))
        p.op("v", lambda e: e.tensor_copy(BTb[:], pm1[0][:].bitcast(BF16)[:, 0:128]),
             reads=("pm0",), writes=("BTb",))
        p.op("v", lambda e: e.tensor_tensor(
            xds[:].rearrange("p (h q) -> p h q", h=NH),
            XTb[:].rearrange("p (h q) -> p h q", h=NH),
            dteT[:].unsqueeze(2).broadcast_to((128, NH, HD)), ALU.mult),
            reads=("XTb", "dteT"), writes=("xds",))

        def dhm(e):
            for j in range(4):
                mm = e.matmul(psW[:, j * 512:(j + 1) * 512], BTb[:],
                              xds[:, j * 512:(j + 1) * 512], start=True, stop=True)
            return mm
        p.op("t", dhm, reads=("BTb", "xds", "ysbF"), writes=("psW",))
        p.op("t", lambda e: e.transpose(pm1[0][0:1, 0:NH], eLa[:, QT - 1:QT],
                                        ids[0:32, 0:32]),
             reads=("eLa", "ids"), writes=("pm0",))
        p.op("v", lambda e: e.tensor_copy(e32r, pm1[0][0:1, 0:NH]),
             reads=("pm0",), writes=("alr",))
        p.op("t", lambda e: e.matmul(pm1[1][:, 0:NH], L.onesr[:], e32r,
                                     start=True, stop=True),
             reads=("alr", "ln_onesr"), writes=("pm1",))
        p.op("v", lambda e: e.tensor_copy(E32[:], pm1[1][:, 0:NH]),
             reads=("pm1",), writes=("E32",))
        p.op("v", lambda e: e.tensor_tensor(
            hst[:].rearrange("n (h q) -> n h q", h=NH),
            hst[:].rearrange("n (h q) -> n h q", h=NH),
            E32[:].unsqueeze(2).broadcast_to((128, NH, HD)), ALU.mult),
            reads=("hst", "E32", "hbf"), writes=("hst",))
        p.op("v", lambda e: e.tensor_add(hst[:], hst[:], psW[:]),
             reads=("hst", "psW"), writes=("hst",))
        p.op("s", lambda e: e.activation(silz[:], zs[:], AF.Silu),
             reads=("zs",), writes=("silz",))
        p.op("s", lambda e: e.activation(Esb[:], ysb[:], AF.Square),
             reads=("ysbF",), writes=("Esb",))

        def rmm(e):
            for k in range(16):
                mm = e.matmul(L.psr[:, 0:QT], L.onesc[:], Esb[:, k * QT:(k + 1) * QT],
                              start=(k == 0), stop=(k == 15))
            return mm
        p.op("t", rmm, reads=("Esb", "ln_onesc"), writes=("ln_psr",))
        p.op("v", lambda e: e.tensor_copy(rmsr[:], L.psr[0:1, 0:QT]),
             reads=("ln_psr",), writes=("ln_rv",))
        p.op("v", lambda e: e.tensor_scalar_mul(rmsr[:], rmsr[:], 1.0 / DIN),
             reads=("ln_rv",), writes=("ln_rv",))
        p.op("s", lambda e: e.activation(rmsr[:], rmsr[:], AF.Sqrt,
                                         bias=L.epsb[0:1, 0:1]),
             reads=("ln_rv", "ln_epsb"), writes=("ln_rv",))
        p.op("v", lambda e: e.reciprocal(rmsr[:], rmsr[:]),
             reads=("ln_rv",), writes=("ln_rv",))
        p.op("t", lambda e: e.matmul(L.psb2[:, 0:QT], L.onesr[:], rmsr[:],
                                     start=True, stop=True),
             reads=("ln_rv", "ln_onesr"), writes=("ln_psb2",))
        p.op("v", lambda e: e.tensor_copy(rmsb[:], L.psb2[:, 0:QT]),
             reads=("ln_psb2",), writes=("ln_mb",))
        p.op("v", lambda e: e.tensor_tensor(
            ysb[:].rearrange("p (g t) -> p g t", g=16),
            ysb[:].rearrange("p (g t) -> p g t", g=16),
            rmsb[:].unsqueeze(1).broadcast_to((128, 16, QT)), ALU.mult),
            reads=("ysbF", "ln_mb"), writes=("ysbF",))
        p.op("v", lambda e: e.tensor_mul(xds[:, 0:16 * QT], ysb[:], silz[:]),
             reads=("ysbF", "silz"), writes=("xds",))
        for dd in range(PC):
            def opm(e, dd=dd):
                for k in range(16):
                    mm = e.matmul(pm1[dd % 2][:],
                                  wop[:, k * D + dd * 128:k * D + (dd + 1) * 128],
                                  xds[:, k * QT:(k + 1) * QT],
                                  start=(k == 0), stop=(k == 15))
                return mm
            p.op("t", opm, reads=("wop", "xds"), writes=(f"pm{dd % 2}",))
            p.op("v", lambda e, dd=dd: e.tensor_tensor(
                vsb[:, dd * QT:(dd + 1) * QT], pm1[dd % 2][:],
                xs[:, dd * QT:(dd + 1) * QT], ALU.add),
                reads=(f"pm{dd % 2}", "xs"), writes=("silz",))
        emit_postln(p, L, vsb, "silz", xs[:], "xs")
        p.op("s", lambda e, c=c: e.dma_start(
            out=yd[:, c * QT:(c + 1) * QT].rearrange("(pc p) t -> p pc t", p=128),
            in_=xs[:]), reads=("xs",), writes=("ydone",), dma=True, group="st")



def build_l0a(T):
    nc = bass.Bass("TRN2", target_bir_lowering=False, dynamic_dma_scratch_size=8192)
    io = {}
    io["x"] = nc.dram_tensor("x", [D, T], F32, kind="ExternalInput").ap()
    io["wipT"] = nc.dram_tensor("wipT", [D, DIP], F32, kind="ExternalInput").ap()
    io["wopT"] = nc.dram_tensor("wopT", [DIN, D], F32, kind="ExternalInput").ap()
    io["cw_pm"] = nc.dram_tensor("cw_pm", [128, CC * 4], F32, kind="ExternalInput").ap()
    io["cb_pm"] = nc.dram_tensor("cb_pm", [128, CC], F32, kind="ExternalInput").ap()
    io["dtb_col"] = nc.dram_tensor("dtb_col", [NH, 1], F32, kind="ExternalInput").ap()
    io["alog_row"] = nc.dram_tensor("alog_row", [1, 96], F32, kind="ExternalInput").ap()
    io["d_pm"] = nc.dram_tensor("d_pm", [128, 16], F32, kind="ExternalInput").ap()
    io["nw_pm"] = nc.dram_tensor("nw_pm", [128, 16], F32, kind="ExternalInput").ap()
    io["maskS"] = nc.dram_tensor("maskS", [128, 128], F32, kind="ExternalInput").ap()
    io["selbig"] = nc.dram_tensor("selbig", [NH, 16 * 128], F32, kind="ExternalInput").ap()
    io["selone"] = nc.dram_tensor("selone", [NH, NH * 128], F32, kind="ExternalInput").ap()
    io["ident"] = nc.dram_tensor("ident", [128, 128], F32, kind="ExternalInput").ap()
    io["y"] = nc.dram_tensor("y", [D, T], F32, kind="ExternalOutput").ap()
    p = Prog()
    emit_l0a(nc, p, io, T)
    p.op("y", lambda e: e.nop(), reads=("ydone",))
    p.emit(nc)
    return nc
def prep_l23f(x_fm, w_gate, w_up, w_down):
    return {"x": x_fm,
            "wgT": np.ascontiguousarray(w_gate.T),
            "wuT": np.ascontiguousarray(w_up.T),
            "wdT": np.ascontiguousarray(w_down.T)}


BLOCK_DECLS = {'build_l23f': [('x', None, 'F32', 'ExternalInput'), ('wgT', (1024, 3072), 'I32', 'ExternalInput'), ('wuT', (1024, 3072), 'I32', 'ExternalInput'), ('wdT', (3072, 1024), 'I32', 'ExternalInput'), ('y', None, 'F32', 'ExternalOutput')], 'build_l3a': [('x', None, 'F32', 'ExternalInput'), ('w_in', (8, 128, 128), 'F32', 'ExternalInput'), ('w_a', (8, 128, 128), 'F32', 'ExternalInput'), ('ap_pm', (128, 8), 'F32', 'ExternalInput'), ('y', None, 'F32', 'ExternalOutput')], 'build_l1a': [('x', None, 'F32', 'ExternalInput'), ('q_pm', (128, 8), 'F32', 'ExternalInput'), ('k_pm', (128, 8), 'F32', 'ExternalInput'), ('nw_pm', (128, 8), 'F32', 'ExternalInput'), ('ld_row', (1, 16), 'F32', 'ExternalInput'), ('owT', (1024, 1024), 'F32', 'ExternalInput'), ('halves', (128, 2), 'F32', 'ExternalInput'), ('hm16', (16, 8), 'F32', 'ExternalInput'), ('h64', (16, 128), 'F32', 'ExternalInput'), ('ident', (128, 128), 'F32', 'ExternalInput'), ('y', None, 'F32', 'ExternalOutput')], 'build_l0f': [('x', None, 'F32', 'ExternalInput'), ('w', (1024, 1024), 'F32', 'ExternalInput'), ('bias_pm', (128, 8), 'F32', 'ExternalInput'), ('ident', (128, 128), 'F32', 'ExternalInput'), ('y', None, 'F32', 'ExternalOutput')], 'build_l1f': [('x', None, 'F32', 'ExternalInput'), ('nw1_pm', (128, 8), 'F32', 'ExternalInput'), ('nw2_pm', (128, 32), 'F32', 'ExternalInput'), ('w1T', (1024, 4096), 'F32', 'ExternalInput'), ('w2T', (4096, 1024), 'F32', 'ExternalInput'), ('acts', (1, 2), 'F32', 'ExternalInput'), ('ident', (128, 128), 'F32', 'ExternalInput'), ('y', None, 'F32', 'ExternalOutput')], 'build_l2a': [('x', None, 'F32', 'ExternalInput'), ('gam_pm', (128, 8), 'F32', 'ExternalInput'), ('wqkv', (1024, 3072), 'F32', 'ExternalInput'), ('wout', (1024, 1024), 'F32', 'ExternalInput'), ('temp16', (1, 16), 'F32', 'ExternalInput'), ('hm16', (16, 8), 'F32', 'ExternalInput'), ('h64', (16, 128), 'F32', 'ExternalInput'), ('ident', (128, 128), 'F32', 'ExternalInput'), ('y', None, 'F32', 'ExternalOutput')], 'build_l0a': [('x', None, 'F32', 'ExternalInput'), ('wipT', (1024, 4384), 'F32', 'ExternalInput'), ('wopT', (2048, 1024), 'F32', 'ExternalInput'), ('cw_pm', (128, 72), 'F32', 'ExternalInput'), ('cb_pm', (128, 18), 'F32', 'ExternalInput'), ('dtb_col', (32, 1), 'F32', 'ExternalInput'), ('alog_row', (1, 96), 'F32', 'ExternalInput'), ('d_pm', (128, 16), 'F32', 'ExternalInput'), ('nw_pm', (128, 16), 'F32', 'ExternalInput'), ('maskS', (128, 128), 'F32', 'ExternalInput'), ('selbig', (32, 2048), 'F32', 'ExternalInput'), ('selone', (32, 4096), 'F32', 'ExternalInput'), ('ident', (128, 128), 'F32', 'ExternalInput'), ('y', None, 'F32', 'ExternalOutput')]}


_DT = {"F32": F32, "BF16": BF16, "I32": I32}
FUSED_BLOCKS = [("l0a", "build_l0a"), ("l0f", "build_l0f"), ("l1a", "build_l1a"),
                ("l1f", "build_l1f"), ("l2a", "build_l2a"), ("l2f", "build_l23f"),
                ("l3a", "build_l3a"), ("l3f", "build_l23f")]


def build_fused(T, upto=None, reset=True):
    nc = bass.Bass("TRN2", target_bir_lowering=False)
    p = Prog()
    xd = nc.dram_tensor("x", [D, T], F32, kind="ExternalInput").ap()
    yd = nc.dram_tensor("y", [D, T], F32, kind="ExternalOutput").ap()
    xscr = ([nc.dram_tensor(f"xscr{i}", [D, T], F32, kind="Internal").ap()
             for i in range(2)] if (upto is None or upto > 1) else [None, None])
    emits = dict(build_l0a=emit_l0a, build_l0f=emit_l0f, build_l1a=emit_l1a,
                 build_l1f=emit_l1f, build_l2a=emit_l2a, build_l23f=emit_l23f,
                 build_l3a=emit_l3a)

    blocks = FUSED_BLOCKS if upto is None else FUSED_BLOCKS[:upto]
    for bi, (tag, bn) in enumerate(blocks):
        io = {}
        for (name, shape, dt, kind) in BLOCK_DECLS[bn]:
            if name == "x":
                io["x"] = xd if bi == 0 else xscr[bi % 2]
            elif name == "y":
                io["y"] = yd if bi == len(blocks) - 1 else xscr[(bi + 1) % 2]
            else:
                io[name] = nc.dram_tensor(tag + "_" + name, list(shape),
                                          _DT[dt], kind="ExternalInput").ap()
        saved_sb = nc.sbuf_base
        saved_ps = nc.psum_base
        if bn == "build_l23f":
            emits[bn](nc, p, io, T, QT=256, tag=tag + "_")
        else:
            emits[bn](nc, p, io, T, tag=tag + "_")
        if reset:
            nc.sbuf_base = saved_sb
            nc.psum_base = saved_ps
        p.barrier_all()
    p.op("y", lambda e: e.nop(), reads=("ydone",))
    p.emit(nc)
    return nc


def fused_in_map(g, xarr):
    NHl = 32
    halves, hm16, h64, selb, selo, mask, ident = _consts()
    cw = np.asarray(g["L0_attn_conv_w"], np.float32).reshape(2304, 4) \
        .reshape(18, 128, 4).transpose(1, 0, 2)
    m = {"x": xarr}
    m.update({
        "l0a_wipT": np.ascontiguousarray(np.asarray(g["L0_attn_in_proj_w"], np.float32).T),
        "l0a_wopT": np.ascontiguousarray(np.asarray(g["L0_attn_out_proj_w"], np.float32).T),
        "l0a_cw_pm": np.ascontiguousarray(cw.reshape(128, 72)),
        "l0a_cb_pm": np.ascontiguousarray(np.asarray(g["L0_attn_conv_b"], np.float32)
                                          .reshape(18, 128).T),
        "l0a_dtb_col": np.asarray(g["L0_attn_dt_bias"], np.float32).reshape(NHl, 1),
        "l0a_alog_row": np.asarray(g["L0_attn_A_log"], np.float32).reshape(1, 96),
        "l0a_d_pm": np.ascontiguousarray(
            np.repeat(np.asarray(g["L0_attn_D"], np.float32), 64).reshape(16, 128).T),
        "l0a_nw_pm": np.ascontiguousarray(
            np.asarray(g["L0_attn_norm_w"], np.float32).reshape(16, 128).T),
        "l0a_maskS": mask, "l0a_selbig": selb, "l0a_selone": selo, "l0a_ident": ident,
        "l0f_w": np.asarray(g["L0_ffn_weight"], np.float32),
        "l0f_bias_pm": pm(np.asarray(g["L0_ffn_bias"], np.float32)),
        "l0f_ident": ident,
        "l1a_q_pm": pm(np.asarray(g["L1_attn_q"], np.float32)),
        "l1a_k_pm": pm(np.asarray(g["L1_attn_k"], np.float32)),
        "l1a_nw_pm": pm(np.asarray(g["L1_attn_norm_weight"], np.float32)),
        "l1a_ld_row": np.asarray(g["L1_attn_log_decay"], np.float32)[None, :],
        "l1a_owT": np.ascontiguousarray(np.asarray(g["L1_attn_o_w"], np.float32).T),
        "l1a_halves": halves, "l1a_hm16": hm16, "l1a_h64": h64, "l1a_ident": ident,
        "l1f_nw1_pm": pm(np.asarray(g["L1_ffn_norm1_w"], np.float32)),
        "l1f_nw2_pm": np.ascontiguousarray(
            np.asarray(g["L1_ffn_norm2_w"], np.float32).reshape(32, 128).T),
        "l1f_w1T": np.ascontiguousarray(np.asarray(g["L1_ffn_w1"], np.float32).T),
        "l1f_w2T": np.ascontiguousarray(np.asarray(g["L1_ffn_w2"], np.float32).T),
        "l1f_acts": np.array([[np.asarray(g["L1_ffn_act_scale"]).ravel()[0],
                               np.asarray(g["L1_ffn_act_bias"]).ravel()[0]]], np.float32),
        "l1f_ident": ident,
        "l2a_gam_pm": pm(np.asarray(g["L2_attn_gamma"], np.float32)),
        "l2a_wqkv": np.asarray(g["L2_attn_w_qkv"], np.float32),
        "l2a_wout": np.asarray(g["L2_attn_w_out"], np.float32),
        "l2a_temp16": np.asarray(g["L2_attn_temperature"], np.float32).reshape(1, 16),
        "l2a_hm16": hm16, "l2a_h64": h64, "l2a_ident": ident,
        "l2f_wgT": np.ascontiguousarray(np.asarray(g["L2_ffn_w_gate"], np.int32).T),
        "l2f_wuT": np.ascontiguousarray(np.asarray(g["L2_ffn_w_up"], np.int32).T),
        "l2f_wdT": np.ascontiguousarray(np.asarray(g["L2_ffn_w_down"], np.int32).T),
        "l3a_w_in": np.asarray(g["L3_attn_w_in"], np.float32),
        "l3a_w_a": np.asarray(g["L3_attn_w_a"], np.float32),
        "l3a_ap_pm": pm(np.asarray(g["L3_attn_a_param"], np.float32)),
        "l3f_wgT": np.ascontiguousarray(np.asarray(g["L3_ffn_w_gate"], np.int32).T),
        "l3f_wuT": np.ascontiguousarray(np.asarray(g["L3_ffn_w_up"], np.int32).T),
        "l3f_wdT": np.ascontiguousarray(np.asarray(g["L3_ffn_w_down"], np.int32).T),
    })
    return m


def _consts():
    H = 16
    NH = 32
    halves = np.zeros((128, 2), np.float32)
    halves[:64, 0] = 1
    halves[64:, 1] = 1
    hm16 = np.zeros((H, 8), np.float32)
    for h in range(H):
        hm16[h, h // 2] = 1
    h64 = np.zeros((H, 128), np.float32)
    for h in range(H):
        h64[h, (h % 2) * 64:(h % 2 + 1) * 64] = 1
    selb = np.zeros((NH, 16 * 128), np.float32)
    for pc in range(16):
        for pp in range(128):
            selb[2 * pc + pp // 64, pc * 128 + pp] = 1
    selo = np.zeros((NH, NH * 128), np.float32)
    for h in range(NH):
        selo[h, h * 128:(h + 1) * 128] = 1
    mask = np.triu(np.ones((128, 128), np.float32))
    ident = np.eye(128, dtype=np.float32)
    return halves, hm16, h64, selb, selo, mask, ident


def kernel(**g):
    T = 8192
    nc = build_fused(T)
    base = fused_in_map(g, None)
    del base["x"]
    u = np.asarray(g["L0_attn_u"], dtype=np.float32)
    zbase = {k: np.zeros_like(v) for k, v in base.items()}
    zx = np.zeros((D, T), np.float32)
    in_maps = []
    for c in range(8):
        if c < 4:
            m = dict(base)
            m["x"] = fm(u[c])
        else:
            m = dict(zbase)
            m["x"] = zx
        in_maps.append(m)
    res = run_bass_kernel_spmd(nc, in_maps, core_ids=list(range(8)))
    out = np.stack([np.ascontiguousarray(np.asarray(res.results[b]["y"]).T)
                    for b in range(4)])
    return np.ascontiguousarray(out.astype(np.float32))
